# version 16
# baseline (speedup 1.0000x reference)
import numpy as np
import concourse.bass as bass
import concourse.mybir as mybir
from concourse.bass_utils import run_bass_kernel_spmd
from contextlib import ExitStack

F32 = mybir.dt.float32
BF16 = mybir.dt.bfloat16
AF = mybir.ActivationFunctionType
ALU = mybir.AluOpType
AX = mybir.AxisListType

NT = 8
D = 2048
KC = 16
H = 32
NPAIR = 32
EPS = 1e-5


def build_program():
    nc = bass.Bass("TRN2", target_bir_lowering=False)

    x_h = nc.dram_tensor("x", [1024, D], F32, kind="ExternalInput")
    wq_h = nc.dram_tensor("wq", [16, 128, 2048], F32, kind="ExternalInput")
    wk_h = nc.dram_tensor("wk", [4, 128, 8192], F32, kind="ExternalInput")
    wv_h = nc.dram_tensor("wv", [4, 128, 8192], F32, kind="ExternalInput")
    wo_h = nc.dram_tensor("wo", [4, 128, 8192], F32, kind="ExternalInput")
    bo_h = nc.dram_tensor("bo", [1, D], F32, kind="ExternalInput")
    gu_h = nc.dram_tensor("gu", [32, 128, 2048], F32, kind="ExternalInput")
    dn_h = nc.dram_tensor("dn", [4, 128, 8192], F32, kind="ExternalInput")
    id_h = nc.dram_tensor("ident", [128, 128], F32, kind="ExternalInput")
    out_h = nc.dram_tensor("out", [1024, D], F32, kind="ExternalOutput")

    cc_in = nc.dram_tensor("cc_in", [128, NPAIR * 65], F32)
    cc_out = nc.dram_tensor("cc_out", [128, NPAIR * 65], F32)
    x2_h = nc.dram_tensor("x2scr", [1024, D], F32)

    with ExitStack() as st:
        def sb(name, shape, dt):
            return st.enter_context(nc.sbuf_tensor(name, shape, dt))

        A = sb("A", [128, KC * 1024], BF16)
        B = sb("B", [128, KC * 1024], BF16)
        SC1 = sb("SC1", [128, 5 * D], F32)
        TMP32 = sb("TMP32", [128, 2 * D], F32)
        XST = sb("XST", [128, 2 * D], F32)
        KTM = sb("KTM", [128, 2 * 512], BF16)
        V65 = sb("V65", [128, 2 * 520], BF16)
        KV65 = sb("KV65", [128, 2 * H * 65], BF16)
        RING = sb("RING", [128, 5 * 4096], BF16)
        BOP = sb("BOP", [128, D], BF16)
        E1 = sb("E1", [128, 128], BF16)
        IDS = sb("IDS", [128, 128], F32)
        GT = sb("GT", [128, 1024], BF16)
        STS = sb("STS", [128, 240], F32)
        CST = sb("CST", [128, 6], F32)
        FL = sb("FL", [128, 4], F32)
        SQD = sb("SQD", [128, 512], F32)

        PBIG = st.enter_context(nc.psum_tensor("PBIG", [128, 4096], F32))
        pb = [PBIG[:, i * 512:(i + 1) * 512] for i in range(8)]

        sx_ = st.enter_context(nc.semaphore("sx"))
        sw_ = st.enter_context(nc.semaphore("sw"))
        sv_ = st.enter_context(nc.semaphore("sv"))
        sa_ = st.enter_context(nc.semaphore("sa"))
        sp_ = st.enter_context(nc.semaphore("sp"))
        sc_ = st.enter_context(nc.semaphore("scc"))
        so_ = st.enter_context(nc.semaphore("so"))
        sems = [sx_, sw_, sv_, sa_, sp_, sc_, so_]
        nums = sorted(s.num for s in sems)
        assert nums[-1] - nums[0] == len(sems) - 1, nums

        C_SX, C_SXX, C_MEAN, C_VAR, C_MSQ, C_LNV, C_RSTD = 0, 8, 16, 24, 32, 40, 48
        C_SQ4, C_T4, C_NM, C_LNN, C_RC = 56, 88, 96, 136, 176
        C_XA = 212

        P = {"sync": [], "gpsimd": [], "vector": [], "scalar": [], "tensor": []}
        C = {"x": 0, "w": 0, "v": 0, "a": 0, "p": 0, "cc": 0, "o": 0}

        def em(eng, fn):
            P[eng].append(fn)

        def wait(eng, sem, val):
            if val > 0:
                em(eng, lambda E, s=sem, v=val: E.wait_ge(s, v))

        def vflush():
            em("vector", lambda E: E.memset(FL[:, 0:1], 0.0))
            C["v"] += 1
            em("vector", lambda E: E.memset(FL[:, 1:2], 0.0).then_inc(sv_, 1))
            return C["v"]

        def aflush():
            em("scalar", lambda E: E.mul(out=FL[:, 2:3], in_=CST[:, 0:1], mul=1.0))
            C["a"] += 1
            em("scalar", lambda E: E.mul(out=FL[:, 3:4], in_=CST[:, 0:1], mul=1.0)
               .then_inc(sa_, 1))
            return C["a"]

        def pinc(mk):
            C["p"] += 1
            em("tensor", lambda E, f=mk: f(E).then_inc(sp_, 1))
            return C["p"]

        def sdma(mk):
            C["x"] += 16
            em("sync", lambda E, f=mk: f(E).then_inc(sx_, 16))
            return C["x"]

        def gdma(mk):
            C["w"] += 16
            em("gpsimd", lambda E, f=mk: f(E).then_inc(sw_, 16))
            return C["w"]

        def odma(mk):
            C["o"] += 16
            em("sync", lambda E, f=mk: f(E).then_inc(so_, 16))
            return C["o"]

        ring_rel = [0] * 5
        ring_ptr = [0]

        def ring_acquire():
            s = ring_ptr[0]
            ring_ptr[0] = (s + 1) % 5
            if ring_rel[s] > 0:
                wait("gpsimd", sp_, ring_rel[s])
            return s

        def rsl(s, nels):
            return RING[:, s * 4096: s * 4096 + nels]

        with nc.Block() as blk0:
            @blk0.gpsimd
            def _(gpsimd):
                gpsimd.sem_clear(range(nums[0], nums[-1] + 1))

        with nc.Block() as blk1:
            @blk1.vector
            def _(vector):
                vector.memset(CST[:, 0:1], EPS)
                vector.memset(CST[:, 1:2], -0.5)
                vector.memset(CST[:, 2:3], -1.0)
                vector.memset(CST[:, 3:4], 0.0)
                vector.memset(CST[:, 4:5], 1.0 / D)
                vector.memset(CST[:, 5:6], float(D) ** -0.5)
                vector.memset(E1[:, :], 0.0)
                vector.memset(E1[0:1, :], 1.0)
                vector.memset(BOP[:, :], 0.0)
                vector.memset(KV65[:, :], 0.0)
                for t in range(2):
                    vector.memset(
                        V65[:, t * 520:(t + 1) * 520]
                        .rearrange("p (h e) -> p h e", e=65)[:, :, 64:65], 1.0)

        ident_dma = sdma(lambda E: E.dma_start(out=IDS[:, :], in_=id_h.ap()))
        bo_dma = gdma(lambda E: E.dma_start(out=BOP[0:1, :], in_=bo_h.ap()))

        tp_last = {}

        def emit_ln_stats(n, tile_ap_fn, v_gate=0, a_gate=0, x_waits=None,
                          xa_mode=False, sx_ready=False):
            if a_gate:
                wait("scalar", sv_, a_gate)
            if v_gate:
                wait("vector", sv_, v_gate)
            for j in range(n):
                if x_waits is not None:
                    wait("vector", sx_, x_waits[j])
                    wait("scalar", sx_, x_waits[j])
                if sx_ready:
                    pass
                elif xa_mode:
                    em("vector", lambda E, j=j: E.tensor_scalar(
                        out=STS[:, C_T4:C_T4 + 4],
                        in0=STS[:, C_XA + 4 * j:C_XA + 4 * j + 4],
                        scalar1=1.0, scalar2=None, op0=ALU.mult, op1=ALU.add,
                        accum_out=STS[:, C_SX + j:C_SX + j + 1]))
                else:
                    em("vector", lambda E, ap=tile_ap_fn(j), j=j: E.reduce_sum(
                        out=STS[:, C_SX + j:C_SX + j + 1], in_=ap, axis=AX.X))
                for q in range(4):
                    apq = tile_ap_fn(j)[:, q * 512:(q + 1) * 512]
                    em("scalar", lambda E, apq=apq, j=j, q=q: E.activation(
                        out=GT[:, 0:512], in_=apq, func=AF.Square,
                        accum_out=STS[:, C_SQ4 + 4 * j + q:C_SQ4 + 4 * j + q + 1]))
            av = aflush()
            vv = vflush()
            wait("vector", sv_, vv)
            wait("vector", sa_, av)
            for j in range(n):
                em("vector", lambda E, j=j: E.tensor_scalar(
                    out=STS[:, C_T4:C_T4 + 4],
                    in0=STS[:, C_SQ4 + 4 * j:C_SQ4 + 4 * j + 4],
                    scalar1=1.0, scalar2=None, op0=ALU.mult, op1=ALU.add,
                    accum_out=STS[:, C_SXX + j:C_SXX + j + 1]))
            vv = vflush()
            wait("vector", sv_, vv)
            em("vector", lambda E, n=n: E.tensor_scalar(
                out=STS[:, C_MEAN:C_MEAN + n], in0=STS[:, C_SX:C_SX + n],
                scalar1=1.0 / D, scalar2=None, op0=ALU.mult))
            em("vector", lambda E, n=n: E.tensor_scalar(
                out=STS[:, C_VAR:C_VAR + n], in0=STS[:, C_SXX:C_SXX + n],
                scalar1=1.0 / D, scalar2=None, op0=ALU.mult))
            vv = vflush()
            wait("vector", sv_, vv)
            em("vector", lambda E, n=n: E.tensor_tensor(
                out=STS[:, C_MSQ:C_MSQ + n], in0=STS[:, C_MEAN:C_MEAN + n],
                in1=STS[:, C_MEAN:C_MEAN + n], op=ALU.mult))
            vv = vflush()
            wait("vector", sv_, vv)
            em("vector", lambda E, n=n: E.tensor_tensor(
                out=STS[:, C_VAR:C_VAR + n], in0=STS[:, C_VAR:C_VAR + n],
                in1=STS[:, C_MSQ:C_MSQ + n], op=ALU.subtract))
            vv = vflush()
            wait("scalar", sv_, vv)
            em("scalar", lambda E, n=n: E.activation(
                out=STS[:, C_LNV:C_LNV + n], in_=STS[:, C_VAR:C_VAR + n],
                func=AF.Ln, bias=CST[:, 0:1]))
            em("scalar", lambda E, n=n: E.activation(
                out=STS[:, C_RSTD:C_RSTD + n], in_=STS[:, C_LNV:C_LNV + n],
                func=AF.Exp, scale=CST[:, 1:2]))
            return aflush()

        def emit_transposes(src_col, dst, bank_pair, extra_gates=()):
            buf, t = dst
            key = tuple(bank_pair)
            prev = tp_last.get(key)
            if prev:
                wait("tensor", sv_, prev[0])
                wait("tensor", sa_, prev[1])
            for sem, val in extra_gates:
                wait("tensor", sem, val)
            ev = {}
            for grp in range(4):
                bk = bank_pair[grp % 2]
                if grp >= 2:
                    sem, val = ev[grp - 2]
                    wait("tensor", sem, val)
                for q in range(4):
                    kc = grp * 4 + q
                    mk = (lambda E, kc=kc, bk=bk, q=q, src_col=src_col:
                          E.transpose(
                              out=pb[bk][:, q * 128:(q + 1) * 128],
                              in_=TMP32[:, src_col + kc * 128: src_col + (kc + 1) * 128],
                              identity=IDS[:, :]))
                    if q < 3:
                        em("tensor", lambda E, f=mk: f(E))
                    else:
                        pv = pinc(mk)
                dview = buf[:, :].rearrange("p (kc tt) -> p kc tt", tt=1024)[
                    :, grp * 4:(grp + 1) * 4, t * 128:(t + 1) * 128]
                if grp % 2 == 0:
                    wait("vector", sp_, pv)
                    em("vector", lambda E, bk=bk, dview=dview: E.tensor_copy(
                        out=dview, in_=pb[bk][:, 0:512]))
                    C["v"] += 1
                    em("vector", lambda E: E.memset(FL[:, 0:1], 0.0)
                       .then_inc(sv_, 1))
                    ev[grp] = (sv_, C["v"])
                else:
                    wait("scalar", sp_, pv)
                    em("scalar", lambda E, bk=bk, dview=dview: E.activation(
                        out=dview, in_=pb[bk][:, 0:512], func=AF.Copy))
                    C["a"] += 1
                    em("scalar", lambda E: E.mul(out=FL[:, 2:3],
                                                 in_=CST[:, 0:1], mul=1.0)
                       .then_inc(sa_, 1))
                    ev[grp] = (sa_, C["a"])
            vg = vflush()
            ag = aflush()
            tp_last[key] = (vg, ag)
            return (vg, ag)

        k_ev_a, v_ev_v = {}, {}
        kv_state = {"flush": 0}

        def emit_kv(t, fc):
            ti = fc * NT + t
            bt = t // 4
            wait("tensor", sa_, k_ev_a[ti])
            wait("tensor", sv_, v_ev_v[ti])
            if t == 0 and kv_state["flush"]:
                wait("tensor", sv_, kv_state["flush"])
            done = None
            for j in range(4):
                for par in range(2):
                    hh = 2 * j + par
                    bank = 4 + bt * 2 + j // 2
                    col = (j % 2) * 130 + par * 65
                    first = (t % 4 == 0) and (j % 2 == 0) and (par == 0)
                    mk = (lambda E, t=t, j=j, hh=hh, bank=bank, col=col,
                          first=first: E.matmul(
                              out=pb[bank][:, col:col + 65],
                              lhsT=KTM[:, (t % 2) * 512 + j * 128: (t % 2) * 512 + (j + 1) * 128],
                              rhs=V65[:, (t % 2) * 520 + hh * 65: (t % 2) * 520 + (hh + 1) * 65],
                              start=first, stop=(t % 4 == 3),
                              skip_group_check=not first))
                    if j == 3 and par == 1:
                        done = pinc(mk)
                    else:
                        em("tensor", lambda E, f=mk: f(E))
            return done

        def emit_kproj(t, fc, bank, strip_waits):
            for kc in range(KC):
                s, wv_ = strip_waits[kc // 8]
                if kc % 8 == 0:
                    wait("tensor", sw_, wv_)
                mk = (lambda E, kc=kc, t=t, bank=bank, s=s: E.matmul(
                    out=pb[bank][:, 0:512],
                    lhsT=A[:, kc * 1024 + t * 128: kc * 1024 + (t + 1) * 128],
                    rhs=rsl(s, 4096)[:, (kc % 8) * 512:(kc % 8 + 1) * 512],
                    start=(kc == 0), stop=(kc == KC - 1)))
                if kc < KC - 1:
                    em("tensor", lambda E, f=mk: f(E))
                else:
                    pk = pinc(mk)
            return pk

        fc0_k, fc0_v = [], []
        for half in range(2):
            s = ring_acquire()
            fc0_k.append((s, gdma(lambda E, half=half, s=s: E.dma_start(
                out=rsl(s, 4096),
                in_=wk_h.ap()[0][:, half * 4096:(half + 1) * 4096],
                max_dma_last_dim=2048))))
        for half in range(2):
            s = ring_acquire()
            fc0_v.append((s, gdma(lambda E, half=half, s=s: E.dma_start(
                out=rsl(s, 4096),
                in_=wv_h.ap()[0][:, half * 4096:(half + 1) * 4096],
                max_dma_last_dim=2048))))

        apply_fl = {}
        pend = {}
        kv_done_p = None
        tpB_pinc = {}

        def fm_dst(buf, t, kc0):
            return buf[:, :].rearrange("p (kc tt) -> p kc tt", tt=1024)[
                :, kc0:kc0 + 8, t * 128:(t + 1) * 128]

        def flush_pending(t):
            p = pend.pop(t)
            wait("vector", sp_, p["pv2"])
            em("vector", lambda E, t=t: E.tensor_copy(
                out=V65[:, (t % 2) * 520:(t % 2) * 520 + 520]
                .rearrange("p (h e) -> p h e", e=65)[:, :, 0:64],
                in_=pb[1][:, 0:512].rearrange("p (h f) -> p h f", f=64)))
            v_ev_v[t] = vflush()
            wait("scalar", sp_, p["tpB"])
            em("scalar", lambda E, t=t: E.activation(
                out=fm_dst(A, t, 8), in_=PBIG[:, 1024:2048], func=AF.Copy))
            wait("scalar", sp_, p["pk"])
            em("scalar", lambda E, t=t: E.activation(
                out=KTM[:, (t % 2) * 512:(t % 2) * 512 + 512],
                in_=pb[0][:, 0:512], func=AF.Relu))
            k_ev_a[t] = aflush()

        for g in range(2):
            tiles = list(range(4 * g, 4 * g + 4))
            if g == 1:
                flush_pending(3)
                wait("sync", sv_, apply_fl[3])
                wait("sync", sa_, C["a"])
            xw = []
            for j, t in enumerate(tiles):
                xw.append(sdma(lambda E, t=t, j=j: E.dma_start(
                    out=SC1[:, j * D:(j + 1) * D],
                    in_=x_h.ap()[t * 128:(t + 1) * 128, :])))
            ra = emit_ln_stats(4, lambda j: SC1[:, j * D:(j + 1) * D], x_waits=xw)
            wait("vector", sa_, ra)
            for j, t in enumerate(tiles):
                slot = t % 2
                if t >= 2:
                    wait("vector", sp_, tpB_pinc[t - 2])
                em("vector", lambda E, j=j, slot=slot: E.tensor_scalar(
                    out=TMP32[:, slot * D:(slot + 1) * D],
                    in0=SC1[:, j * D:(j + 1) * D],
                    scalar1=STS[:, C_MEAN + j:C_MEAN + j + 1],
                    scalar2=STS[:, C_RSTD + j:C_RSTD + j + 1],
                    op0=ALU.subtract, op1=ALU.mult))
                apply_fl[t] = vflush()
                if t >= 1 and (t - 1) in pend:
                    flush_pending(t - 1)
                wait("tensor", sv_, apply_fl[t])
                if t == 0:
                    wait("tensor", sx_, ident_dma)
                else:
                    wait("tensor", sa_, k_ev_a[t - 1])
                    wait("tensor", sv_, v_ev_v[t - 1])
                for q in range(8):
                    kc = q
                    bk = q // 4
                    mk = (lambda E, kc=kc, bk=bk, q=q, slot=slot: E.transpose(
                        out=pb[bk][:, (q % 4) * 128:(q % 4 + 1) * 128],
                        in_=TMP32[:, slot * D + kc * 128: slot * D + (kc + 1) * 128],
                        identity=IDS[:, :]))
                    if q < 7:
                        em("tensor", lambda E, f=mk: f(E))
                    else:
                        tpA_p = pinc(mk)
                wait("vector", sp_, tpA_p)
                em("vector", lambda E, t=t: E.tensor_copy(
                    out=fm_dst(A, t, 0), in_=PBIG[:, 0:1024]))
                eaf = vflush()
                for q in range(8):
                    kc = 8 + q
                    bk = 2 + q // 4
                    mk = (lambda E, kc=kc, bk=bk, q=q, slot=slot: E.transpose(
                        out=pb[bk][:, (q % 4) * 128:(q % 4 + 1) * 128],
                        in_=TMP32[:, slot * D + kc * 128: slot * D + (kc + 1) * 128],
                        identity=IDS[:, :]))
                    if q < 7:
                        em("tensor", lambda E, f=mk: f(E))
                    else:
                        tpB_pinc[t] = pinc(mk)
                if t >= 1:
                    kv_done_p = emit_kv(t - 1, 0)
                wait("tensor", sv_, eaf)
                pk = emit_kproj(t, 0, 0, fc0_k)
                pv2 = emit_kproj(t, 0, 1, fc0_v)
                pend[t] = {"pv2": pv2, "tpB": tpB_pinc[t], "pk": pk}
                if t == NT - 1:
                    ring_rel[fc0_k[0][0]] = pk
                    ring_rel[fc0_k[1][0]] = pk
                    ring_rel[fc0_v[0][0]] = pv2
                    ring_rel[fc0_v[1][0]] = pv2
        flush_pending(NT - 1)
        kv_done_p = emit_kv(NT - 1, 0)
        ln1_ready = (C["v"], C["a"])

        em("vector", lambda E: E.memset(SC1[:, 0:NPAIR * 65], 0.0))
        kvz_v = vflush()

        def emit_kv_evict(fc):
            wait("vector", sp_, kv_done_p)
            if fc == 0:
                wait("vector", sv_, kvz_v)
            for bt in range(2):
                for q in range(2):
                    bank = 4 + bt * 2 + q
                    for par in range(2):
                        gp0 = bt * 16 + fc * 4 + 2 * q
                        em("vector", lambda E, bank=bank, par=par, gp0=gp0:
                           E.tensor_copy(
                               out=SC1[par * 64:(par + 1) * 64,
                                       gp0 * 65:(gp0 + 2) * 65]
                               .rearrange("p (n e) -> p n e", e=65),
                               in_=pb[bank][par * 64:(par + 1) * 64, 0:260]
                               .rearrange("p (n e) -> p n e", e=130)
                               [:, :, par * 65: par * 65 + 65]))
            kv_state["flush"] = vflush()

        emit_kv_evict(0)

        for fc in range(1, 4):
            wkw, wvw = [], []
            for half in range(2):
                s = ring_acquire()
                wkw.append((s, gdma(lambda E, fc=fc, half=half, s=s: E.dma_start(
                    out=rsl(s, 4096),
                    in_=wk_h.ap()[fc][:, half * 4096:(half + 1) * 4096],
                    max_dma_last_dim=2048))))
            for half in range(2):
                s = ring_acquire()
                wvw.append((s, gdma(lambda E, fc=fc, half=half, s=s: E.dma_start(
                    out=rsl(s, 4096),
                    in_=wv_h.ap()[fc][:, half * 4096:(half + 1) * 4096],
                    max_dma_last_dim=2048))))
            for t in range(NT):
                ti = fc * NT + t
                bkk = t % 2
                bkv = 2 + t % 2
                if fc == 1 and t < 2:
                    wait("tensor", sa_, k_ev_a[7])
                    wait("tensor", sv_, v_ev_v[7])
                if ti >= 2:
                    wait("tensor", sa_, k_ev_a[ti - 2])
                pk = emit_kproj(t, fc, bkk, wkw)
                if t == NT - 1:
                    ring_rel[wkw[0][0]] = pk
                    ring_rel[wkw[1][0]] = pk
                wait("scalar", sp_, pk)
                em("scalar", lambda E, t=t, bkk=bkk: E.activation(
                    out=KTM[:, (t % 2) * 512:(t % 2) * 512 + 512],
                    in_=pb[bkk][:, 0:512], func=AF.Relu))
                k_ev_a[ti] = aflush()
                if ti >= 2:
                    wait("tensor", sv_, v_ev_v[ti - 2])
                pv2 = emit_kproj(t, fc, bkv, wvw)
                if t == NT - 1:
                    ring_rel[wvw[0][0]] = pv2
                    ring_rel[wvw[1][0]] = pv2
                wait("vector", sp_, pv2)
                em("vector", lambda E, t=t, bkv=bkv: E.tensor_copy(
                    out=V65[:, (t % 2) * 520:(t % 2) * 520 + 520]
                    .rearrange("p (h e) -> p h e", e=65)[:, :, 0:64],
                    in_=pb[bkv][:, 0:512].rearrange("p (h f) -> p h f", f=64)))
                v_ev_v[ti] = vflush()
                if t >= 1:
                    kv_done_p = emit_kv(t - 1, fc)
            kv_done_p = emit_kv(NT - 1, fc)
            emit_kv_evict(fc)
        kv_ev_flush = kv_state["flush"]

        wait("sync", sv_, kv_ev_flush)
        cc_dma = sdma(lambda E: E.dma_start(out=cc_in.ap(),
                                            in_=SC1[:, 0:NPAIR * 65]))
        early_wq = []
        for fq in range(3):
            s = ring_acquire()
            early_wq.append((s, gdma(lambda E, fq=fq, s=s: E.dma_start(
                out=rsl(s, 2048),
                in_=wq_h.ap()[fq], max_dma_last_dim=2048))))
        wait("gpsimd", sx_, cc_dma)
        C["cc"] += 1
        em("gpsimd", lambda E: E.collective_compute(
            "AllReduce", ALU.add,
            replica_groups=[[0, 1, 2, 3], [4, 5, 6, 7]],
            ins=[cc_in.ap().opt()], outs=[cc_out.ap().opt()]).then_inc(sc_, 1))

        q_ev_a = {}
        for fq in range(16):
            if fq < 3:
                s, wq_dma = early_wq[fq]
            else:
                s = ring_acquire()
                wq_dma = gdma(lambda E, fq=fq, s=s: E.dma_start(
                    out=rsl(s, 2048),
                    in_=wq_h.ap()[fq], max_dma_last_dim=2048))
            if fq == 0:
                wait("tensor", sv_, ln1_ready[0])
                wait("tensor", sa_, ln1_ready[1])
            if fq >= 2:
                wait("tensor", sa_, q_ev_a[fq - 2])
            wait("tensor", sw_, wq_dma)
            for kc in range(KC):
                for tg in range(2):
                    bank = (fq % 2) * 2 + tg
                    mk = (lambda E, kc=kc, tg=tg, bank=bank, s=s: E.matmul(
                        out=pb[bank][:, 0:512],
                        lhsT=rsl(s, 2048)[:, kc * 128:(kc + 1) * 128],
                        rhs=A[:, kc * 1024 + tg * 512: kc * 1024 + (tg + 1) * 512],
                        start=(kc == 0), stop=(kc == KC - 1)))
                    if kc == KC - 1 and tg == 1:
                        pq = pinc(mk)
                    else:
                        em("tensor", lambda E, f=mk: f(E))
            ring_rel[s] = pq
            wait("scalar", sp_, pq)
            for tg in range(2):
                bank = (fq % 2) * 2 + tg
                em("scalar", lambda E, fq=fq, tg=tg, bank=bank: E.activation(
                    out=B[:, fq * 1024 + tg * 512: fq * 1024 + (tg + 1) * 512],
                    in_=pb[bank][:, 0:512], func=AF.Relu))
            q_ev_a[fq] = aflush()
        q_ready_a = q_ev_a[15]

        wait("gpsimd", sc_, C["cc"])
        kv_ready_w = 0
        for par in range(2):
            src_ = cc_out.ap().rearrange("p (b fc jp e) -> p b fc jp e",
                                         b=2, fc=4, jp=4)[par * 64:(par + 1) * 64]
            dst = KV65[par * 64:(par + 1) * 64, :].rearrange(
                "p (b fc jp two e) -> p b fc jp two e",
                b=2, fc=4, jp=4, two=2)[:, :, :, :, par, :]
            kv_ready_w = gdma(lambda E, src_=src_, dst=dst: E.dma_start(
                out=dst, in_=src_))

        ROUNDS = [7, 7, 7, 7, 4]
        div_fl = {}
        tp_done = {}
        tile_inner = {}
        inner_ready = (0, 0)
        wo_ev_v = {}
        wo0_halves = []
        for half in range(2):
            s = ring_acquire()
            wo0_halves.append((s, gdma(
                lambda E, half=half, s=s: E.dma_start(
                    out=rsl(s, 4096),
                    in_=wo_h.ap()[0][:, half * 4096:(half + 1) * 4096],
                    max_dma_last_dim=2048))))
        wo0_pw = {}
        for t in range(NT):
            bt = t // 4
            if t == 0:
                wait("tensor", sw_, kv_ready_w)
                wait("tensor", sa_, q_ready_a)
            if t >= 2:
                wait("vector", sp_, tp_done[t - 2])
            if 2 <= t and t - 2 < 4:
                tw = t - 2
                gates = tile_inner[tw]
                wait("tensor", sv_, gates[0])
                wait("tensor", sa_, gates[1])
                if tw == 0:
                    wait("tensor", sw_, bo_dma)
                    wait("tensor", sw_, wo0_halves[1][1])
                else:
                    wait("tensor", sv_, wo_ev_v[(0, 0, tw - 1)])
                for ic in range(KC):
                    s = wo0_halves[ic // 8][0]
                    mk = (lambda E, ic=ic, tw=tw, s=s: E.matmul(
                        out=pb[7][:, 0:512],
                        lhsT=A[:, ic * 1024 + tw * 128: ic * 1024 + (tw + 1) * 128],
                        rhs=rsl(s, 4096)[:, (ic % 8) * 512:(ic % 8 + 1) * 512],
                        start=(ic == 0), stop=False))
                    em("tensor", lambda E, f=mk: f(E))
                mkb = (lambda E: E.matmul(
                    out=pb[7][:, 0:512], lhsT=E1[:, :],
                    rhs=BOP[:, 0:512],
                    start=False, stop=True, skip_group_check=True))
                wo0_pw[tw] = pinc(mkb)
                wait("vector", sp_, wo0_pw[tw])
                if tw == 0:
                    wait("vector", sx_, cc_dma)
                em("vector", lambda E, tw=tw: E.tensor_scalar(
                    out=SC1[:, tw * D: tw * D + 512],
                    in0=pb[7][:, 0:512], scalar1=1.0, scalar2=None,
                    op0=ALU.mult, op1=ALU.add,
                    accum_out=STS[:, C_XA + 4 * tw:C_XA + 4 * tw + 1]))
                wo_ev_v[(0, 0, tw)] = vflush()
            if t >= 1:
                wait("tensor", sv_, div_fl[t - 1])
                tile_inner[t - 1] = emit_transposes(
                    ((t - 1) % 2) * D, (A, t - 1), [5, 6])
                tp_done[t - 1] = C["p"]
            h = 0
            for r, nh in enumerate(ROUNDS):
                for i in range(nh):
                    mk = (lambda E, h=h, i=i, t=t, r=r, bt=bt, nh=nh: E.matmul(
                        out=PBIG[:, r * 512 + i * 65: r * 512 + (i + 1) * 65],
                        lhsT=B[:, (h // 2) * 1024 + t * 128:
                               (h // 2) * 1024 + (t + 1) * 128],
                        rhs=KV65[:, (bt * H + h) * 65:(bt * H + h + 1) * 65],
                        start=(i == 0), stop=(i == nh - 1),
                        skip_group_check=(i != 0)))
                    if r == len(ROUNDS) - 1 and i == nh - 1:
                        pr = pinc(mk)
                    else:
                        em("tensor", lambda E, f=mk: f(E))
                    h += 1
            wait("vector", sp_, pr)
            em("vector", lambda E: E.tensor_scalar(
                out=STS[:, C_NM:C_NM + 35].rearrange("p (r i) -> p r i", i=7),
                in0=PBIG[:, :].rearrange("p (r x) -> p r x", x=512)
                [:, 0:5, 0:455]
                .rearrange("p r (i e) -> p r i e", e=65)[:, :, :, 64:65],
                scalar1=1e-6, scalar2=None, op0=ALU.max))
            nv = vflush()
            wait("scalar", sv_, nv)
            em("scalar", lambda E: E.activation(
                out=STS[:, C_LNN:C_LNN + 35], in_=STS[:, C_NM:C_NM + 35],
                func=AF.Ln))
            em("scalar", lambda E: E.activation(
                out=STS[:, C_RC:C_RC + 35], in_=STS[:, C_LNN:C_LNN + 35],
                func=AF.Exp, scale=CST[:, 2:3]))
            ra = aflush()
            wait("vector", sa_, ra)
            slot = t % 2
            in0a = PBIG[:, :].rearrange("p (r x) -> p r x", x=512)[:, 0:4, 0:455] \
                .rearrange("p r (i e) -> p r i e", e=65)[:, :, :, 0:64]
            in1a = STS[:, C_RC:C_RC + 28].rearrange(
                "p (r i one) -> p r i one", r=4, i=7, one=1)
            a0, a1 = bass.broadcast_tensor_aps(in0a, in1a)
            outa = TMP32[:, slot * D: slot * D + 1792].rearrange(
                "p (r i e) -> p r i e", i=7, e=64)
            em("vector", lambda E, a0=a0, a1=a1, outa=outa: E.tensor_tensor(
                out=outa, in0=a0, in1=a1, op=ALU.mult))
            in0b = PBIG[:, 4 * 512: 4 * 512 + 4 * 65].rearrange(
                "p (i e) -> p i e", e=65)[:, :, 0:64]
            in1b = STS[:, C_RC + 28:C_RC + 32].rearrange(
                "p (i one) -> p i one", one=1)
            b0, b1 = bass.broadcast_tensor_aps(in0b, in1b)
            outb = TMP32[:, slot * D + 1792: slot * D + 2048].rearrange(
                "p (i e) -> p i e", e=64)
            em("vector", lambda E, b0=b0, b1=b1, outb=outb: E.tensor_tensor(
                out=outb, in0=b0, in1=b1, op=ALU.mult))
            div_fl[t] = vflush()
        wait("tensor", sv_, div_fl[NT - 1])
        tile_inner[NT - 1] = emit_transposes(
            ((NT - 1) % 2) * D, (A, NT - 1), [5, 6])
        tp_done[NT - 1] = C["p"]
        ring_rel[wo0_halves[0][0]] = wo0_pw[3]
        ring_rel[wo0_halves[1][0]] = wo0_pw[3]
        inner_ready = (C["v"], C["a"])

        x2_dma = {}

        def wo_strips(fc):
            halves = []
            for half in range(2):
                s = ring_acquire()
                halves.append((s, gdma(
                    lambda E, fc=fc, half=half, s=s: E.dma_start(
                        out=rsl(s, 4096),
                        in_=wo_h.ap()[fc][:, half * 4096:(half + 1) * 4096],
                        max_dma_last_dim=2048))))
            return halves

        def wo_mms(g, fc, halves, tiles):
            fcp = fc % 2
            pws = []
            for j, t in enumerate(tiles):
                bank = fcp * 4 + j
                if g == 0 and fc == 0 and j == 0:
                    wait("tensor", sv_, inner_ready[0])
                    wait("tensor", sa_, inner_ready[1])
                    wait("tensor", sw_, bo_dma)
                prev_ev = wo_ev_v.get((g, fc - 2, j))
                if g == 1 and fc < 2:
                    prev_ev = wo_ev_v.get((0, fc + 2, j))
                if prev_ev:
                    wait("tensor", sv_, prev_ev)
                for ic in range(KC):
                    s, wdma = halves[ic // 8]
                    if ic % 8 == 0 and j == 0:
                        wait("tensor", sw_, wdma)
                    mk = (lambda E, ic=ic, t=t, bank=bank, s=s: E.matmul(
                        out=pb[bank][:, 0:512],
                        lhsT=A[:, ic * 1024 + t * 128: ic * 1024 + (t + 1) * 128],
                        rhs=rsl(s, 4096)[:, (ic % 8) * 512:(ic % 8 + 1) * 512],
                        start=(ic == 0), stop=False))
                    em("tensor", lambda E, f=mk: f(E))
                mkb = (lambda E, fc=fc, bank=bank: E.matmul(
                    out=pb[bank][:, 0:512], lhsT=E1[:, :],
                    rhs=BOP[:, fc * 512:(fc + 1) * 512],
                    start=False, stop=True, skip_group_check=True))
                pws.append(pinc(mkb))
            ring_rel[halves[0][0]] = pws[-1]
            ring_rel[halves[1][0]] = pws[-1]
            return pws

        def wo_evicts(g, fc, pws, tiles, first_gates=()):
            fcp = fc % 2
            for j, t in enumerate(tiles):
                bank = fcp * 4 + j
                wait("vector", sp_, pws[j])
                for sem, val in (first_gates if j == 0 else ()):
                    wait("vector", sem, val)
                em("vector", lambda E, j=j, fc=fc, bank=bank: E.tensor_scalar(
                    out=SC1[:, j * D + fc * 512: j * D + (fc + 1) * 512],
                    in0=pb[bank][:, 0:512], scalar1=1.0, scalar2=None,
                    op0=ALU.mult, op1=ALU.add,
                    accum_out=STS[:, C_XA + 4 * j + fc:C_XA + 4 * j + fc + 1]))
                wo_ev_v[(g, fc, j)] = vflush()

        G1SLOT5 = [4, 0, 1, 2]

        def act_fin(j, with_sx):
            if with_sx:
                em("scalar", lambda E, j=j: E.activation(
                    out=STS[:, C_T4:C_T4 + 4],
                    in_=STS[:, C_XA + 4 * j:C_XA + 4 * j + 4], func=AF.Copy,
                    accum_out=STS[:, C_SX + j:C_SX + j + 1]))
            em("scalar", lambda E, j=j: E.activation(
                out=STS[:, C_T4 + 4:C_T4 + 8],
                in_=STS[:, C_SQ4 + 4 * j:C_SQ4 + 4 * j + 4], func=AF.Copy,
                accum_out=STS[:, C_SXX + j:C_SXX + j + 1]))
            em("scalar", lambda E, j=j: E.mul(
                out=STS[:, C_MEAN + j:C_MEAN + j + 1],
                in_=STS[:, C_SX + j:C_SX + j + 1], mul=1.0 / D))
            em("scalar", lambda E, j=j: E.activation(
                out=STS[:, C_MSQ + j:C_MSQ + j + 1],
                in_=STS[:, C_SX + j:C_SX + j + 1], func=AF.Square,
                scale=CST[:, 5:6]))
            em("scalar", lambda E, j=j: E.activation(
                out=STS[:, C_VAR + j:C_VAR + j + 1],
                in_=STS[:, C_MSQ + j:C_MSQ + j + 1], func=AF.Relu,
                scale=CST[:, 2:3], bias=STS[:, C_SXX + j:C_SXX + j + 1]))
            em("scalar", lambda E, j=j: E.activation(
                out=STS[:, C_LNV + j:C_LNV + j + 1],
                in_=STS[:, C_VAR + j:C_VAR + j + 1],
                func=AF.Ln, scale=CST[:, 4:5], bias=CST[:, 0:1]))
            em("scalar", lambda E, j=j: E.activation(
                out=STS[:, C_RSTD + j:C_RSTD + j + 1],
                in_=STS[:, C_LNV + j:C_LNV + j + 1],
                func=AF.Exp, scale=CST[:, 1:2]))
            return aflush()

        def wo_evict1(fc, j, pw, gates=()):
            bank = (fc % 2) * 4 + j
            slot = G1SLOT5[j]
            wait("vector", sp_, pw)
            for sem, val in gates:
                wait("vector", sem, val)
            em("vector", lambda E, j=j, fc=fc, bank=bank, slot=slot:
               E.tensor_scalar(
                   out=SC1[:, slot * D + fc * 512: slot * D + (fc + 1) * 512],
                   in0=pb[bank][:, 0:512], scalar1=1.0, scalar2=None,
                   op0=ALU.mult, op1=ALU.add,
                   accum_out=STS[:, C_XA + 4 * j + fc:C_XA + 4 * j + fc + 1]))
            fl = vflush()
            wo_ev_v[(1, fc, j)] = fl
            wait("scalar", sv_, fl)
            em("scalar", lambda E, j=j, fc=fc, slot=slot: E.activation(
                out=SQD[:, 0:512],
                in_=SC1[:, slot * D + fc * 512: slot * D + (fc + 1) * 512],
                func=AF.Square,
                accum_out=STS[:, C_SQ4 + 4 * j + fc:C_SQ4 + 4 * j + fc + 1]))
            return fl

        for fc in range(1, 4):
            halves = wo_strips(fc)
            if fc == 1:
                wait("tensor", sv_, C["v"])
                wait("tensor", sa_, C["a"])
            pws = wo_mms(0, fc, halves, [0, 1, 2, 3])
            wo_evicts(0, fc, pws, [0, 1, 2, 3])
        t1 = [4, 5, 6, 7]
        pws10 = wo_mms(1, 0, wo_strips(0), t1)
        pws11 = wo_mms(1, 1, wo_strips(1), t1)

        ra = emit_ln_stats(4, lambda j: SC1[:, j * D:(j + 1) * D],
                           v_gate=wo_ev_v[(0, 3, 3)], a_gate=wo_ev_v[(0, 3, 3)],
                           xa_mode=True)
        wait("vector", sa_, ra)
        rfl5, aq2, ap2fl, tpd = {}, {}, {}, {}
        for j0 in (0, 2):
            xw = []
            for j in (j0, j0 + 1):
                xw.append(sdma(lambda E, t=j: E.dma_start(
                    out=XST[:, (t % 2) * D:(t % 2 + 1) * D],
                    in_=x_h.ap()[t * 128:(t + 1) * 128, :])))
            for j in (j0, j0 + 1):
                em("vector", lambda E, j=j: E.tensor_scalar(
                    out=TMP32[:, (j % 2) * D:(j % 2 + 1) * D],
                    in0=SC1[:, j * D:(j + 1) * D],
                    scalar1=STS[:, C_MEAN + j:C_MEAN + j + 1],
                    scalar2=STS[:, C_RSTD + j:C_RSTD + j + 1],
                    op0=ALU.subtract, op1=ALU.mult))
            fv = vflush()
            wait("vector", sv_, fv)
            for jj, j in enumerate((j0, j0 + 1)):
                wait("vector", sx_, xw[jj])
                em("vector", lambda E, j=j: E.scalar_tensor_tensor(
                    out=SC1[:, j * D:(j + 1) * D],
                    in0=TMP32[:, (j % 2) * D:(j % 2 + 1) * D], scalar=1.0,
                    in1=XST[:, (j % 2) * D:(j % 2 + 1) * D],
                    op0=ALU.mult, op1=ALU.add,
                    accum_out=STS[:, C_SX + j:C_SX + j + 1]))
                rfl5[j] = vflush()
                wait("scalar", sv_, rfl5[j])
                for q in range(4):
                    em("scalar", lambda E, j=j, q=q: E.activation(
                        out=SQD[:, 0:512],
                        in_=SC1[:, j * D + q * 512: j * D + (q + 1) * 512],
                        func=AF.Square,
                        accum_out=STS[:, C_SQ4 + 4 * j + q:
                                      C_SQ4 + 4 * j + q + 1]))
                aq2[j] = act_fin(j, with_sx=False)
                wait("sync", sv_, rfl5[j])
                x2_dma[j] = sdma(lambda E, j=j: E.dma_start(
                    out=x2_h.ap()[j * 128:(j + 1) * 128, :],
                    in_=SC1[:, j * D:(j + 1) * D]))
            if j0 == 0:
                wo_evict1(0, 0, pws10[0])
                wo_evict1(1, 0, pws11[0])
        wait("sync", sx_, x2_dma[2])
        x2r2 = sdma(lambda E: E.dma_start(
            out=XST[:, 0:D], in_=x2_h.ap()[2 * 128:3 * 128, :]))

        def ln2_apply(j, src_ap, gates=()):
            wait("vector", sa_, aq2[j])
            for sem, val in gates:
                wait("vector", sem, val)
            em("vector", lambda E, j=j, src_ap=src_ap: E.tensor_scalar(
                out=TMP32[:, (j % 2) * D:(j % 2 + 1) * D],
                in0=src_ap,
                scalar1=STS[:, C_MEAN + j:C_MEAN + j + 1],
                scalar2=STS[:, C_RSTD + j:C_RSTD + j + 1],
                op0=ALU.subtract, op1=ALU.mult))
            ap2fl[j] = vflush()

        ln2_apply(0, SC1[:, 0 * D:1 * D])
        wo_evict1(0, 1, pws10[1], gates=((sx_, x2_dma[0]),))
        wo_evict1(1, 1, pws11[1])
        ln2_apply(1, SC1[:, 1 * D:2 * D])
        wo_evict1(0, 2, pws10[2], gates=((sx_, x2_dma[1]),))
        wo_evict1(1, 2, pws11[2])
        emit_transposes(0, (B, 0), [4, 5],
                        extra_gates=((sv_, ap2fl[0]), (sv_, wo_ev_v[(1, 1, 1)])))
        tpd[0] = C["p"]
        emit_transposes(D, (B, 1), [4, 5], extra_gates=((sv_, ap2fl[1]),))
        tpd[1] = C["p"]
        ln2_apply(2, XST[:, 0:D], gates=((sx_, x2r2), (sp_, tpd[0])))
        wo_evict1(0, 3, pws10[3], gates=((sx_, x2_dma[2]), (sa_, aq2[2])))
        wo_evict1(1, 3, pws11[3])
        emit_transposes(0, (B, 2), [4, 5], extra_gates=((sv_, ap2fl[2]),))
        tpd[2] = C["p"]
        ln2_apply(3, SC1[:, 3 * D:4 * D], gates=((sp_, tpd[1]),))
        ln2_ready_g0 = emit_transposes(D, (B, 3), [4, 5],
                                       extra_gates=((sv_, ap2fl[3]),))
        tpd[3] = C["p"]

        pws12 = wo_mms(1, 2, wo_strips(2), t1)
        for j in range(4):
            wo_evict1(2, j, pws12[j])
        pws13 = wo_mms(1, 3, wo_strips(3), t1)

        gu_slot, gu_dma, gu_prev, gu_div_v = {}, {}, {}, {}

        def gu_load(jp):
            s = ring_acquire()
            gu_slot[jp] = s
            gdma(lambda E, jp=jp, s=s: E.dma_start(
                out=rsl(s, 4096)[:, 0:2048], in_=gu_h.ap()[jp],
                max_dma_last_dim=2048))
            gu_dma[jp] = gdma(lambda E, jp=jp, s=s: E.dma_start(
                out=rsl(s, 4096)[:, 2048:4096], in_=gu_h.ap()[16 + jp],
                max_dma_last_dim=2048))

        def gu_mms(jp, tg, extra=()):
            s = gu_slot[jp]
            if tg == 0:
                bg, bu = 2, 3
            else:
                bg, bu = 4 + (jp % 2) * 2, 5 + (jp % 2) * 2
            wait("tensor", sw_, gu_dma[jp])
            for sem, val in extra:
                wait("tensor", sem, val)
            prev = gu_prev.get((tg, bg))
            if prev:
                wait("tensor", sv_, prev)
            for part, bank in ((0, bg), (1, bu)):
                for kc in range(KC):
                    mk = (lambda E, kc=kc, part=part, bank=bank, s=s, tg=tg:
                          E.matmul(
                              out=pb[bank][:, 0:512],
                              lhsT=rsl(s, 4096)[:, part * 2048 + kc * 128:
                                                part * 2048 + (kc + 1) * 128],
                              rhs=B[:, kc * 1024 + tg * 512:
                                    kc * 1024 + tg * 512 + 512],
                              start=(kc == 0), stop=(kc == KC - 1)))
                    if part == 1 and kc == KC - 1:
                        pg = pinc(mk)
                    else:
                        em("tensor", lambda E, f=mk: f(E))
            if tg == 1:
                ring_rel[s] = pg
            wait("scalar", sp_, pg)
            em("scalar", lambda E, tg=tg, bg=bg: E.activation(
                out=GT[:, tg * 512:(tg + 1) * 512], in_=pb[bg][:, 0:512],
                func=AF.Silu))
            ga = aflush()
            wait("vector", sa_, ga)
            em("vector", lambda E, jp=jp, tg=tg, bu=bu: E.tensor_tensor(
                out=A[:, jp * 1024 + tg * 512: jp * 1024 + tg * 512 + 512],
                in0=GT[:, tg * 512:(tg + 1) * 512],
                in1=pb[bu][:, 0:512], op=ALU.mult))
            fl = vflush()
            gu_div_v[(jp, tg)] = fl
            gu_prev[(tg, bg)] = fl

        xw5 = {}
        wait("sync", sv_, ap2fl[2])
        xw5[0] = sdma(lambda E: E.dma_start(
            out=XST[:, 0:D], in_=x_h.ap()[4 * 128:5 * 128, :]))
        wait("sync", sv_, ap2fl[3])
        xw5[1] = sdma(lambda E: E.dma_start(
            out=XST[:, D:2 * D], in_=x_h.ap()[5 * 128:6 * 128, :]))
        tpg1 = {}
        ln2_ready_g1 = None
        for j in range(4):
            t = 4 + j
            slot = G1SLOT5[j]
            wo_evict1(3, j, pws13[j])
            aqA = act_fin(j, with_sx=True)
            wait("vector", sa_, aqA)
            wait("vector", sp_, tpd[2 + j] if j < 2 else tpg1[j - 2])
            em("vector", lambda E, j=j, slot=slot: E.tensor_scalar(
                out=TMP32[:, (j % 2) * D:(j % 2 + 1) * D],
                in0=SC1[:, slot * D:(slot + 1) * D],
                scalar1=STS[:, C_MEAN + j:C_MEAN + j + 1],
                scalar2=STS[:, C_RSTD + j:C_RSTD + j + 1],
                op0=ALU.subtract, op1=ALU.mult))
            wait("vector", sx_, xw5[j])
            em("vector", lambda E, j=j, slot=slot: E.scalar_tensor_tensor(
                out=SC1[:, slot * D:(slot + 1) * D],
                in0=TMP32[:, (j % 2) * D:(j % 2 + 1) * D], scalar=1.0,
                in1=XST[:, (j % 2) * D:(j % 2 + 1) * D],
                op0=ALU.mult, op1=ALU.add,
                accum_out=STS[:, C_SX + j:C_SX + j + 1]))
            rfl = vflush()
            wait("sync", sv_, rfl)
            x2_dma[t] = sdma(lambda E, t=t, slot=slot: E.dma_start(
                out=x2_h.ap()[t * 128:(t + 1) * 128, :],
                in_=SC1[:, slot * D:(slot + 1) * D]))
            if j < 2:
                xw5[j + 2] = sdma(lambda E, j=j: E.dma_start(
                    out=XST[:, (j % 2) * D:(j % 2 + 1) * D],
                    in_=x_h.ap()[(6 + j) * 128:(7 + j) * 128, :]))
            wait("scalar", sv_, rfl)
            for q in range(4):
                em("scalar", lambda E, j=j, q=q, slot=slot: E.activation(
                    out=SQD[:, 0:512],
                    in_=SC1[:, slot * D + q * 512: slot * D + (q + 1) * 512],
                    func=AF.Square,
                    accum_out=STS[:, C_SQ4 + 4 * j + q:C_SQ4 + 4 * j + q + 1]))
            aq2g1 = act_fin(j, with_sx=False)
            wait("vector", sa_, aq2g1)
            em("vector", lambda E, j=j, slot=slot: E.tensor_scalar(
                out=TMP32[:, (j % 2) * D:(j % 2 + 1) * D],
                in0=SC1[:, slot * D:(slot + 1) * D],
                scalar1=STS[:, C_MEAN + j:C_MEAN + j + 1],
                scalar2=STS[:, C_RSTD + j:C_RSTD + j + 1],
                op0=ALU.subtract, op1=ALU.mult))
            ap2 = vflush()
            gates = [(sv_, ap2)]
            if j == 0:
                gates.append((sv_, wo_ev_v[(1, 2, 1)]))
            ln2_ready_g1 = emit_transposes((j % 2) * D, (B, t), [0, 1],
                                           extra_gates=tuple(gates))
            tpg1[j] = C["p"]
            gu_load(j)
            if j >= 1:
                extra = []
                if j == 1:
                    extra = [(sv_, wo_ev_v[(1, 2, 3)]), (sv_, ln2_ready_g0[0]),
                             (sa_, ln2_ready_g0[1])]
                gu_mms(j - 1, 0, extra=extra)
        gu_mms(3, 0)
        for j in range(4):
            extra = []
            if j == 0:
                extra = [(sv_, ln2_ready_g1[0]), (sa_, ln2_ready_g1[1]),
                         (sv_, wo_ev_v[(1, 3, 1)])]
            elif j == 1:
                extra = [(sv_, wo_ev_v[(1, 3, 3)])]
            gu_mms(j, 1, extra=extra)
        for jp in range(4, 16):
            gu_load(jp)
            gu_mms(jp, 0)
            gu_mms(jp, 1)
        inter_ready_v = gu_div_v[(15, 1)]

        dn_ev_v = {}
        G1SLOT = [4, 0, 1, 2]

        def dn_strips(fc):
            halves = []
            for half in range(2):
                s = ring_acquire()
                halves.append((s, gdma(
                    lambda E, fc=fc, half=half, s=s: E.dma_start(
                        out=rsl(s, 4096),
                        in_=dn_h.ap()[fc][:, half * 4096:(half + 1) * 4096],
                        max_dma_last_dim=2048))))
            return halves

        def dn_mms(g, fc, halves, tiles):
            fcp = fc % 2
            pws = []
            for j, t in enumerate(tiles):
                bank = fcp * 4 + j
                if g == 0 and fc == 0 and j == 0:
                    wait("tensor", sv_, inter_ready_v)
                prev_ev = dn_ev_v.get((g, fc - 2, j))
                if g == 1 and fc < 2:
                    prev_ev = dn_ev_v.get((0, fc + 2, j))
                if prev_ev:
                    wait("tensor", sv_, prev_ev)
                for ic in range(KC):
                    s, wdma = halves[ic // 8]
                    if ic % 8 == 0 and j == 0:
                        wait("tensor", sw_, wdma)
                    mk = (lambda E, ic=ic, t=t, bank=bank, s=s: E.matmul(
                        out=pb[bank][:, 0:512],
                        lhsT=A[:, ic * 1024 + t * 128: ic * 1024 + (t + 1) * 128],
                        rhs=rsl(s, 4096)[:, (ic % 8) * 512:(ic % 8 + 1) * 512],
                        start=(ic == 0), stop=(ic == KC - 1)))
                    if ic == KC - 1:
                        pws.append(pinc(mk))
                    else:
                        em("tensor", lambda E, f=mk: f(E))
            ring_rel[halves[0][0]] = pws[-1]
            ring_rel[halves[1][0]] = pws[-1]
            return pws

        def dn_evicts_g0(fc, pws):
            fcp = fc % 2
            for j in range(4):
                bank = fcp * 4 + j
                wait("vector", sp_, pws[j])
                em("vector", lambda E, j=j, fc=fc, bank=bank: E.tensor_scalar(
                    out=SC1[:, j * D + fc * 512: j * D + (fc + 1) * 512],
                    in0=pb[bank][:, 0:512], scalar1=1.0, scalar2=None,
                    op0=ALU.mult, op1=ALU.add,
                    accum_out=STS[:, C_XA + 4 * j + fc:C_XA + 4 * j + fc + 1]))
                dn_ev_v[(0, fc, j)] = vflush()

        def dn_evict1(fc, j, pw, gates=()):
            bank = (fc % 2) * 4 + j
            slot = G1SLOT[j]
            wait("vector", sp_, pw)
            for sem, val in gates:
                wait("vector", sem, val)
            em("vector", lambda E, j=j, fc=fc, bank=bank, slot=slot:
               E.tensor_scalar(
                   out=SC1[:, slot * D + fc * 512: slot * D + (fc + 1) * 512],
                   in0=pb[bank][:, 0:512], scalar1=1.0, scalar2=None,
                   op0=ALU.mult, op1=ALU.add,
                   accum_out=STS[:, C_XA + 4 * j + fc:C_XA + 4 * j + fc + 1]))
            fl = vflush()
            dn_ev_v[(1, fc, j)] = fl
            wait("scalar", sv_, fl)
            em("scalar", lambda E, j=j, fc=fc, slot=slot: E.activation(
                out=GT[:, 0:512],
                in_=SC1[:, slot * D + fc * 512: slot * D + (fc + 1) * 512],
                func=AF.Square,
                accum_out=STS[:, C_SQ4 + 4 * j + fc:C_SQ4 + 4 * j + fc + 1]))
            return fl

        for fc in range(4):
            halves = dn_strips(fc)
            pws = dn_mms(0, fc, halves, [0, 1, 2, 3])
            dn_evicts_g0(fc, pws)
        t1 = [4, 5, 6, 7]
        pws10 = dn_mms(1, 0, dn_strips(0), t1)
        pws11 = dn_mms(1, 1, dn_strips(1), t1)

        last_flush = dn_ev_v[(0, 3, 3)]
        ra = emit_ln_stats(4, lambda j: SC1[:, j * D:(j + 1) * D],
                           v_gate=last_flush, a_gate=last_flush, xa_mode=True)
        wait("vector", sa_, ra)
        g0_odma = {}
        xw7 = {}
        for j in (0, 1):
            xw7[j] = sdma(lambda E, t=j: E.dma_start(
                out=XST[:, (t % 2) * D:(t % 2 + 1) * D],
                in_=x2_h.ap()[t * 128:(t + 1) * 128, :]))
        for j in range(4):
            em("vector", lambda E, j=j: E.tensor_scalar(
                out=TMP32[:, (j % 2) * D:(j % 2 + 1) * D],
                in0=SC1[:, j * D:(j + 1) * D],
                scalar1=STS[:, C_MEAN + j:C_MEAN + j + 1],
                scalar2=STS[:, C_RSTD + j:C_RSTD + j + 1],
                op0=ALU.subtract, op1=ALU.mult))
            wait("vector", sx_, xw7[j])
            em("vector", lambda E, j=j: E.tensor_tensor(
                out=SC1[:, j * D:(j + 1) * D],
                in0=TMP32[:, (j % 2) * D:(j % 2 + 1) * D],
                in1=XST[:, (j % 2) * D:(j % 2 + 1) * D], op=ALU.add))
            rfl = vflush()
            wait("sync", sv_, rfl)
            g0_odma[j] = odma(lambda E, j=j: E.dma_start(
                out=out_h.ap()[j * 128:(j + 1) * 128, :],
                in_=SC1[:, j * D:(j + 1) * D]))
            if j < 2:
                xw7[j + 2] = sdma(lambda E, t=j + 2: E.dma_start(
                    out=XST[:, (t % 2) * D:(t % 2 + 1) * D],
                    in_=x2_h.ap()[t * 128:(t + 1) * 128, :]))
            if j == 0:
                dn_evict1(0, 0, pws10[0])
            else:
                dn_evict1(0, j, pws10[j], gates=((so_, g0_odma[j - 1]),))
        for j in range(4):
            dn_evict1(1, j, pws11[j])
        pws12 = dn_mms(1, 2, dn_strips(2), t1)
        for j in range(4):
            dn_evict1(2, j, pws12[j])
        pws13 = dn_mms(1, 3, dn_strips(3), t1)

        xw_g1 = {}
        for j in (0, 1):
            xw_g1[j] = sdma(lambda E, j=j: E.dma_start(
                out=XST[:, (j % 2) * D:(j % 2 + 1) * D],
                in_=x2_h.ap()[(4 + j) * 128:(5 + j) * 128, :]))
        for j in range(4):
            t = 4 + j
            slot = G1SLOT[j]
            dn_evict1(3, j, pws13[j])
            aqA = act_fin(j, with_sx=True)
            wait("vector", sa_, aqA)
            em("vector", lambda E, j=j, slot=slot: E.tensor_scalar(
                out=TMP32[:, (j % 2) * D:(j % 2 + 1) * D],
                in0=SC1[:, slot * D:(slot + 1) * D],
                scalar1=STS[:, C_MEAN + j:C_MEAN + j + 1],
                scalar2=STS[:, C_RSTD + j:C_RSTD + j + 1],
                op0=ALU.subtract, op1=ALU.mult))
            wait("vector", sx_, xw_g1[j])
            em("vector", lambda E, j=j, slot=slot: E.tensor_tensor(
                out=SC1[:, slot * D:(slot + 1) * D],
                in0=TMP32[:, (j % 2) * D:(j % 2 + 1) * D],
                in1=XST[:, (j % 2) * D:(j % 2 + 1) * D], op=ALU.add))
            rfl = vflush()
            wait("sync", sv_, rfl)
            if j < 2:
                xw_g1[j + 2] = sdma(lambda E, j=j: E.dma_start(
                    out=XST[:, (j % 2) * D:(j % 2 + 1) * D],
                    in_=x2_h.ap()[(6 + j) * 128:(7 + j) * 128, :]))
            odma(lambda E, t=t, slot=slot: E.dma_start(
                out=out_h.ap()[t * 128:(t + 1) * 128, :],
                in_=SC1[:, slot * D:(slot + 1) * D]))

        final_o = C["o"]
        em("sync", lambda E, v=final_o: E.wait_ge(so_, v))
        em("gpsimd", lambda E, v=final_o: E.wait_ge(so_, v))

        with nc.Block() as block:
            @block.sync
            def _(sync):
                for fn in P["sync"]:
                    fn(sync)

            @block.gpsimd
            def _(gpsimd):
                for fn in P["gpsimd"]:
                    fn(gpsimd)

            @block.vector
            def _(vector):
                for fn in P["vector"]:
                    fn(vector)

            @block.scalar
            def _(scalar):
                for fn in P["scalar"]:
                    fn(scalar)

            @block.tensor
            def _(tensor):
                for fn in P["tensor"]:
                    fn(tensor)

    return nc


def _strips(w, width):
    n = w.shape[1] // width
    out = np.empty((n, 128, 16 * width), dtype=np.float32)
    for i in range(n):
        s = w[:, i * width:(i + 1) * width].reshape(16, 128, width)
        out[i] = s.transpose(1, 0, 2).reshape(128, 16 * width)
    return np.ascontiguousarray(out)


def kernel(L0_attn_hidden_states, L0_attn_Wq, L0_attn_Wk, L0_attn_Wv,
           L0_attn_Wo, L0_attn_bo, L0_ffn_gate_up_proj, L0_ffn_down_proj):
    X = np.asarray(L0_attn_hidden_states, dtype=np.float32)
    Wq = np.asarray(L0_attn_Wq, dtype=np.float32)
    Wk = np.asarray(L0_attn_Wk, dtype=np.float32)
    Wv = np.asarray(L0_attn_Wv, dtype=np.float32)
    Wo = np.asarray(L0_attn_Wo, dtype=np.float32)
    bo = np.asarray(L0_attn_bo, dtype=np.float32).reshape(1, D)
    GU = np.asarray(L0_ffn_gate_up_proj, dtype=np.float32)
    DN = np.asarray(L0_ffn_down_proj, dtype=np.float32)

    wq_s = _strips(Wq, 128)
    wk_s = _strips(Wk, 512)
    wv_s = _strips(Wv, 512)
    wo_s = _strips(Wo, 512)
    ident = np.eye(128, dtype=np.float32)

    nc = build_program()
    in_maps = []
    for c in range(8):
        G, e = c // 4, c % 4
        xs = np.ascontiguousarray(
            np.concatenate([X[2 * G, e::4, :], X[2 * G + 1, e::4, :]], axis=0))
        gu_s = _strips(np.ascontiguousarray(GU[e]), 128)
        dn_s = _strips(np.ascontiguousarray(DN[e]), 512)
        in_maps.append({
            "x": xs, "wq": wq_s, "wk": wk_s, "wv": wv_s, "wo": wo_s,
            "bo": bo, "gu": gu_s, "dn": dn_s, "ident": ident,
        })
    res = run_bass_kernel_spmd(nc, in_maps, core_ids=list(range(8)))

    out = np.empty((4, 2048, D), dtype=np.float32)
    for c in range(8):
        G, e = c // 4, c % 4
        r = res.results[c]["out"]
        out[2 * G, e::4, :] = r[:512]
        out[2 * G + 1, e::4, :] = r[512:]
    return out


# version 17
# speedup vs baseline: 1.0036x; 1.0036x over previous
import numpy as np
import concourse.bass as bass
import concourse.mybir as mybir
from concourse.bass_utils import run_bass_kernel_spmd
from contextlib import ExitStack

F32 = mybir.dt.float32
BF16 = mybir.dt.bfloat16
AF = mybir.ActivationFunctionType
ALU = mybir.AluOpType
AX = mybir.AxisListType

NT = 8
D = 2048
KC = 16
H = 32
NPAIR = 32
EPS = 1e-5


def build_program():
    nc = bass.Bass("TRN2", target_bir_lowering=False)

    x_h = nc.dram_tensor("x", [1024, D], F32, kind="ExternalInput")
    wq_h = nc.dram_tensor("wq", [16, 128, 2048], F32, kind="ExternalInput")
    wk_h = nc.dram_tensor("wk", [4, 128, 8192], F32, kind="ExternalInput")
    wv_h = nc.dram_tensor("wv", [4, 128, 8192], F32, kind="ExternalInput")
    wo_h = nc.dram_tensor("wo", [4, 128, 8192], F32, kind="ExternalInput")
    bo_h = nc.dram_tensor("bo", [1, D], F32, kind="ExternalInput")
    gu_h = nc.dram_tensor("gu", [32, 128, 2048], F32, kind="ExternalInput")
    dn_h = nc.dram_tensor("dn", [4, 128, 8192], F32, kind="ExternalInput")
    id_h = nc.dram_tensor("ident", [128, 128], F32, kind="ExternalInput")
    out_h = nc.dram_tensor("out", [1024, D], F32, kind="ExternalOutput")

    cc_in = nc.dram_tensor("cc_in", [128, NPAIR * 65], F32)
    cc_out = nc.dram_tensor("cc_out", [128, NPAIR * 65], F32)
    x2_h = nc.dram_tensor("x2scr", [1024, D], F32)

    with ExitStack() as st:
        def sb(name, shape, dt):
            return st.enter_context(nc.sbuf_tensor(name, shape, dt))

        A = sb("A", [128, KC * 1024], BF16)
        B = sb("B", [128, KC * 1024], BF16)
        SC1 = sb("SC1", [128, 5 * D], F32)
        TMP32 = sb("TMP32", [128, 2 * D], F32)
        XST = sb("XST", [128, 2 * D], F32)
        KTM = sb("KTM", [128, 2 * 512], BF16)
        V65 = sb("V65", [128, 2 * 520], BF16)
        KV65 = sb("KV65", [128, 2 * H * 65], BF16)
        RING = sb("RING", [128, 5 * 4096], BF16)
        BOP = sb("BOP", [128, D], BF16)
        E1 = sb("E1", [128, 128], BF16)
        IDS = sb("IDS", [128, 128], F32)
        GT = sb("GT", [128, 1024], BF16)
        STS = sb("STS", [128, 240], F32)
        CST = sb("CST", [128, 6], F32)
        FL = sb("FL", [128, 4], F32)
        SQD = sb("SQD", [128, 512], F32)

        PBIG = st.enter_context(nc.psum_tensor("PBIG", [128, 4096], F32))
        pb = [PBIG[:, i * 512:(i + 1) * 512] for i in range(8)]

        sx_ = st.enter_context(nc.semaphore("sx"))
        sw_ = st.enter_context(nc.semaphore("sw"))
        sv_ = st.enter_context(nc.semaphore("sv"))
        sa_ = st.enter_context(nc.semaphore("sa"))
        sp_ = st.enter_context(nc.semaphore("sp"))
        sc_ = st.enter_context(nc.semaphore("scc"))
        so_ = st.enter_context(nc.semaphore("so"))
        sems = [sx_, sw_, sv_, sa_, sp_, sc_, so_]
        nums = sorted(s.num for s in sems)
        assert nums[-1] - nums[0] == len(sems) - 1, nums

        C_SX, C_SXX, C_MEAN, C_VAR, C_MSQ, C_LNV, C_RSTD = 0, 8, 16, 24, 32, 40, 48
        C_SQ4, C_T4, C_NM, C_LNN, C_RC = 56, 88, 96, 136, 176
        C_XA = 212

        P = {"sync": [], "gpsimd": [], "vector": [], "scalar": [], "tensor": []}
        C = {"x": 0, "w": 0, "v": 0, "a": 0, "p": 0, "cc": 0, "o": 0}

        def em(eng, fn):
            P[eng].append(fn)

        def wait(eng, sem, val):
            if val > 0:
                em(eng, lambda E, s=sem, v=val: E.wait_ge(s, v))

        def vflush():
            em("vector", lambda E: E.memset(FL[:, 0:1], 0.0))
            C["v"] += 1
            em("vector", lambda E: E.memset(FL[:, 1:2], 0.0).then_inc(sv_, 1))
            return C["v"]

        def aflush():
            em("scalar", lambda E: E.mul(out=FL[:, 2:3], in_=CST[:, 0:1], mul=1.0))
            C["a"] += 1
            em("scalar", lambda E: E.mul(out=FL[:, 3:4], in_=CST[:, 0:1], mul=1.0)
               .then_inc(sa_, 1))
            return C["a"]

        def pinc(mk):
            C["p"] += 1
            em("tensor", lambda E, f=mk: f(E).then_inc(sp_, 1))
            return C["p"]

        def sdma(mk):
            C["x"] += 16
            em("sync", lambda E, f=mk: f(E).then_inc(sx_, 16))
            return C["x"]

        def gdma(mk):
            C["w"] += 16
            em("gpsimd", lambda E, f=mk: f(E).then_inc(sw_, 16))
            return C["w"]

        def odma(mk):
            C["o"] += 16
            em("sync", lambda E, f=mk: f(E).then_inc(so_, 16))
            return C["o"]

        ring_rel = [0] * 5
        ring_ptr = [0]

        def ring_acquire():
            s = ring_ptr[0]
            ring_ptr[0] = (s + 1) % 5
            if ring_rel[s] > 0:
                wait("gpsimd", sp_, ring_rel[s])
            return s

        def rsl(s, nels):
            return RING[:, s * 4096: s * 4096 + nels]

        with nc.Block() as blk0:
            @blk0.gpsimd
            def _(gpsimd):
                gpsimd.sem_clear(range(nums[0], nums[-1] + 1))

        with nc.Block() as blk1:
            @blk1.vector
            def _(vector):
                vector.memset(CST[:, 0:1], EPS)
                vector.memset(CST[:, 1:2], -0.5)
                vector.memset(CST[:, 2:3], -1.0)
                vector.memset(CST[:, 3:4], 0.0)
                vector.memset(CST[:, 4:5], 1.0 / D)
                vector.memset(CST[:, 5:6], float(D) ** -0.5)
                vector.memset(E1[:, :], 0.0)
                vector.memset(E1[0:1, :], 1.0)
                vector.memset(BOP[:, :], 0.0)
                vector.memset(KV65[:, :], 0.0)
                for t in range(2):
                    vector.memset(
                        V65[:, t * 520:(t + 1) * 520]
                        .rearrange("p (h e) -> p h e", e=65)[:, :, 64:65], 1.0)

        ident_dma = sdma(lambda E: E.dma_start(out=IDS[:, :], in_=id_h.ap()))
        bo_dma = gdma(lambda E: E.dma_start(out=BOP[0:1, :], in_=bo_h.ap()))

        tp_last = {}

        def emit_ln_stats(n, tile_ap_fn, v_gate=0, a_gate=0, x_waits=None,
                          xa_mode=False, sx_ready=False):
            if a_gate:
                wait("scalar", sv_, a_gate)
            if v_gate:
                wait("vector", sv_, v_gate)
            for j in range(n):
                if x_waits is not None:
                    wait("vector", sx_, x_waits[j])
                    wait("scalar", sx_, x_waits[j])
                if sx_ready:
                    pass
                elif xa_mode:
                    em("vector", lambda E, j=j: E.tensor_scalar(
                        out=STS[:, C_T4:C_T4 + 4],
                        in0=STS[:, C_XA + 4 * j:C_XA + 4 * j + 4],
                        scalar1=1.0, scalar2=None, op0=ALU.mult, op1=ALU.add,
                        accum_out=STS[:, C_SX + j:C_SX + j + 1]))
                else:
                    em("vector", lambda E, ap=tile_ap_fn(j), j=j: E.reduce_sum(
                        out=STS[:, C_SX + j:C_SX + j + 1], in_=ap, axis=AX.X))
                for q in range(4):
                    apq = tile_ap_fn(j)[:, q * 512:(q + 1) * 512]
                    em("scalar", lambda E, apq=apq, j=j, q=q: E.activation(
                        out=GT[:, 0:512], in_=apq, func=AF.Square,
                        accum_out=STS[:, C_SQ4 + 4 * j + q:C_SQ4 + 4 * j + q + 1]))
            av = aflush()
            vv = vflush()
            wait("vector", sv_, vv)
            wait("vector", sa_, av)
            for j in range(n):
                em("vector", lambda E, j=j: E.tensor_scalar(
                    out=STS[:, C_T4:C_T4 + 4],
                    in0=STS[:, C_SQ4 + 4 * j:C_SQ4 + 4 * j + 4],
                    scalar1=1.0, scalar2=None, op0=ALU.mult, op1=ALU.add,
                    accum_out=STS[:, C_SXX + j:C_SXX + j + 1]))
            vv = vflush()
            wait("vector", sv_, vv)
            em("vector", lambda E, n=n: E.tensor_scalar(
                out=STS[:, C_MEAN:C_MEAN + n], in0=STS[:, C_SX:C_SX + n],
                scalar1=1.0 / D, scalar2=None, op0=ALU.mult))
            em("vector", lambda E, n=n: E.tensor_scalar(
                out=STS[:, C_VAR:C_VAR + n], in0=STS[:, C_SXX:C_SXX + n],
                scalar1=1.0 / D, scalar2=None, op0=ALU.mult))
            vv = vflush()
            wait("vector", sv_, vv)
            em("vector", lambda E, n=n: E.tensor_tensor(
                out=STS[:, C_MSQ:C_MSQ + n], in0=STS[:, C_MEAN:C_MEAN + n],
                in1=STS[:, C_MEAN:C_MEAN + n], op=ALU.mult))
            vv = vflush()
            wait("vector", sv_, vv)
            em("vector", lambda E, n=n: E.tensor_tensor(
                out=STS[:, C_VAR:C_VAR + n], in0=STS[:, C_VAR:C_VAR + n],
                in1=STS[:, C_MSQ:C_MSQ + n], op=ALU.subtract))
            vv = vflush()
            wait("scalar", sv_, vv)
            em("scalar", lambda E, n=n: E.activation(
                out=STS[:, C_LNV:C_LNV + n], in_=STS[:, C_VAR:C_VAR + n],
                func=AF.Ln, bias=CST[:, 0:1]))
            em("scalar", lambda E, n=n: E.activation(
                out=STS[:, C_RSTD:C_RSTD + n], in_=STS[:, C_LNV:C_LNV + n],
                func=AF.Exp, scale=CST[:, 1:2]))
            return aflush()

        def emit_transposes(src_col, dst, bank_pair, extra_gates=()):
            buf, t = dst
            key = tuple(bank_pair)
            prev = tp_last.get(key)
            if prev:
                wait("tensor", sv_, prev[0])
                wait("tensor", sa_, prev[1])
            for sem, val in extra_gates:
                wait("tensor", sem, val)
            ev = {}
            for grp in range(4):
                bk = bank_pair[grp % 2]
                if grp >= 2:
                    sem, val = ev[grp - 2]
                    wait("tensor", sem, val)
                for q in range(4):
                    kc = grp * 4 + q
                    mk = (lambda E, kc=kc, bk=bk, q=q, src_col=src_col:
                          E.transpose(
                              out=pb[bk][:, q * 128:(q + 1) * 128],
                              in_=TMP32[:, src_col + kc * 128: src_col + (kc + 1) * 128],
                              identity=IDS[:, :]))
                    if q < 3:
                        em("tensor", lambda E, f=mk: f(E))
                    else:
                        pv = pinc(mk)
                dview = buf[:, :].rearrange("p (kc tt) -> p kc tt", tt=1024)[
                    :, grp * 4:(grp + 1) * 4, t * 128:(t + 1) * 128]
                if grp % 2 == 0:
                    wait("vector", sp_, pv)
                    em("vector", lambda E, bk=bk, dview=dview: E.tensor_copy(
                        out=dview, in_=pb[bk][:, 0:512]))
                    C["v"] += 1
                    em("vector", lambda E: E.memset(FL[:, 0:1], 0.0)
                       .then_inc(sv_, 1))
                    ev[grp] = (sv_, C["v"])
                else:
                    wait("scalar", sp_, pv)
                    em("scalar", lambda E, bk=bk, dview=dview: E.activation(
                        out=dview, in_=pb[bk][:, 0:512], func=AF.Copy))
                    C["a"] += 1
                    em("scalar", lambda E: E.mul(out=FL[:, 2:3],
                                                 in_=CST[:, 0:1], mul=1.0)
                       .then_inc(sa_, 1))
                    ev[grp] = (sa_, C["a"])
            vg = vflush()
            ag = aflush()
            tp_last[key] = (vg, ag)
            return (vg, ag)

        k_ev_a, v_ev_v = {}, {}
        kv_state = {"flush": 0}

        def emit_kv(t, fc):
            ti = fc * NT + t
            bt = t // 4
            wait("tensor", sa_, k_ev_a[ti])
            wait("tensor", sv_, v_ev_v[ti])
            if t == 0 and kv_state["flush"]:
                wait("tensor", sv_, kv_state["flush"])
            done = None
            for j in range(4):
                for par in range(2):
                    hh = 2 * j + par
                    bank = 4 + bt * 2 + j // 2
                    col = (j % 2) * 130 + par * 65
                    first = (t % 4 == 0) and (j % 2 == 0) and (par == 0)
                    mk = (lambda E, t=t, j=j, hh=hh, bank=bank, col=col,
                          first=first: E.matmul(
                              out=pb[bank][:, col:col + 65],
                              lhsT=KTM[:, (t % 2) * 512 + j * 128: (t % 2) * 512 + (j + 1) * 128],
                              rhs=V65[:, (t % 2) * 520 + hh * 65: (t % 2) * 520 + (hh + 1) * 65],
                              start=first, stop=(t % 4 == 3),
                              skip_group_check=not first))
                    if j == 3 and par == 1:
                        done = pinc(mk)
                    else:
                        em("tensor", lambda E, f=mk: f(E))
            return done

        def emit_kproj(t, fc, bank, strip_waits):
            for kc in range(KC):
                s, wv_ = strip_waits[kc // 8]
                if kc % 8 == 0:
                    wait("tensor", sw_, wv_)
                mk = (lambda E, kc=kc, t=t, bank=bank, s=s: E.matmul(
                    out=pb[bank][:, 0:512],
                    lhsT=A[:, kc * 1024 + t * 128: kc * 1024 + (t + 1) * 128],
                    rhs=rsl(s, 4096)[:, (kc % 8) * 512:(kc % 8 + 1) * 512],
                    start=(kc == 0), stop=(kc == KC - 1)))
                if kc < KC - 1:
                    em("tensor", lambda E, f=mk: f(E))
                else:
                    pk = pinc(mk)
            return pk

        fc0_k, fc0_v = [], []
        for half in range(2):
            s = ring_acquire()
            fc0_k.append((s, gdma(lambda E, half=half, s=s: E.dma_start(
                out=rsl(s, 4096),
                in_=wk_h.ap()[0][:, half * 4096:(half + 1) * 4096],
                max_dma_last_dim=2048))))
        for half in range(2):
            s = ring_acquire()
            fc0_v.append((s, gdma(lambda E, half=half, s=s: E.dma_start(
                out=rsl(s, 4096),
                in_=wv_h.ap()[0][:, half * 4096:(half + 1) * 4096],
                max_dma_last_dim=2048))))

        apply_fl = {}
        pend = {}
        kv_done_p = None
        tpB_pinc = {}

        def fm_dst(buf, t, kc0):
            return buf[:, :].rearrange("p (kc tt) -> p kc tt", tt=1024)[
                :, kc0:kc0 + 8, t * 128:(t + 1) * 128]

        def flush_pending(t):
            p = pend.pop(t)
            wait("vector", sp_, p["pv2"])
            em("vector", lambda E, t=t: E.tensor_copy(
                out=V65[:, (t % 2) * 520:(t % 2) * 520 + 520]
                .rearrange("p (h e) -> p h e", e=65)[:, :, 0:64],
                in_=pb[1][:, 0:512].rearrange("p (h f) -> p h f", f=64)))
            v_ev_v[t] = vflush()
            wait("scalar", sp_, p["tpB"])
            em("scalar", lambda E, t=t: E.activation(
                out=fm_dst(A, t, 8), in_=PBIG[:, 1024:2048], func=AF.Copy))
            wait("scalar", sp_, p["pk"])
            em("scalar", lambda E, t=t: E.activation(
                out=KTM[:, (t % 2) * 512:(t % 2) * 512 + 512],
                in_=pb[0][:, 0:512], func=AF.Relu))
            k_ev_a[t] = aflush()

        for g in range(2):
            tiles = list(range(4 * g, 4 * g + 4))
            if g == 1:
                flush_pending(3)
                wait("sync", sv_, apply_fl[3])
                wait("sync", sa_, C["a"])
            xw = []
            for j, t in enumerate(tiles):
                xw.append(sdma(lambda E, t=t, j=j: E.dma_start(
                    out=SC1[:, j * D:(j + 1) * D],
                    in_=x_h.ap()[t * 128:(t + 1) * 128, :])))
            ra = emit_ln_stats(4, lambda j: SC1[:, j * D:(j + 1) * D], x_waits=xw)
            wait("vector", sa_, ra)
            for j, t in enumerate(tiles):
                slot = t % 2
                if t >= 2:
                    wait("vector", sp_, tpB_pinc[t - 2])
                em("vector", lambda E, j=j, slot=slot: E.tensor_scalar(
                    out=TMP32[:, slot * D:(slot + 1) * D],
                    in0=SC1[:, j * D:(j + 1) * D],
                    scalar1=STS[:, C_MEAN + j:C_MEAN + j + 1],
                    scalar2=STS[:, C_RSTD + j:C_RSTD + j + 1],
                    op0=ALU.subtract, op1=ALU.mult))
                apply_fl[t] = vflush()
                if t >= 1 and (t - 1) in pend:
                    flush_pending(t - 1)
                wait("tensor", sv_, apply_fl[t])
                if t == 0:
                    wait("tensor", sx_, ident_dma)
                else:
                    wait("tensor", sa_, k_ev_a[t - 1])
                    wait("tensor", sv_, v_ev_v[t - 1])
                for q in range(8):
                    kc = q
                    bk = q // 4
                    mk = (lambda E, kc=kc, bk=bk, q=q, slot=slot: E.transpose(
                        out=pb[bk][:, (q % 4) * 128:(q % 4 + 1) * 128],
                        in_=TMP32[:, slot * D + kc * 128: slot * D + (kc + 1) * 128],
                        identity=IDS[:, :]))
                    if q < 7:
                        em("tensor", lambda E, f=mk: f(E))
                    else:
                        tpA_p = pinc(mk)
                wait("vector", sp_, tpA_p)
                em("vector", lambda E, t=t: E.tensor_copy(
                    out=fm_dst(A, t, 0), in_=PBIG[:, 0:1024]))
                eaf = vflush()
                for q in range(8):
                    kc = 8 + q
                    bk = 2 + q // 4
                    mk = (lambda E, kc=kc, bk=bk, q=q, slot=slot: E.transpose(
                        out=pb[bk][:, (q % 4) * 128:(q % 4 + 1) * 128],
                        in_=TMP32[:, slot * D + kc * 128: slot * D + (kc + 1) * 128],
                        identity=IDS[:, :]))
                    if q < 7:
                        em("tensor", lambda E, f=mk: f(E))
                    else:
                        tpB_pinc[t] = pinc(mk)
                if t >= 1:
                    kv_done_p = emit_kv(t - 1, 0)
                wait("tensor", sv_, eaf)
                pk = emit_kproj(t, 0, 0, fc0_k)
                pv2 = emit_kproj(t, 0, 1, fc0_v)
                pend[t] = {"pv2": pv2, "tpB": tpB_pinc[t], "pk": pk}
                if t == NT - 1:
                    ring_rel[fc0_k[0][0]] = pk
                    ring_rel[fc0_k[1][0]] = pk
                    ring_rel[fc0_v[0][0]] = pv2
                    ring_rel[fc0_v[1][0]] = pv2
        flush_pending(NT - 1)
        kv_done_p = emit_kv(NT - 1, 0)
        ln1_ready = (C["v"], C["a"])

        em("vector", lambda E: E.memset(SC1[:, 0:NPAIR * 65], 0.0))
        kvz_v = vflush()

        def emit_kv_evict(fc):
            wait("vector", sp_, kv_done_p)
            if fc == 0:
                wait("vector", sv_, kvz_v)
            for bt in range(2):
                for q in range(2):
                    bank = 4 + bt * 2 + q
                    for par in range(2):
                        gp0 = bt * 16 + fc * 4 + 2 * q
                        em("vector", lambda E, bank=bank, par=par, gp0=gp0:
                           E.tensor_copy(
                               out=SC1[par * 64:(par + 1) * 64,
                                       gp0 * 65:(gp0 + 2) * 65]
                               .rearrange("p (n e) -> p n e", e=65),
                               in_=pb[bank][par * 64:(par + 1) * 64, 0:260]
                               .rearrange("p (n e) -> p n e", e=130)
                               [:, :, par * 65: par * 65 + 65]))
            kv_state["flush"] = vflush()

        emit_kv_evict(0)

        for fc in range(1, 4):
            wkw, wvw = [], []
            for half in range(2):
                s = ring_acquire()
                wkw.append((s, gdma(lambda E, fc=fc, half=half, s=s: E.dma_start(
                    out=rsl(s, 4096),
                    in_=wk_h.ap()[fc][:, half * 4096:(half + 1) * 4096],
                    max_dma_last_dim=2048))))
            for half in range(2):
                s = ring_acquire()
                wvw.append((s, gdma(lambda E, fc=fc, half=half, s=s: E.dma_start(
                    out=rsl(s, 4096),
                    in_=wv_h.ap()[fc][:, half * 4096:(half + 1) * 4096],
                    max_dma_last_dim=2048))))
            for t in range(NT):
                ti = fc * NT + t
                bkk = t % 2
                bkv = 2 + t % 2
                if fc == 1 and t < 2:
                    wait("tensor", sa_, k_ev_a[7])
                    wait("tensor", sv_, v_ev_v[7])
                if ti >= 2:
                    wait("tensor", sa_, k_ev_a[ti - 2])
                pk = emit_kproj(t, fc, bkk, wkw)
                if t == NT - 1:
                    ring_rel[wkw[0][0]] = pk
                    ring_rel[wkw[1][0]] = pk
                wait("scalar", sp_, pk)
                em("scalar", lambda E, t=t, bkk=bkk: E.activation(
                    out=KTM[:, (t % 2) * 512:(t % 2) * 512 + 512],
                    in_=pb[bkk][:, 0:512], func=AF.Relu))
                k_ev_a[ti] = aflush()
                if ti >= 2:
                    wait("tensor", sv_, v_ev_v[ti - 2])
                pv2 = emit_kproj(t, fc, bkv, wvw)
                if t == NT - 1:
                    ring_rel[wvw[0][0]] = pv2
                    ring_rel[wvw[1][0]] = pv2
                wait("vector", sp_, pv2)
                em("vector", lambda E, t=t, bkv=bkv: E.tensor_copy(
                    out=V65[:, (t % 2) * 520:(t % 2) * 520 + 520]
                    .rearrange("p (h e) -> p h e", e=65)[:, :, 0:64],
                    in_=pb[bkv][:, 0:512].rearrange("p (h f) -> p h f", f=64)))
                v_ev_v[ti] = vflush()
                if t >= 1:
                    kv_done_p = emit_kv(t - 1, fc)
            kv_done_p = emit_kv(NT - 1, fc)
            emit_kv_evict(fc)
        kv_ev_flush = kv_state["flush"]

        wait("sync", sv_, kv_ev_flush)
        cc_dma = sdma(lambda E: E.dma_start(out=cc_in.ap(),
                                            in_=SC1[:, 0:NPAIR * 65]))
        early_wq = []
        for fq in range(3):
            s = ring_acquire()
            early_wq.append((s, gdma(lambda E, fq=fq, s=s: E.dma_start(
                out=rsl(s, 2048),
                in_=wq_h.ap()[fq], max_dma_last_dim=2048))))
        wait("gpsimd", sx_, cc_dma)
        C["cc"] += 1
        em("gpsimd", lambda E: E.collective_compute(
            "AllReduce", ALU.add,
            replica_groups=[[0, 1, 2, 3], [4, 5, 6, 7]],
            ins=[cc_in.ap().opt()], outs=[cc_out.ap().opt()]).then_inc(sc_, 1))

        q_ev_a = {}
        for fq in range(16):
            if fq < 3:
                s, wq_dma = early_wq[fq]
            else:
                s = ring_acquire()
                wq_dma = gdma(lambda E, fq=fq, s=s: E.dma_start(
                    out=rsl(s, 2048),
                    in_=wq_h.ap()[fq], max_dma_last_dim=2048))
            if fq == 0:
                wait("tensor", sv_, ln1_ready[0])
                wait("tensor", sa_, ln1_ready[1])
            if fq >= 2:
                wait("tensor", sa_, q_ev_a[fq - 2])
            wait("tensor", sw_, wq_dma)
            for kc in range(KC):
                for tg in range(2):
                    bank = (fq % 2) * 2 + tg
                    mk = (lambda E, kc=kc, tg=tg, bank=bank, s=s: E.matmul(
                        out=pb[bank][:, 0:512],
                        lhsT=rsl(s, 2048)[:, kc * 128:(kc + 1) * 128],
                        rhs=A[:, kc * 1024 + tg * 512: kc * 1024 + (tg + 1) * 512],
                        start=(kc == 0), stop=(kc == KC - 1)))
                    if kc == KC - 1 and tg == 1:
                        pq = pinc(mk)
                    else:
                        em("tensor", lambda E, f=mk: f(E))
            ring_rel[s] = pq
            wait("scalar", sp_, pq)
            for tg in range(2):
                bank = (fq % 2) * 2 + tg
                em("scalar", lambda E, fq=fq, tg=tg, bank=bank: E.activation(
                    out=B[:, fq * 1024 + tg * 512: fq * 1024 + (tg + 1) * 512],
                    in_=pb[bank][:, 0:512], func=AF.Relu))
            q_ev_a[fq] = aflush()
        q_ready_a = q_ev_a[15]

        wait("gpsimd", sc_, C["cc"])
        kv_ready_w = 0
        for par in range(2):
            src_ = cc_out.ap().rearrange("p (b fc jp e) -> p b fc jp e",
                                         b=2, fc=4, jp=4)[par * 64:(par + 1) * 64]
            dst = KV65[par * 64:(par + 1) * 64, :].rearrange(
                "p (b fc jp two e) -> p b fc jp two e",
                b=2, fc=4, jp=4, two=2)[:, :, :, :, par, :]
            kv_ready_w = gdma(lambda E, src_=src_, dst=dst: E.dma_start(
                out=dst, in_=src_))

        ROUNDS = [7, 7, 7, 7, 4]
        div_fl = {}
        tp_done = {}
        tile_inner = {}
        inner_ready = (0, 0)
        wo_ev_v = {}
        wo0_halves = []
        for half in range(2):
            s = ring_acquire()
            wo0_halves.append((s, gdma(
                lambda E, half=half, s=s: E.dma_start(
                    out=rsl(s, 4096),
                    in_=wo_h.ap()[0][:, half * 4096:(half + 1) * 4096],
                    max_dma_last_dim=2048))))
        wo0_pw = {}
        for t in range(NT):
            bt = t // 4
            if t == 0:
                wait("tensor", sw_, kv_ready_w)
                wait("tensor", sa_, q_ready_a)
            if t >= 2:
                wait("vector", sp_, tp_done[t - 2])
            if 2 <= t and t - 2 < 4:
                tw = t - 2
                gates = tile_inner[tw]
                wait("tensor", sv_, gates[0])
                wait("tensor", sa_, gates[1])
                if tw == 0:
                    wait("tensor", sw_, bo_dma)
                    wait("tensor", sw_, wo0_halves[1][1])
                else:
                    wait("tensor", sv_, wo_ev_v[(0, 0, tw - 1)])
                for ic in range(KC):
                    s = wo0_halves[ic // 8][0]
                    mk = (lambda E, ic=ic, tw=tw, s=s: E.matmul(
                        out=pb[7][:, 0:512],
                        lhsT=A[:, ic * 1024 + tw * 128: ic * 1024 + (tw + 1) * 128],
                        rhs=rsl(s, 4096)[:, (ic % 8) * 512:(ic % 8 + 1) * 512],
                        start=(ic == 0), stop=False))
                    em("tensor", lambda E, f=mk: f(E))
                mkb = (lambda E: E.matmul(
                    out=pb[7][:, 0:512], lhsT=E1[:, :],
                    rhs=BOP[:, 0:512],
                    start=False, stop=True, skip_group_check=True))
                wo0_pw[tw] = pinc(mkb)
                wait("vector", sp_, wo0_pw[tw])
                if tw == 0:
                    wait("vector", sx_, cc_dma)
                em("vector", lambda E, tw=tw: E.tensor_scalar(
                    out=SC1[:, tw * D: tw * D + 512],
                    in0=pb[7][:, 0:512], scalar1=1.0, scalar2=None,
                    op0=ALU.mult, op1=ALU.add,
                    accum_out=STS[:, C_XA + 4 * tw:C_XA + 4 * tw + 1]))
                wo_ev_v[(0, 0, tw)] = vflush()
            if t >= 1:
                wait("tensor", sv_, div_fl[t - 1])
                tile_inner[t - 1] = emit_transposes(
                    ((t - 1) % 2) * D, (A, t - 1), [5, 6])
                tp_done[t - 1] = C["p"]
            h = 0
            for r, nh in enumerate(ROUNDS):
                for i in range(nh):
                    mk = (lambda E, h=h, i=i, t=t, r=r, bt=bt, nh=nh: E.matmul(
                        out=PBIG[:, r * 512 + i * 65: r * 512 + (i + 1) * 65],
                        lhsT=B[:, (h // 2) * 1024 + t * 128:
                               (h // 2) * 1024 + (t + 1) * 128],
                        rhs=KV65[:, (bt * H + h) * 65:(bt * H + h + 1) * 65],
                        start=(i == 0), stop=(i == nh - 1),
                        skip_group_check=(i != 0)))
                    if r == len(ROUNDS) - 1 and i == nh - 1:
                        pr = pinc(mk)
                    else:
                        em("tensor", lambda E, f=mk: f(E))
                    h += 1
            wait("vector", sp_, pr)
            em("vector", lambda E: E.tensor_scalar(
                out=STS[:, C_NM:C_NM + 35].rearrange("p (r i) -> p r i", i=7),
                in0=PBIG[:, :].rearrange("p (r x) -> p r x", x=512)
                [:, 0:5, 0:455]
                .rearrange("p r (i e) -> p r i e", e=65)[:, :, :, 64:65],
                scalar1=1e-6, scalar2=None, op0=ALU.max))
            nv = vflush()
            wait("scalar", sv_, nv)
            em("scalar", lambda E: E.activation(
                out=STS[:, C_LNN:C_LNN + 35], in_=STS[:, C_NM:C_NM + 35],
                func=AF.Ln))
            em("scalar", lambda E: E.activation(
                out=STS[:, C_RC:C_RC + 35], in_=STS[:, C_LNN:C_LNN + 35],
                func=AF.Exp, scale=CST[:, 2:3]))
            ra = aflush()
            wait("vector", sa_, ra)
            slot = t % 2
            in0a = PBIG[:, :].rearrange("p (r x) -> p r x", x=512)[:, 0:4, 0:455] \
                .rearrange("p r (i e) -> p r i e", e=65)[:, :, :, 0:64]
            in1a = STS[:, C_RC:C_RC + 28].rearrange(
                "p (r i one) -> p r i one", r=4, i=7, one=1)
            a0, a1 = bass.broadcast_tensor_aps(in0a, in1a)
            outa = TMP32[:, slot * D: slot * D + 1792].rearrange(
                "p (r i e) -> p r i e", i=7, e=64)
            em("vector", lambda E, a0=a0, a1=a1, outa=outa: E.tensor_tensor(
                out=outa, in0=a0, in1=a1, op=ALU.mult))
            in0b = PBIG[:, 4 * 512: 4 * 512 + 4 * 65].rearrange(
                "p (i e) -> p i e", e=65)[:, :, 0:64]
            in1b = STS[:, C_RC + 28:C_RC + 32].rearrange(
                "p (i one) -> p i one", one=1)
            b0, b1 = bass.broadcast_tensor_aps(in0b, in1b)
            outb = TMP32[:, slot * D + 1792: slot * D + 2048].rearrange(
                "p (i e) -> p i e", e=64)
            em("vector", lambda E, b0=b0, b1=b1, outb=outb: E.tensor_tensor(
                out=outb, in0=b0, in1=b1, op=ALU.mult))
            div_fl[t] = vflush()
        wait("tensor", sv_, div_fl[NT - 1])
        tile_inner[NT - 1] = emit_transposes(
            ((NT - 1) % 2) * D, (A, NT - 1), [5, 6])
        tp_done[NT - 1] = C["p"]
        ring_rel[wo0_halves[0][0]] = wo0_pw[3]
        ring_rel[wo0_halves[1][0]] = wo0_pw[3]
        inner_ready = (C["v"], C["a"])

        x2_dma = {}

        def wo_strips(fc):
            halves = []
            for half in range(2):
                s = ring_acquire()
                halves.append((s, gdma(
                    lambda E, fc=fc, half=half, s=s: E.dma_start(
                        out=rsl(s, 4096),
                        in_=wo_h.ap()[fc][:, half * 4096:(half + 1) * 4096],
                        max_dma_last_dim=2048))))
            return halves

        def wo_mms(g, fc, halves, tiles):
            fcp = fc % 2
            pws = []
            for j, t in enumerate(tiles):
                bank = fcp * 4 + j
                if g == 0 and fc == 0 and j == 0:
                    wait("tensor", sv_, inner_ready[0])
                    wait("tensor", sa_, inner_ready[1])
                    wait("tensor", sw_, bo_dma)
                prev_ev = wo_ev_v.get((g, fc - 2, j))
                if g == 1 and fc < 2:
                    prev_ev = wo_ev_v.get((0, fc + 2, j))
                if prev_ev:
                    wait("tensor", sv_, prev_ev)
                for ic in range(KC):
                    s, wdma = halves[ic // 8]
                    if ic % 8 == 0 and j == 0:
                        wait("tensor", sw_, wdma)
                    mk = (lambda E, ic=ic, t=t, bank=bank, s=s: E.matmul(
                        out=pb[bank][:, 0:512],
                        lhsT=A[:, ic * 1024 + t * 128: ic * 1024 + (t + 1) * 128],
                        rhs=rsl(s, 4096)[:, (ic % 8) * 512:(ic % 8 + 1) * 512],
                        start=(ic == 0), stop=False))
                    em("tensor", lambda E, f=mk: f(E))
                mkb = (lambda E, fc=fc, bank=bank: E.matmul(
                    out=pb[bank][:, 0:512], lhsT=E1[:, :],
                    rhs=BOP[:, fc * 512:(fc + 1) * 512],
                    start=False, stop=True, skip_group_check=True))
                pws.append(pinc(mkb))
            ring_rel[halves[0][0]] = pws[-1]
            ring_rel[halves[1][0]] = pws[-1]
            return pws

        G1SLOT = [4, 0, 1, 2]

        def act_fin(j, with_sx):
            if with_sx:
                em("scalar", lambda E, j=j: E.activation(
                    out=STS[:, C_T4:C_T4 + 4],
                    in_=STS[:, C_XA + 4 * j:C_XA + 4 * j + 4], func=AF.Copy,
                    accum_out=STS[:, C_SX + j:C_SX + j + 1]))
            em("scalar", lambda E, j=j: E.activation(
                out=STS[:, C_T4 + 4:C_T4 + 8],
                in_=STS[:, C_SQ4 + 4 * j:C_SQ4 + 4 * j + 4], func=AF.Copy,
                accum_out=STS[:, C_SXX + j:C_SXX + j + 1]))
            em("scalar", lambda E, j=j: E.mul(
                out=STS[:, C_MEAN + j:C_MEAN + j + 1],
                in_=STS[:, C_SX + j:C_SX + j + 1], mul=1.0 / D))
            em("scalar", lambda E, j=j: E.activation(
                out=STS[:, C_MSQ + j:C_MSQ + j + 1],
                in_=STS[:, C_SX + j:C_SX + j + 1], func=AF.Square,
                scale=CST[:, 5:6]))
            em("scalar", lambda E, j=j: E.activation(
                out=STS[:, C_VAR + j:C_VAR + j + 1],
                in_=STS[:, C_MSQ + j:C_MSQ + j + 1], func=AF.Relu,
                scale=CST[:, 2:3], bias=STS[:, C_SXX + j:C_SXX + j + 1]))
            em("scalar", lambda E, j=j: E.activation(
                out=STS[:, C_LNV + j:C_LNV + j + 1],
                in_=STS[:, C_VAR + j:C_VAR + j + 1],
                func=AF.Ln, scale=CST[:, 4:5], bias=CST[:, 0:1]))
            em("scalar", lambda E, j=j: E.activation(
                out=STS[:, C_RSTD + j:C_RSTD + j + 1],
                in_=STS[:, C_LNV + j:C_LNV + j + 1],
                func=AF.Exp, scale=CST[:, 1:2]))
            return aflush()

        def wo_evct(g, fc, j, pw, gates=()):
            bank = (fc % 2) * 4 + j
            slot = j if g == 0 else G1SLOT[j]
            wait("vector", sp_, pw)
            for sem, val in gates:
                wait("vector", sem, val)
            em("vector", lambda E, j=j, fc=fc, bank=bank, slot=slot:
               E.tensor_scalar(
                   out=SC1[:, slot * D + fc * 512: slot * D + (fc + 1) * 512],
                   in0=pb[bank][:, 0:512], scalar1=1.0, scalar2=None,
                   op0=ALU.mult, op1=ALU.add,
                   accum_out=STS[:, C_XA + 4 * j + fc:C_XA + 4 * j + fc + 1]))
            fl = vflush()
            wo_ev_v[(g, fc, j)] = fl
            wait("scalar", sv_, fl)
            em("scalar", lambda E, j=j, fc=fc, slot=slot: E.activation(
                out=SQD[:, 0:512],
                in_=SC1[:, slot * D + fc * 512: slot * D + (fc + 1) * 512],
                func=AF.Square,
                accum_out=STS[:, C_SQ4 + 4 * j + fc:C_SQ4 + 4 * j + fc + 1]))
            return fl

        for j in range(4):
            wait("scalar", sv_, wo_ev_v[(0, 0, j)])
            em("scalar", lambda E, j=j: E.activation(
                out=SQD[:, 0:512], in_=SC1[:, j * D: j * D + 512],
                func=AF.Square,
                accum_out=STS[:, C_SQ4 + 4 * j:C_SQ4 + 4 * j + 1]))
        for fc in range(1, 4):
            halves = wo_strips(fc)
            if fc == 1:
                wait("tensor", sv_, C["v"])
                wait("tensor", sa_, C["a"])
            pws = wo_mms(0, fc, halves, [0, 1, 2, 3])
            for j in range(4):
                wo_evct(0, fc, j, pws[j])
        t1 = [4, 5, 6, 7]
        pws10 = wo_mms(1, 0, wo_strips(0), t1)

        aqA0 = [act_fin(j, True) for j in range(4)]
        xw = {}
        xw[0] = sdma(lambda E: E.dma_start(
            out=XST[:, 0:D], in_=x_h.ap()[0:128, :]))
        xw[1] = sdma(lambda E: E.dma_start(
            out=XST[:, D:2 * D], in_=x_h.ap()[128:256, :]))
        rfl5, aq2, ap2fl, tpd, tpg1 = {}, {}, {}, {}, {}
        xw5 = {}

        def lna_apply_resid(g, j, tmp_gate=None):
            t = 4 * g + j
            slot = j if g == 0 else G1SLOT[j]
            aq = aqA0[j] if g == 0 else aqA1[j]
            xwd = xw if g == 0 else xw5
            wait("vector", sa_, aq)
            if tmp_gate is not None:
                wait("vector", sp_, tmp_gate)
            em("vector", lambda E, j=j, slot=slot: E.tensor_scalar(
                out=TMP32[:, (j % 2) * D:(j % 2 + 1) * D],
                in0=SC1[:, slot * D:(slot + 1) * D],
                scalar1=STS[:, C_MEAN + j:C_MEAN + j + 1],
                scalar2=STS[:, C_RSTD + j:C_RSTD + j + 1],
                op0=ALU.subtract, op1=ALU.mult))
            wait("vector", sx_, xwd[j])
            em("vector", lambda E, j=j, slot=slot: E.scalar_tensor_tensor(
                out=SC1[:, slot * D:(slot + 1) * D],
                in0=TMP32[:, (j % 2) * D:(j % 2 + 1) * D], scalar=1.0,
                in1=XST[:, (j % 2) * D:(j % 2 + 1) * D],
                op0=ALU.mult, op1=ALU.add,
                accum_out=STS[:, C_SX + j:C_SX + j + 1]))
            rfl = vflush()
            rfl5[(g, j)] = rfl
            wait("sync", sv_, rfl)
            x2_dma[t] = sdma(lambda E, t=t, slot=slot: E.dma_start(
                out=x2_h.ap()[t * 128:(t + 1) * 128, :],
                in_=SC1[:, slot * D:(slot + 1) * D]))
            wait("scalar", sv_, rfl)
            for q in range(4):
                em("scalar", lambda E, j=j, q=q, slot=slot: E.activation(
                    out=SQD[:, 0:512],
                    in_=SC1[:, slot * D + q * 512: slot * D + (q + 1) * 512],
                    func=AF.Square,
                    accum_out=STS[:, C_SQ4 + 4 * j + q:C_SQ4 + 4 * j + q + 1]))
            aq2[(g, j)] = act_fin(j, False)

        def ln2_apply(g, j, src_ap, gates=()):
            wait("vector", sa_, aq2[(g, j)])
            for sem, val in gates:
                wait("vector", sem, val)
            em("vector", lambda E, j=j, src_ap=src_ap: E.tensor_scalar(
                out=TMP32[:, (j % 2) * D:(j % 2 + 1) * D],
                in0=src_ap,
                scalar1=STS[:, C_MEAN + j:C_MEAN + j + 1],
                scalar2=STS[:, C_RSTD + j:C_RSTD + j + 1],
                op0=ALU.subtract, op1=ALU.mult))
            ap2fl[(g, j)] = vflush()

        lna_apply_resid(0, 0)
        wait("sync", sv_, rfl5[(0, 0)])
        xw[2] = sdma(lambda E: E.dma_start(
            out=XST[:, 0:D], in_=x_h.ap()[256:384, :]))
        lna_apply_resid(0, 1)
        wait("sync", sv_, rfl5[(0, 1)])
        xw[3] = sdma(lambda E: E.dma_start(
            out=XST[:, D:2 * D], in_=x_h.ap()[384:512, :]))
        ln2_apply(0, 0, SC1[:, 0:D])
        wo_evct(1, 0, 0, pws10[0], gates=((sa_, aqA0[0]),))
        emit_transposes(0, (B, 0), [4, 5],
                        extra_gates=((sv_, ap2fl[(0, 0)]),
                                     (sv_, wo_ev_v[(0, 3, 1)])))
        tpd[0] = C["p"]
        lna_apply_resid(0, 2, tmp_gate=tpd[0])
        ln2_apply(0, 1, SC1[:, D:2 * D])
        wo_evct(1, 0, 1, pws10[1],
                gates=((sx_, x2_dma[0]), (sa_, aq2[(0, 0)])))
        emit_transposes(D, (B, 1), [4, 5], extra_gates=((sv_, ap2fl[(0, 1)]),))
        tpd[1] = C["p"]
        lna_apply_resid(0, 3, tmp_gate=tpd[1])
        wait("sync", sv_, rfl5[(0, 2)])
        wait("sync", sx_, x2_dma[2])
        x2r2 = sdma(lambda E: E.dma_start(
            out=XST[:, 0:D], in_=x2_h.ap()[256:384, :]))
        ln2_apply(0, 2, XST[:, 0:D], gates=((sx_, x2r2),))
        wo_evct(1, 0, 2, pws10[2],
                gates=((sx_, x2_dma[1]), (sa_, aq2[(0, 1)])))
        emit_transposes(0, (B, 2), [4, 5], extra_gates=((sv_, ap2fl[(0, 2)]),))
        tpd[2] = C["p"]
        ln2_apply(0, 3, SC1[:, 3 * D:4 * D])
        wo_evct(1, 0, 3, pws10[3],
                gates=((sx_, x2_dma[2]), (sa_, aq2[(0, 2)])))
        ln2_ready_g0 = emit_transposes(D, (B, 3), [4, 5],
                                       extra_gates=((sv_, ap2fl[(0, 3)]),))
        tpd[3] = C["p"]

        wait("tensor", sv_, ln2_ready_g0[0])
        wait("tensor", sa_, ln2_ready_g0[1])
        pws11 = wo_mms(1, 1, wo_strips(1), t1)
        for j in range(4):
            wo_evct(1, 1, j, pws11[j])


        pws12 = wo_mms(1, 2, wo_strips(2), t1)
        for j in range(4):
            wo_evct(1, 2, j, pws12[j])
        pws13 = wo_mms(1, 3, wo_strips(3), t1)

        gu_slot, gu_dma, gu_prev, gu_div_v = {}, {}, {}, {}
        gu_pg = {}

        def gu_load(jp):
            s = ring_acquire()
            gu_slot[jp] = s
            gdma(lambda E, jp=jp, s=s: E.dma_start(
                out=rsl(s, 4096)[:, 0:2048], in_=gu_h.ap()[jp],
                max_dma_last_dim=2048))
            gu_dma[jp] = gdma(lambda E, jp=jp, s=s: E.dma_start(
                out=rsl(s, 4096)[:, 2048:4096], in_=gu_h.ap()[16 + jp],
                max_dma_last_dim=2048))

        def gu_banks(jp, tg):
            if tg == 0:
                return 2, 3
            return 4 + (jp % 2) * 2, 5 + (jp % 2) * 2

        def gu_pe(jp, tg, extra=()):
            s = gu_slot[jp]
            bg, bu = gu_banks(jp, tg)
            wait("tensor", sw_, gu_dma[jp])
            for sem, val in extra:
                wait("tensor", sem, val)
            prev = gu_prev.get((tg, bg))
            if prev:
                wait("tensor", sv_, prev)
            for part, bank in ((0, bg), (1, bu)):
                for kc in range(KC):
                    mk = (lambda E, kc=kc, part=part, bank=bank, s=s, tg=tg:
                          E.matmul(
                              out=pb[bank][:, 0:512],
                              lhsT=rsl(s, 4096)[:, part * 2048 + kc * 128:
                                                part * 2048 + (kc + 1) * 128],
                              rhs=B[:, kc * 1024 + tg * 512:
                                    kc * 1024 + tg * 512 + 512],
                              start=(kc == 0), stop=(kc == KC - 1)))
                    if part == 1 and kc == KC - 1:
                        pg = pinc(mk)
                    else:
                        em("tensor", lambda E, f=mk: f(E))
            if tg == 1:
                ring_rel[s] = pg
            gu_pg[(jp, tg)] = pg

        def gu_div(jp, tg):
            bg, bu = gu_banks(jp, tg)
            wait("scalar", sp_, gu_pg[(jp, tg)])
            em("scalar", lambda E, tg=tg, bg=bg: E.activation(
                out=GT[:, tg * 512:(tg + 1) * 512], in_=pb[bg][:, 0:512],
                func=AF.Silu))
            ga = aflush()
            wait("vector", sa_, ga)
            em("vector", lambda E, jp=jp, tg=tg, bu=bu: E.tensor_tensor(
                out=A[:, jp * 1024 + tg * 512: jp * 1024 + tg * 512 + 512],
                in0=GT[:, tg * 512:(tg + 1) * 512],
                in1=pb[bu][:, 0:512], op=ALU.mult))
            fl = vflush()
            gu_div_v[(jp, tg)] = fl
            gu_prev[(tg, bg)] = fl

        def gu_mms(jp, tg, extra=()):
            gu_pe(jp, tg, extra=extra)
            gu_div(jp, tg)

        for j in range(4):
            wo_evct(1, 3, j, pws13[j])
        aqA1 = [act_fin(j, True) for j in range(4)]
        wait("sync", sv_, ap2fl[(0, 2)])
        xw5[0] = sdma(lambda E: E.dma_start(
            out=XST[:, 0:D], in_=x_h.ap()[4 * 128:5 * 128, :]))
        wait("sync", sv_, rfl5[(0, 3)])
        xw5[1] = sdma(lambda E: E.dma_start(
            out=XST[:, D:2 * D], in_=x_h.ap()[5 * 128:6 * 128, :]))
        for jp in range(4):
            gu_load(jp)
        lna_apply_resid(1, 0, tmp_gate=tpd[2])
        wait("sync", sv_, rfl5[(1, 0)])
        xw5[2] = sdma(lambda E: E.dma_start(
            out=XST[:, 0:D], in_=x_h.ap()[6 * 128:7 * 128, :]))
        lna_apply_resid(1, 1, tmp_gate=tpd[3])
        wait("sync", sv_, rfl5[(1, 1)])
        xw5[3] = sdma(lambda E: E.dma_start(
            out=XST[:, D:2 * D], in_=x_h.ap()[7 * 128:8 * 128, :]))
        ln2_apply(1, 0, SC1[:, 4 * D:5 * D])
        emit_transposes(0, (B, 4), [0, 1],
                        extra_gates=((sv_, ap2fl[(1, 0)]),
                                     (sv_, wo_ev_v[(1, 2, 1)])))
        tpg1[0] = C["p"]
        gu_pe(0, 0, extra=((sv_, wo_ev_v[(1, 2, 3)]),
                           (sv_, ln2_ready_g0[0]), (sa_, ln2_ready_g0[1])))
        lna_apply_resid(1, 2, tmp_gate=tpg1[0])
        ln2_apply(1, 1, SC1[:, 0:D])
        emit_transposes(D, (B, 5), [0, 1], extra_gates=((sv_, ap2fl[(1, 1)]),))
        tpg1[1] = C["p"]
        gu_div(0, 0)
        gu_pe(1, 0)
        lna_apply_resid(1, 3, tmp_gate=tpg1[1])
        ln2_apply(1, 2, SC1[:, D:2 * D])
        emit_transposes(0, (B, 6), [0, 1], extra_gates=((sv_, ap2fl[(1, 2)]),))
        tpg1[2] = C["p"]
        gu_div(1, 0)
        gu_pe(2, 0)
        ln2_apply(1, 3, SC1[:, 2 * D:3 * D])
        ln2_ready_g1 = emit_transposes(D, (B, 7), [0, 1],
                                       extra_gates=((sv_, ap2fl[(1, 3)]),))
        tpg1[3] = C["p"]
        gu_div(2, 0)
        gu_pe(3, 0)
        gu_div(3, 0)
        for j in range(4):
            extra = []
            if j == 0:
                extra = [(sv_, ln2_ready_g1[0]), (sa_, ln2_ready_g1[1]),
                         (sv_, wo_ev_v[(1, 3, 1)])]
            elif j == 1:
                extra = [(sv_, wo_ev_v[(1, 3, 3)])]
            gu_mms(j, 1, extra=extra)
        for jp in range(4, 16):
            gu_load(jp)
            gu_mms(jp, 0)
            gu_mms(jp, 1)
        inter_ready_v = gu_div_v[(15, 1)]

        dn_ev_v = {}

        def dn_strips(fc):
            halves = []
            for half in range(2):
                s = ring_acquire()
                halves.append((s, gdma(
                    lambda E, fc=fc, half=half, s=s: E.dma_start(
                        out=rsl(s, 4096),
                        in_=dn_h.ap()[fc][:, half * 4096:(half + 1) * 4096],
                        max_dma_last_dim=2048))))
            return halves

        def dn_mms(g, fc, halves, tiles):
            fcp = fc % 2
            pws = []
            for j, t in enumerate(tiles):
                bank = fcp * 4 + j
                if g == 0 and fc == 0 and j == 0:
                    wait("tensor", sv_, inter_ready_v)
                prev_ev = dn_ev_v.get((g, fc - 2, j))
                if g == 1 and fc < 2:
                    prev_ev = dn_ev_v.get((0, fc + 2, j))
                if prev_ev:
                    wait("tensor", sv_, prev_ev)
                for ic in range(KC):
                    s, wdma = halves[ic // 8]
                    if ic % 8 == 0 and j == 0:
                        wait("tensor", sw_, wdma)
                    mk = (lambda E, ic=ic, t=t, bank=bank, s=s: E.matmul(
                        out=pb[bank][:, 0:512],
                        lhsT=A[:, ic * 1024 + t * 128: ic * 1024 + (t + 1) * 128],
                        rhs=rsl(s, 4096)[:, (ic % 8) * 512:(ic % 8 + 1) * 512],
                        start=(ic == 0), stop=(ic == KC - 1)))
                    if ic == KC - 1:
                        pws.append(pinc(mk))
                    else:
                        em("tensor", lambda E, f=mk: f(E))
            ring_rel[halves[0][0]] = pws[-1]
            ring_rel[halves[1][0]] = pws[-1]
            return pws

        def dn_evct(g, fc, j, pw, gates=()):
            bank = (fc % 2) * 4 + j
            slot = j if g == 0 else G1SLOT[j]
            wait("vector", sp_, pw)
            for sem, val in gates:
                wait("vector", sem, val)
            em("vector", lambda E, j=j, fc=fc, bank=bank, slot=slot:
               E.tensor_scalar(
                   out=SC1[:, slot * D + fc * 512: slot * D + (fc + 1) * 512],
                   in0=pb[bank][:, 0:512], scalar1=1.0, scalar2=None,
                   op0=ALU.mult, op1=ALU.add,
                   accum_out=STS[:, C_XA + 4 * j + fc:C_XA + 4 * j + fc + 1]))
            fl = vflush()
            dn_ev_v[(g, fc, j)] = fl
            wait("scalar", sv_, fl)
            em("scalar", lambda E, j=j, fc=fc, slot=slot: E.activation(
                out=SQD[:, 0:512],
                in_=SC1[:, slot * D + fc * 512: slot * D + (fc + 1) * 512],
                func=AF.Square,
                accum_out=STS[:, C_SQ4 + 4 * j + fc:C_SQ4 + 4 * j + fc + 1]))
            return fl

        for fc in range(4):
            halves = dn_strips(fc)
            pws = dn_mms(0, fc, halves, [0, 1, 2, 3])
            for j in range(4):
                dn_evct(0, fc, j, pws[j])
        pws10 = dn_mms(1, 0, dn_strips(0), t1)
        pws11 = dn_mms(1, 1, dn_strips(1), t1)

        aqA0_7 = [act_fin(j, True) for j in range(4)]
        g0_odma = {}
        xw7 = {}
        for j in (0, 1):
            xw7[j] = sdma(lambda E, t=j: E.dma_start(
                out=XST[:, (t % 2) * D:(t % 2 + 1) * D],
                in_=x2_h.ap()[t * 128:(t + 1) * 128, :]))
        for j in range(4):
            wait("vector", sa_, aqA0_7[j])
            em("vector", lambda E, j=j: E.tensor_scalar(
                out=TMP32[:, (j % 2) * D:(j % 2 + 1) * D],
                in0=SC1[:, j * D:(j + 1) * D],
                scalar1=STS[:, C_MEAN + j:C_MEAN + j + 1],
                scalar2=STS[:, C_RSTD + j:C_RSTD + j + 1],
                op0=ALU.subtract, op1=ALU.mult))
            wait("vector", sx_, xw7[j])
            em("vector", lambda E, j=j: E.tensor_tensor(
                out=SC1[:, j * D:(j + 1) * D],
                in0=TMP32[:, (j % 2) * D:(j % 2 + 1) * D],
                in1=XST[:, (j % 2) * D:(j % 2 + 1) * D], op=ALU.add))
            rfl = vflush()
            wait("sync", sv_, rfl)
            g0_odma[j] = odma(lambda E, j=j: E.dma_start(
                out=out_h.ap()[j * 128:(j + 1) * 128, :],
                in_=SC1[:, j * D:(j + 1) * D]))
            if j < 2:
                xw7[j + 2] = sdma(lambda E, t=j + 2: E.dma_start(
                    out=XST[:, (t % 2) * D:(t % 2 + 1) * D],
                    in_=x2_h.ap()[t * 128:(t + 1) * 128, :]))
            if j == 0:
                dn_evct(1, 0, 0, pws10[0], gates=((sa_, aqA0_7[0]),))
            else:
                dn_evct(1, 0, j, pws10[j],
                        gates=((so_, g0_odma[j - 1]), (sa_, aqA0_7[j])))
        for j in range(4):
            dn_evct(1, 1, j, pws11[j])
        pws12 = dn_mms(1, 2, dn_strips(2), t1)
        for j in range(4):
            dn_evct(1, 2, j, pws12[j])
        pws13 = dn_mms(1, 3, dn_strips(3), t1)

        xw_g1 = {}
        for j in (0, 1):
            xw_g1[j] = sdma(lambda E, j=j: E.dma_start(
                out=XST[:, (j % 2) * D:(j % 2 + 1) * D],
                in_=x2_h.ap()[(4 + j) * 128:(5 + j) * 128, :]))
        for j in range(4):
            dn_evct(1, 3, j, pws13[j])
        aqA1_7 = [act_fin(j, True) for j in range(4)]
        for j in range(4):
            t = 4 + j
            slot = G1SLOT[j]
            wait("vector", sa_, aqA1_7[j])
            em("vector", lambda E, j=j, slot=slot: E.tensor_scalar(
                out=TMP32[:, (j % 2) * D:(j % 2 + 1) * D],
                in0=SC1[:, slot * D:(slot + 1) * D],
                scalar1=STS[:, C_MEAN + j:C_MEAN + j + 1],
                scalar2=STS[:, C_RSTD + j:C_RSTD + j + 1],
                op0=ALU.subtract, op1=ALU.mult))
            wait("vector", sx_, xw_g1[j])
            em("vector", lambda E, j=j, slot=slot: E.tensor_tensor(
                out=SC1[:, slot * D:(slot + 1) * D],
                in0=TMP32[:, (j % 2) * D:(j % 2 + 1) * D],
                in1=XST[:, (j % 2) * D:(j % 2 + 1) * D], op=ALU.add))
            rfl = vflush()
            wait("sync", sv_, rfl)
            if j < 2:
                xw_g1[j + 2] = sdma(lambda E, j=j: E.dma_start(
                    out=XST[:, (j % 2) * D:(j % 2 + 1) * D],
                    in_=x2_h.ap()[(6 + j) * 128:(7 + j) * 128, :]))
            odma(lambda E, t=t, slot=slot: E.dma_start(
                out=out_h.ap()[t * 128:(t + 1) * 128, :],
                in_=SC1[:, slot * D:(slot + 1) * D]))

        final_o = C["o"]
        em("sync", lambda E, v=final_o: E.wait_ge(so_, v))
        em("gpsimd", lambda E, v=final_o: E.wait_ge(so_, v))

        with nc.Block() as block:
            @block.sync
            def _(sync):
                for fn in P["sync"]:
                    fn(sync)

            @block.gpsimd
            def _(gpsimd):
                for fn in P["gpsimd"]:
                    fn(gpsimd)

            @block.vector
            def _(vector):
                for fn in P["vector"]:
                    fn(vector)

            @block.scalar
            def _(scalar):
                for fn in P["scalar"]:
                    fn(scalar)

            @block.tensor
            def _(tensor):
                for fn in P["tensor"]:
                    fn(tensor)

    return nc


def _strips(w, width):
    n = w.shape[1] // width
    out = np.empty((n, 128, 16 * width), dtype=np.float32)
    for i in range(n):
        s = w[:, i * width:(i + 1) * width].reshape(16, 128, width)
        out[i] = s.transpose(1, 0, 2).reshape(128, 16 * width)
    return np.ascontiguousarray(out)


def kernel(L0_attn_hidden_states, L0_attn_Wq, L0_attn_Wk, L0_attn_Wv,
           L0_attn_Wo, L0_attn_bo, L0_ffn_gate_up_proj, L0_ffn_down_proj):
    X = np.asarray(L0_attn_hidden_states, dtype=np.float32)
    Wq = np.asarray(L0_attn_Wq, dtype=np.float32)
    Wk = np.asarray(L0_attn_Wk, dtype=np.float32)
    Wv = np.asarray(L0_attn_Wv, dtype=np.float32)
    Wo = np.asarray(L0_attn_Wo, dtype=np.float32)
    bo = np.asarray(L0_attn_bo, dtype=np.float32).reshape(1, D)
    GU = np.asarray(L0_ffn_gate_up_proj, dtype=np.float32)
    DN = np.asarray(L0_ffn_down_proj, dtype=np.float32)

    wq_s = _strips(Wq, 128)
    wk_s = _strips(Wk, 512)
    wv_s = _strips(Wv, 512)
    wo_s = _strips(Wo, 512)
    ident = np.eye(128, dtype=np.float32)

    nc = build_program()
    in_maps = []
    for c in range(8):
        G, e = c // 4, c % 4
        xs = np.ascontiguousarray(
            np.concatenate([X[2 * G, e::4, :], X[2 * G + 1, e::4, :]], axis=0))
        gu_s = _strips(np.ascontiguousarray(GU[e]), 128)
        dn_s = _strips(np.ascontiguousarray(DN[e]), 512)
        in_maps.append({
            "x": xs, "wq": wq_s, "wk": wk_s, "wv": wv_s, "wo": wo_s,
            "bo": bo, "gu": gu_s, "dn": dn_s, "ident": ident,
        })
    res = run_bass_kernel_spmd(nc, in_maps, core_ids=list(range(8)))

    out = np.empty((4, 2048, D), dtype=np.float32)
    for c in range(8):
        G, e = c // 4, c % 4
        r = res.results[c]["out"]
        out[2 * G, e::4, :] = r[:512]
        out[2 * G + 1, e::4, :] = r[512:]
    return out


# version 18
# speedup vs baseline: 1.0187x; 1.0150x over previous
import numpy as np
import concourse.bass as bass
import concourse.mybir as mybir
from concourse.bass_utils import run_bass_kernel_spmd
from contextlib import ExitStack

F32 = mybir.dt.float32
BF16 = mybir.dt.bfloat16
AF = mybir.ActivationFunctionType
ALU = mybir.AluOpType
AX = mybir.AxisListType

NT = 8
D = 2048
KC = 16
H = 32
NPAIR = 32
EPS = 1e-5


def build_program():
    nc = bass.Bass("TRN2", target_bir_lowering=False)

    x_h = nc.dram_tensor("x", [1024, D], F32, kind="ExternalInput")
    wq_h = nc.dram_tensor("wq", [16, 128, 2048], F32, kind="ExternalInput")
    wk_h = nc.dram_tensor("wk", [4, 128, 8192], F32, kind="ExternalInput")
    wv_h = nc.dram_tensor("wv", [4, 128, 8192], F32, kind="ExternalInput")
    wo_h = nc.dram_tensor("wo", [4, 128, 8192], F32, kind="ExternalInput")
    bo_h = nc.dram_tensor("bo", [1, D], F32, kind="ExternalInput")
    gu_h = nc.dram_tensor("gu", [32, 128, 2048], F32, kind="ExternalInput")
    dn_h = nc.dram_tensor("dn", [4, 128, 8192], F32, kind="ExternalInput")
    id_h = nc.dram_tensor("ident", [128, 128], F32, kind="ExternalInput")
    out_h = nc.dram_tensor("out", [1024, D], F32, kind="ExternalOutput")

    cc_in = nc.dram_tensor("cc_in", [128, NPAIR * 65], F32)
    cc_out = nc.dram_tensor("cc_out", [128, NPAIR * 65], F32)
    x2_h = nc.dram_tensor("x2scr", [1024, D], F32)

    with ExitStack() as st:
        def sb(name, shape, dt):
            return st.enter_context(nc.sbuf_tensor(name, shape, dt))

        A = sb("A", [128, KC * 1024], BF16)
        B = sb("B", [128, KC * 1024], BF16)
        SC1 = sb("SC1", [128, 5 * D], F32)
        TMP32 = sb("TMP32", [128, 2 * D], F32)
        XST = sb("XST", [128, 2 * D], F32)
        KTM = sb("KTM", [128, 2 * 512], BF16)
        V65 = sb("V65", [128, 2 * 520], BF16)
        KV65 = sb("KV65", [128, 2 * H * 65], BF16)
        RING = sb("RING", [128, 5 * 4096], BF16)
        BOP = sb("BOP", [128, D], BF16)
        E1 = sb("E1", [128, 128], BF16)
        IDS = sb("IDS", [128, 128], F32)
        GT = sb("GT", [128, 1024], BF16)
        STS = sb("STS", [128, 240], F32)
        CST = sb("CST", [128, 6], F32)
        FL = sb("FL", [128, 4], F32)
        SQD = sb("SQD", [128, 512], F32)

        PBIG = st.enter_context(nc.psum_tensor("PBIG", [128, 4096], F32))
        pb = [PBIG[:, i * 512:(i + 1) * 512] for i in range(8)]

        sx_ = st.enter_context(nc.semaphore("sx"))
        sw_ = st.enter_context(nc.semaphore("sw"))
        sv_ = st.enter_context(nc.semaphore("sv"))
        sa_ = st.enter_context(nc.semaphore("sa"))
        sp_ = st.enter_context(nc.semaphore("sp"))
        sc_ = st.enter_context(nc.semaphore("scc"))
        so_ = st.enter_context(nc.semaphore("so"))
        sems = [sx_, sw_, sv_, sa_, sp_, sc_, so_]
        nums = sorted(s.num for s in sems)
        assert nums[-1] - nums[0] == len(sems) - 1, nums

        C_SX, C_SXX, C_MEAN, C_VAR, C_MSQ, C_LNV, C_RSTD = 0, 8, 16, 24, 32, 40, 48
        C_SQ4, C_T4, C_NM, C_LNN, C_RC = 56, 88, 96, 136, 176
        C_XA = 212

        P = {"sync": [], "gpsimd": [], "vector": [], "scalar": [], "tensor": []}
        C = {"x": 0, "w": 0, "v": 0, "a": 0, "p": 0, "cc": 0, "o": 0}

        def em(eng, fn):
            P[eng].append(fn)

        def wait(eng, sem, val):
            if val > 0:
                em(eng, lambda E, s=sem, v=val: E.wait_ge(s, v))

        def vflush():
            em("vector", lambda E: E.memset(FL[:, 0:1], 0.0))
            C["v"] += 1
            em("vector", lambda E: E.memset(FL[:, 1:2], 0.0).then_inc(sv_, 1))
            return C["v"]

        def aflush():
            em("scalar", lambda E: E.mul(out=FL[:, 2:3], in_=CST[:, 0:1], mul=1.0))
            C["a"] += 1
            em("scalar", lambda E: E.mul(out=FL[:, 3:4], in_=CST[:, 0:1], mul=1.0)
               .then_inc(sa_, 1))
            return C["a"]

        def pinc(mk):
            C["p"] += 1
            em("tensor", lambda E, f=mk: f(E).then_inc(sp_, 1))
            return C["p"]

        def sdma(mk):
            C["x"] += 16
            em("sync", lambda E, f=mk: f(E).then_inc(sx_, 16))
            return C["x"]

        def gdma(mk):
            C["w"] += 16
            em("gpsimd", lambda E, f=mk: f(E).then_inc(sw_, 16))
            return C["w"]

        def odma(mk):
            C["o"] += 16
            em("sync", lambda E, f=mk: f(E).then_inc(so_, 16))
            return C["o"]

        ring_rel = [0] * 5
        ring_ptr = [0]

        def ring_acquire():
            s = ring_ptr[0]
            ring_ptr[0] = (s + 1) % 5
            if ring_rel[s] > 0:
                wait("gpsimd", sp_, ring_rel[s])
            return s

        def rsl(s, nels):
            return RING[:, s * 4096: s * 4096 + nels]

        with nc.Block() as blk0:
            @blk0.gpsimd
            def _(gpsimd):
                gpsimd.sem_clear(range(nums[0], nums[-1] + 1))

        with nc.Block() as blk1:
            @blk1.vector
            def _(vector):
                vector.memset(CST[:, 0:1], EPS)
                vector.memset(CST[:, 1:2], -0.5)
                vector.memset(CST[:, 2:3], -1.0)
                vector.memset(CST[:, 3:4], 0.0)
                vector.memset(CST[:, 4:5], 1.0 / D)
                vector.memset(CST[:, 5:6], float(D) ** -0.5)
                vector.memset(E1[:, :], 0.0)
                vector.memset(E1[0:1, :], 1.0)
                vector.memset(BOP[:, :], 0.0)
                vector.memset(KV65[:, :], 0.0)
                for t in range(2):
                    vector.memset(
                        V65[:, t * 520:(t + 1) * 520]
                        .rearrange("p (h e) -> p h e", e=65)[:, :, 64:65], 1.0)

        ident_dma = sdma(lambda E: E.dma_start(out=IDS[:, :], in_=id_h.ap()))
        bo_dma = gdma(lambda E: E.dma_start(out=BOP[0:1, :], in_=bo_h.ap()))

        tp_last = {}

        def emit_ln_stats(n, tile_ap_fn, v_gate=0, a_gate=0, x_waits=None,
                          xa_mode=False, sx_ready=False):
            if a_gate:
                wait("scalar", sv_, a_gate)
            if v_gate:
                wait("vector", sv_, v_gate)
            for j in range(n):
                if x_waits is not None:
                    wait("vector", sx_, x_waits[j])
                    wait("scalar", sx_, x_waits[j])
                if sx_ready:
                    pass
                elif xa_mode:
                    em("vector", lambda E, j=j: E.tensor_scalar(
                        out=STS[:, C_T4:C_T4 + 4],
                        in0=STS[:, C_XA + 4 * j:C_XA + 4 * j + 4],
                        scalar1=1.0, scalar2=None, op0=ALU.mult, op1=ALU.add,
                        accum_out=STS[:, C_SX + j:C_SX + j + 1]))
                else:
                    em("vector", lambda E, ap=tile_ap_fn(j), j=j: E.reduce_sum(
                        out=STS[:, C_SX + j:C_SX + j + 1], in_=ap, axis=AX.X))
                for q in range(4):
                    apq = tile_ap_fn(j)[:, q * 512:(q + 1) * 512]
                    em("scalar", lambda E, apq=apq, j=j, q=q: E.activation(
                        out=GT[:, 0:512], in_=apq, func=AF.Square,
                        accum_out=STS[:, C_SQ4 + 4 * j + q:C_SQ4 + 4 * j + q + 1]))
            av = aflush()
            vv = vflush()
            wait("vector", sv_, vv)
            wait("vector", sa_, av)
            for j in range(n):
                em("vector", lambda E, j=j: E.tensor_scalar(
                    out=STS[:, C_T4:C_T4 + 4],
                    in0=STS[:, C_SQ4 + 4 * j:C_SQ4 + 4 * j + 4],
                    scalar1=1.0, scalar2=None, op0=ALU.mult, op1=ALU.add,
                    accum_out=STS[:, C_SXX + j:C_SXX + j + 1]))
            vv = vflush()
            wait("vector", sv_, vv)
            em("vector", lambda E, n=n: E.tensor_scalar(
                out=STS[:, C_MEAN:C_MEAN + n], in0=STS[:, C_SX:C_SX + n],
                scalar1=1.0 / D, scalar2=None, op0=ALU.mult))
            em("vector", lambda E, n=n: E.tensor_scalar(
                out=STS[:, C_VAR:C_VAR + n], in0=STS[:, C_SXX:C_SXX + n],
                scalar1=1.0 / D, scalar2=None, op0=ALU.mult))
            vv = vflush()
            wait("vector", sv_, vv)
            em("vector", lambda E, n=n: E.tensor_tensor(
                out=STS[:, C_MSQ:C_MSQ + n], in0=STS[:, C_MEAN:C_MEAN + n],
                in1=STS[:, C_MEAN:C_MEAN + n], op=ALU.mult))
            vv = vflush()
            wait("vector", sv_, vv)
            em("vector", lambda E, n=n: E.tensor_tensor(
                out=STS[:, C_VAR:C_VAR + n], in0=STS[:, C_VAR:C_VAR + n],
                in1=STS[:, C_MSQ:C_MSQ + n], op=ALU.subtract))
            vv = vflush()
            wait("scalar", sv_, vv)
            em("scalar", lambda E, n=n: E.activation(
                out=STS[:, C_LNV:C_LNV + n], in_=STS[:, C_VAR:C_VAR + n],
                func=AF.Ln, bias=CST[:, 0:1]))
            em("scalar", lambda E, n=n: E.activation(
                out=STS[:, C_RSTD:C_RSTD + n], in_=STS[:, C_LNV:C_LNV + n],
                func=AF.Exp, scale=CST[:, 1:2]))
            return aflush()

        def emit_transposes(src_col, dst, bank_pair, extra_gates=()):
            buf, t = dst
            key = tuple(bank_pair)
            prev = tp_last.get(key)
            if prev:
                wait("tensor", sv_, prev[0])
                wait("tensor", sa_, prev[1])
            for sem, val in extra_gates:
                wait("tensor", sem, val)
            ev = {}
            for grp in range(4):
                bk = bank_pair[grp % 2]
                if grp >= 2:
                    sem, val = ev[grp - 2]
                    wait("tensor", sem, val)
                for q in range(4):
                    kc = grp * 4 + q
                    mk = (lambda E, kc=kc, bk=bk, q=q, src_col=src_col:
                          E.transpose(
                              out=pb[bk][:, q * 128:(q + 1) * 128],
                              in_=TMP32[:, src_col + kc * 128: src_col + (kc + 1) * 128],
                              identity=IDS[:, :]))
                    if q < 3:
                        em("tensor", lambda E, f=mk: f(E))
                    else:
                        pv = pinc(mk)
                dview = buf[:, :].rearrange("p (kc tt) -> p kc tt", tt=1024)[
                    :, grp * 4:(grp + 1) * 4, t * 128:(t + 1) * 128]
                if grp % 2 == 0:
                    wait("vector", sp_, pv)
                    em("vector", lambda E, bk=bk, dview=dview: E.tensor_copy(
                        out=dview, in_=pb[bk][:, 0:512]))
                    C["v"] += 1
                    em("vector", lambda E: E.memset(FL[:, 0:1], 0.0)
                       .then_inc(sv_, 1))
                    ev[grp] = (sv_, C["v"])
                else:
                    wait("scalar", sp_, pv)
                    em("scalar", lambda E, bk=bk, dview=dview: E.activation(
                        out=dview, in_=pb[bk][:, 0:512], func=AF.Copy))
                    C["a"] += 1
                    em("scalar", lambda E: E.mul(out=FL[:, 2:3],
                                                 in_=CST[:, 0:1], mul=1.0)
                       .then_inc(sa_, 1))
                    ev[grp] = (sa_, C["a"])
            vg = vflush()
            ag = aflush()
            tp_last[key] = (vg, ag)
            return (vg, ag)

        k_ev_a, v_ev_v = {}, {}
        kv_state = {"flush": 0}

        def emit_kv(t, fc):
            ti = fc * NT + t
            bt = t // 4
            wait("tensor", sa_, k_ev_a[ti])
            wait("tensor", sv_, v_ev_v[ti])
            if t == 0 and kv_state["flush"]:
                wait("tensor", sv_, kv_state["flush"])
            done = None
            for j in range(4):
                for par in range(2):
                    hh = 2 * j + par
                    bank = 4 + bt * 2 + j // 2
                    col = (j % 2) * 130 + par * 65
                    first = (t % 4 == 0) and (j % 2 == 0) and (par == 0)
                    mk = (lambda E, t=t, j=j, hh=hh, bank=bank, col=col,
                          first=first: E.matmul(
                              out=pb[bank][:, col:col + 65],
                              lhsT=KTM[:, (t % 2) * 512 + j * 128: (t % 2) * 512 + (j + 1) * 128],
                              rhs=V65[:, (t % 2) * 520 + hh * 65: (t % 2) * 520 + (hh + 1) * 65],
                              start=first, stop=(t % 4 == 3),
                              skip_group_check=not first))
                    if j == 3 and par == 1:
                        done = pinc(mk)
                    else:
                        em("tensor", lambda E, f=mk: f(E))
            return done

        def emit_kproj(t, fc, bank, strip_waits):
            for kc in range(KC):
                s, wv_ = strip_waits[kc // 8]
                if kc % 8 == 0:
                    wait("tensor", sw_, wv_)
                mk = (lambda E, kc=kc, t=t, bank=bank, s=s: E.matmul(
                    out=pb[bank][:, 0:512],
                    lhsT=A[:, kc * 1024 + t * 128: kc * 1024 + (t + 1) * 128],
                    rhs=rsl(s, 4096)[:, (kc % 8) * 512:(kc % 8 + 1) * 512],
                    start=(kc == 0), stop=(kc == KC - 1)))
                if kc < KC - 1:
                    em("tensor", lambda E, f=mk: f(E))
                else:
                    pk = pinc(mk)
            return pk

        fc0_k, fc0_v = [], []
        for half in range(2):
            s = ring_acquire()
            fc0_k.append((s, gdma(lambda E, half=half, s=s: E.dma_start(
                out=rsl(s, 4096),
                in_=wk_h.ap()[0][:, half * 4096:(half + 1) * 4096],
                max_dma_last_dim=2048))))
        for half in range(2):
            s = ring_acquire()
            fc0_v.append((s, gdma(lambda E, half=half, s=s: E.dma_start(
                out=rsl(s, 4096),
                in_=wv_h.ap()[0][:, half * 4096:(half + 1) * 4096],
                max_dma_last_dim=2048))))

        apply_fl = {}
        pend = {}
        kv_done_p = None
        tpB_pinc = {}

        def fm_dst(buf, t, kc0):
            return buf[:, :].rearrange("p (kc tt) -> p kc tt", tt=1024)[
                :, kc0:kc0 + 8, t * 128:(t + 1) * 128]

        def flush_pending(t):
            p = pend.pop(t)
            wait("vector", sp_, p["pv2"])
            em("vector", lambda E, t=t: E.tensor_copy(
                out=V65[:, (t % 2) * 520:(t % 2) * 520 + 520]
                .rearrange("p (h e) -> p h e", e=65)[:, :, 0:64],
                in_=pb[1][:, 0:512].rearrange("p (h f) -> p h f", f=64)))
            v_ev_v[t] = vflush()
            wait("scalar", sp_, p["tpB"])
            em("scalar", lambda E, t=t: E.activation(
                out=fm_dst(A, t, 8), in_=PBIG[:, 1024:2048], func=AF.Copy))
            wait("scalar", sp_, p["pk"])
            em("scalar", lambda E, t=t: E.activation(
                out=KTM[:, (t % 2) * 512:(t % 2) * 512 + 512],
                in_=pb[0][:, 0:512], func=AF.Relu))
            k_ev_a[t] = aflush()

        for g in range(2):
            tiles = list(range(4 * g, 4 * g + 4))
            if g == 1:
                flush_pending(3)
                wait("sync", sv_, apply_fl[3])
                wait("sync", sa_, C["a"])
            xw = []
            for j, t in enumerate(tiles):
                xw.append(sdma(lambda E, t=t, j=j: E.dma_start(
                    out=SC1[:, j * D:(j + 1) * D],
                    in_=x_h.ap()[t * 128:(t + 1) * 128, :])))
            ra = emit_ln_stats(4, lambda j: SC1[:, j * D:(j + 1) * D], x_waits=xw)
            wait("vector", sa_, ra)
            for j, t in enumerate(tiles):
                slot = t % 2
                if t >= 2:
                    wait("vector", sp_, tpB_pinc[t - 2])
                em("vector", lambda E, j=j, slot=slot: E.tensor_scalar(
                    out=TMP32[:, slot * D:(slot + 1) * D],
                    in0=SC1[:, j * D:(j + 1) * D],
                    scalar1=STS[:, C_MEAN + j:C_MEAN + j + 1],
                    scalar2=STS[:, C_RSTD + j:C_RSTD + j + 1],
                    op0=ALU.subtract, op1=ALU.mult))
                apply_fl[t] = vflush()
                if t >= 1 and (t - 1) in pend:
                    flush_pending(t - 1)
                wait("tensor", sv_, apply_fl[t])
                if t == 0:
                    wait("tensor", sx_, ident_dma)
                else:
                    wait("tensor", sa_, k_ev_a[t - 1])
                    wait("tensor", sv_, v_ev_v[t - 1])
                for q in range(8):
                    kc = q
                    bk = q // 4
                    mk = (lambda E, kc=kc, bk=bk, q=q, slot=slot: E.transpose(
                        out=pb[bk][:, (q % 4) * 128:(q % 4 + 1) * 128],
                        in_=TMP32[:, slot * D + kc * 128: slot * D + (kc + 1) * 128],
                        identity=IDS[:, :]))
                    if q < 7:
                        em("tensor", lambda E, f=mk: f(E))
                    else:
                        tpA_p = pinc(mk)
                wait("vector", sp_, tpA_p)
                em("vector", lambda E, t=t: E.tensor_copy(
                    out=fm_dst(A, t, 0), in_=PBIG[:, 0:1024]))
                eaf = vflush()
                for q in range(8):
                    kc = 8 + q
                    bk = 2 + q // 4
                    mk = (lambda E, kc=kc, bk=bk, q=q, slot=slot: E.transpose(
                        out=pb[bk][:, (q % 4) * 128:(q % 4 + 1) * 128],
                        in_=TMP32[:, slot * D + kc * 128: slot * D + (kc + 1) * 128],
                        identity=IDS[:, :]))
                    if q < 7:
                        em("tensor", lambda E, f=mk: f(E))
                    else:
                        tpB_pinc[t] = pinc(mk)
                if t >= 1:
                    kv_done_p = emit_kv(t - 1, 0)
                wait("tensor", sv_, eaf)
                pk = emit_kproj(t, 0, 0, fc0_k)
                pv2 = emit_kproj(t, 0, 1, fc0_v)
                pend[t] = {"pv2": pv2, "tpB": tpB_pinc[t], "pk": pk}
                if t == NT - 1:
                    ring_rel[fc0_k[0][0]] = pk
                    ring_rel[fc0_k[1][0]] = pk
                    ring_rel[fc0_v[0][0]] = pv2
                    ring_rel[fc0_v[1][0]] = pv2
        flush_pending(NT - 1)
        kv_done_p = emit_kv(NT - 1, 0)
        ln1_ready = (C["v"], C["a"])

        em("vector", lambda E: E.memset(SC1[:, 0:NPAIR * 65], 0.0))
        kvz_v = vflush()

        def emit_kv_evict(fc):
            wait("vector", sp_, kv_done_p)
            if fc == 0:
                wait("vector", sv_, kvz_v)
            for bt in range(2):
                for q in range(2):
                    bank = 4 + bt * 2 + q
                    for par in range(2):
                        gp0 = bt * 16 + fc * 4 + 2 * q
                        em("vector", lambda E, bank=bank, par=par, gp0=gp0:
                           E.tensor_copy(
                               out=SC1[par * 64:(par + 1) * 64,
                                       gp0 * 65:(gp0 + 2) * 65]
                               .rearrange("p (n e) -> p n e", e=65),
                               in_=pb[bank][par * 64:(par + 1) * 64, 0:260]
                               .rearrange("p (n e) -> p n e", e=130)
                               [:, :, par * 65: par * 65 + 65]))
            kv_state["flush"] = vflush()

        emit_kv_evict(0)

        for fc in range(1, 4):
            wkw, wvw = [], []
            for half in range(2):
                s = ring_acquire()
                wkw.append((s, gdma(lambda E, fc=fc, half=half, s=s: E.dma_start(
                    out=rsl(s, 4096),
                    in_=wk_h.ap()[fc][:, half * 4096:(half + 1) * 4096],
                    max_dma_last_dim=2048))))
            for half in range(2):
                s = ring_acquire()
                wvw.append((s, gdma(lambda E, fc=fc, half=half, s=s: E.dma_start(
                    out=rsl(s, 4096),
                    in_=wv_h.ap()[fc][:, half * 4096:(half + 1) * 4096],
                    max_dma_last_dim=2048))))
            for t in range(NT):
                ti = fc * NT + t
                bkk = t % 2
                bkv = 2 + t % 2
                if fc == 1 and t < 2:
                    wait("tensor", sa_, k_ev_a[7])
                    wait("tensor", sv_, v_ev_v[7])
                if ti >= 2:
                    wait("tensor", sa_, k_ev_a[ti - 2])
                pk = emit_kproj(t, fc, bkk, wkw)
                if t == NT - 1:
                    ring_rel[wkw[0][0]] = pk
                    ring_rel[wkw[1][0]] = pk
                wait("scalar", sp_, pk)
                em("scalar", lambda E, t=t, bkk=bkk: E.activation(
                    out=KTM[:, (t % 2) * 512:(t % 2) * 512 + 512],
                    in_=pb[bkk][:, 0:512], func=AF.Relu))
                k_ev_a[ti] = aflush()
                if ti >= 2:
                    wait("tensor", sv_, v_ev_v[ti - 2])
                pv2 = emit_kproj(t, fc, bkv, wvw)
                if t == NT - 1:
                    ring_rel[wvw[0][0]] = pv2
                    ring_rel[wvw[1][0]] = pv2
                wait("vector", sp_, pv2)
                em("vector", lambda E, t=t, bkv=bkv: E.tensor_copy(
                    out=V65[:, (t % 2) * 520:(t % 2) * 520 + 520]
                    .rearrange("p (h e) -> p h e", e=65)[:, :, 0:64],
                    in_=pb[bkv][:, 0:512].rearrange("p (h f) -> p h f", f=64)))
                v_ev_v[ti] = vflush()
                if t >= 1:
                    kv_done_p = emit_kv(t - 1, fc)
            kv_done_p = emit_kv(NT - 1, fc)
            emit_kv_evict(fc)
        kv_ev_flush = kv_state["flush"]

        wait("sync", sv_, kv_ev_flush)
        cc_dma = sdma(lambda E: E.dma_start(out=cc_in.ap(),
                                            in_=SC1[:, 0:NPAIR * 65]))
        early_wq = []
        for fq in range(3):
            s = ring_acquire()
            early_wq.append((s, gdma(lambda E, fq=fq, s=s: E.dma_start(
                out=rsl(s, 2048),
                in_=wq_h.ap()[fq], max_dma_last_dim=2048))))
        wait("gpsimd", sx_, cc_dma)
        C["cc"] += 1
        em("gpsimd", lambda E: E.collective_compute(
            "AllReduce", ALU.add,
            replica_groups=[[0, 1, 2, 3], [4, 5, 6, 7]],
            ins=[cc_in.ap().opt()], outs=[cc_out.ap().opt()]).then_inc(sc_, 1))

        q_ev_a = {}
        for fq in range(16):
            if fq < 3:
                s, wq_dma = early_wq[fq]
            else:
                s = ring_acquire()
                wq_dma = gdma(lambda E, fq=fq, s=s: E.dma_start(
                    out=rsl(s, 2048),
                    in_=wq_h.ap()[fq], max_dma_last_dim=2048))
            if fq == 0:
                wait("tensor", sv_, ln1_ready[0])
                wait("tensor", sa_, ln1_ready[1])
            if fq >= 2:
                wait("tensor", sa_, q_ev_a[fq - 2])
            wait("tensor", sw_, wq_dma)
            for kc in range(KC):
                for tg in range(2):
                    bank = (fq % 2) * 2 + tg
                    mk = (lambda E, kc=kc, tg=tg, bank=bank, s=s: E.matmul(
                        out=pb[bank][:, 0:512],
                        lhsT=rsl(s, 2048)[:, kc * 128:(kc + 1) * 128],
                        rhs=A[:, kc * 1024 + tg * 512: kc * 1024 + (tg + 1) * 512],
                        start=(kc == 0), stop=(kc == KC - 1)))
                    if kc == KC - 1 and tg == 1:
                        pq = pinc(mk)
                    else:
                        em("tensor", lambda E, f=mk: f(E))
            ring_rel[s] = pq
            wait("scalar", sp_, pq)
            for tg in range(2):
                bank = (fq % 2) * 2 + tg
                em("scalar", lambda E, fq=fq, tg=tg, bank=bank: E.activation(
                    out=B[:, fq * 1024 + tg * 512: fq * 1024 + (tg + 1) * 512],
                    in_=pb[bank][:, 0:512], func=AF.Relu))
            q_ev_a[fq] = aflush()
        q_ready_a = q_ev_a[15]

        wait("gpsimd", sc_, C["cc"])
        kv_ready_w = 0
        for par in range(2):
            src_ = cc_out.ap().rearrange("p (b fc jp e) -> p b fc jp e",
                                         b=2, fc=4, jp=4)[par * 64:(par + 1) * 64]
            dst = KV65[par * 64:(par + 1) * 64, :].rearrange(
                "p (b fc jp two e) -> p b fc jp two e",
                b=2, fc=4, jp=4, two=2)[:, :, :, :, par, :]
            kv_ready_w = gdma(lambda E, src_=src_, dst=dst: E.dma_start(
                out=dst, in_=src_))

        ROUNDS = [7, 7, 7, 7, 4]
        div_fl = {}
        tp_done = {}
        tile_inner = {}
        inner_ready = (0, 0)
        wo_ev_v = {}
        wo0_halves = []
        for half in range(2):
            s = ring_acquire()
            wo0_halves.append((s, gdma(
                lambda E, half=half, s=s: E.dma_start(
                    out=rsl(s, 4096),
                    in_=wo_h.ap()[0][:, half * 4096:(half + 1) * 4096],
                    max_dma_last_dim=2048))))
        wo0_pw = {}
        for t in range(NT):
            bt = t // 4
            if t == 0:
                wait("tensor", sw_, kv_ready_w)
                wait("tensor", sa_, q_ready_a)
            if t >= 2:
                wait("vector", sp_, tp_done[t - 2])
            if 2 <= t and t - 2 < 4:
                tw = t - 2
                gates = tile_inner[tw]
                wait("tensor", sv_, gates[0])
                wait("tensor", sa_, gates[1])
                if tw == 0:
                    wait("tensor", sw_, bo_dma)
                    wait("tensor", sw_, wo0_halves[1][1])
                else:
                    wait("tensor", sv_, wo_ev_v[(0, 0, tw - 1)])
                for ic in range(KC):
                    s = wo0_halves[ic // 8][0]
                    mk = (lambda E, ic=ic, tw=tw, s=s: E.matmul(
                        out=pb[7][:, 0:512],
                        lhsT=A[:, ic * 1024 + tw * 128: ic * 1024 + (tw + 1) * 128],
                        rhs=rsl(s, 4096)[:, (ic % 8) * 512:(ic % 8 + 1) * 512],
                        start=(ic == 0), stop=False))
                    em("tensor", lambda E, f=mk: f(E))
                mkb = (lambda E: E.matmul(
                    out=pb[7][:, 0:512], lhsT=E1[:, :],
                    rhs=BOP[:, 0:512],
                    start=False, stop=True, skip_group_check=True))
                wo0_pw[tw] = pinc(mkb)
                wait("vector", sp_, wo0_pw[tw])
                if tw == 0:
                    wait("vector", sx_, cc_dma)
                em("vector", lambda E, tw=tw: E.tensor_scalar(
                    out=SC1[:, tw * D: tw * D + 512],
                    in0=pb[7][:, 0:512], scalar1=1.0, scalar2=None,
                    op0=ALU.mult, op1=ALU.add,
                    accum_out=STS[:, C_XA + 4 * tw:C_XA + 4 * tw + 1]))
                wo_ev_v[(0, 0, tw)] = vflush()
            if t >= 1:
                wait("tensor", sv_, div_fl[t - 1])
                tile_inner[t - 1] = emit_transposes(
                    ((t - 1) % 2) * D, (A, t - 1), [5, 6])
                tp_done[t - 1] = C["p"]
            h = 0
            for r, nh in enumerate(ROUNDS):
                for i in range(nh):
                    mk = (lambda E, h=h, i=i, t=t, r=r, bt=bt, nh=nh: E.matmul(
                        out=PBIG[:, r * 512 + i * 65: r * 512 + (i + 1) * 65],
                        lhsT=B[:, (h // 2) * 1024 + t * 128:
                               (h // 2) * 1024 + (t + 1) * 128],
                        rhs=KV65[:, (bt * H + h) * 65:(bt * H + h + 1) * 65],
                        start=(i == 0), stop=(i == nh - 1),
                        skip_group_check=(i != 0)))
                    if r == len(ROUNDS) - 1 and i == nh - 1:
                        pr = pinc(mk)
                    else:
                        em("tensor", lambda E, f=mk: f(E))
                    h += 1
            wait("vector", sp_, pr)
            em("vector", lambda E: E.tensor_scalar(
                out=STS[:, C_NM:C_NM + 35].rearrange("p (r i) -> p r i", i=7),
                in0=PBIG[:, :].rearrange("p (r x) -> p r x", x=512)
                [:, 0:5, 0:455]
                .rearrange("p r (i e) -> p r i e", e=65)[:, :, :, 64:65],
                scalar1=1e-6, scalar2=None, op0=ALU.max))
            nv = vflush()
            wait("scalar", sv_, nv)
            em("scalar", lambda E: E.activation(
                out=STS[:, C_LNN:C_LNN + 35], in_=STS[:, C_NM:C_NM + 35],
                func=AF.Ln))
            em("scalar", lambda E: E.activation(
                out=STS[:, C_RC:C_RC + 35], in_=STS[:, C_LNN:C_LNN + 35],
                func=AF.Exp, scale=CST[:, 2:3]))
            ra = aflush()
            wait("vector", sa_, ra)
            slot = t % 2
            in0a = PBIG[:, :].rearrange("p (r x) -> p r x", x=512)[:, 0:4, 0:455] \
                .rearrange("p r (i e) -> p r i e", e=65)[:, :, :, 0:64]
            in1a = STS[:, C_RC:C_RC + 28].rearrange(
                "p (r i one) -> p r i one", r=4, i=7, one=1)
            a0, a1 = bass.broadcast_tensor_aps(in0a, in1a)
            outa = TMP32[:, slot * D: slot * D + 1792].rearrange(
                "p (r i e) -> p r i e", i=7, e=64)
            em("vector", lambda E, a0=a0, a1=a1, outa=outa: E.tensor_tensor(
                out=outa, in0=a0, in1=a1, op=ALU.mult))
            in0b = PBIG[:, 4 * 512: 4 * 512 + 4 * 65].rearrange(
                "p (i e) -> p i e", e=65)[:, :, 0:64]
            in1b = STS[:, C_RC + 28:C_RC + 32].rearrange(
                "p (i one) -> p i one", one=1)
            b0, b1 = bass.broadcast_tensor_aps(in0b, in1b)
            outb = TMP32[:, slot * D + 1792: slot * D + 2048].rearrange(
                "p (i e) -> p i e", e=64)
            em("vector", lambda E, b0=b0, b1=b1, outb=outb: E.tensor_tensor(
                out=outb, in0=b0, in1=b1, op=ALU.mult))
            div_fl[t] = vflush()
        wait("tensor", sv_, div_fl[NT - 1])
        tile_inner[NT - 1] = emit_transposes(
            ((NT - 1) % 2) * D, (A, NT - 1), [5, 6])
        tp_done[NT - 1] = C["p"]
        ring_rel[wo0_halves[0][0]] = wo0_pw[3]
        ring_rel[wo0_halves[1][0]] = wo0_pw[3]
        inner_ready = (C["v"], C["a"])

        x2_dma = {}

        def wo_strips(fc):
            halves = []
            for half in range(2):
                s = ring_acquire()
                halves.append((s, gdma(
                    lambda E, fc=fc, half=half, s=s: E.dma_start(
                        out=rsl(s, 4096),
                        in_=wo_h.ap()[fc][:, half * 4096:(half + 1) * 4096],
                        max_dma_last_dim=2048))))
            return halves

        def wo_mms(g, fc, halves, tiles):
            fcp = fc % 2
            pws = []
            for j, t in enumerate(tiles):
                bank = fcp * 4 + j
                if g == 0 and fc == 0 and j == 0:
                    wait("tensor", sv_, inner_ready[0])
                    wait("tensor", sa_, inner_ready[1])
                    wait("tensor", sw_, bo_dma)
                prev_ev = wo_ev_v.get((g, fc - 2, j))
                if g == 1 and fc < 2:
                    prev_ev = wo_ev_v.get((0, fc + 2, j))
                if prev_ev:
                    wait("tensor", sv_, prev_ev)
                for ic in range(KC):
                    s, wdma = halves[ic // 8]
                    if ic % 8 == 0 and j == 0:
                        wait("tensor", sw_, wdma)
                    mk = (lambda E, ic=ic, t=t, bank=bank, s=s: E.matmul(
                        out=pb[bank][:, 0:512],
                        lhsT=A[:, ic * 1024 + t * 128: ic * 1024 + (t + 1) * 128],
                        rhs=rsl(s, 4096)[:, (ic % 8) * 512:(ic % 8 + 1) * 512],
                        start=(ic == 0), stop=False))
                    em("tensor", lambda E, f=mk: f(E))
                mkb = (lambda E, fc=fc, bank=bank: E.matmul(
                    out=pb[bank][:, 0:512], lhsT=E1[:, :],
                    rhs=BOP[:, fc * 512:(fc + 1) * 512],
                    start=False, stop=True, skip_group_check=True))
                pws.append(pinc(mkb))
            ring_rel[halves[0][0]] = pws[-1]
            ring_rel[halves[1][0]] = pws[-1]
            return pws

        G1SLOT = [4, 0, 1, 2]

        def act_fin(j, with_sx):
            if with_sx:
                em("scalar", lambda E, j=j: E.activation(
                    out=STS[:, C_T4:C_T4 + 4],
                    in_=STS[:, C_XA + 4 * j:C_XA + 4 * j + 4], func=AF.Copy,
                    accum_out=STS[:, C_SX + j:C_SX + j + 1]))
            em("scalar", lambda E, j=j: E.activation(
                out=STS[:, C_T4 + 4:C_T4 + 8],
                in_=STS[:, C_SQ4 + 4 * j:C_SQ4 + 4 * j + 4], func=AF.Copy,
                accum_out=STS[:, C_SXX + j:C_SXX + j + 1]))
            em("scalar", lambda E, j=j: E.mul(
                out=STS[:, C_MEAN + j:C_MEAN + j + 1],
                in_=STS[:, C_SX + j:C_SX + j + 1], mul=1.0 / D))
            em("scalar", lambda E, j=j: E.activation(
                out=STS[:, C_MSQ + j:C_MSQ + j + 1],
                in_=STS[:, C_SX + j:C_SX + j + 1], func=AF.Square,
                scale=CST[:, 5:6]))
            em("scalar", lambda E, j=j: E.activation(
                out=STS[:, C_VAR + j:C_VAR + j + 1],
                in_=STS[:, C_MSQ + j:C_MSQ + j + 1], func=AF.Relu,
                scale=CST[:, 2:3], bias=STS[:, C_SXX + j:C_SXX + j + 1]))
            em("scalar", lambda E, j=j: E.activation(
                out=STS[:, C_LNV + j:C_LNV + j + 1],
                in_=STS[:, C_VAR + j:C_VAR + j + 1],
                func=AF.Ln, scale=CST[:, 4:5], bias=CST[:, 0:1]))
            em("scalar", lambda E, j=j: E.activation(
                out=STS[:, C_RSTD + j:C_RSTD + j + 1],
                in_=STS[:, C_LNV + j:C_LNV + j + 1],
                func=AF.Exp, scale=CST[:, 1:2]))
            return aflush()

        def wo_evct(g, fc, j, pw, gates=()):
            bank = (fc % 2) * 4 + j
            slot = j if g == 0 else G1SLOT[j]
            wait("vector", sp_, pw)
            for sem, val in gates:
                wait("vector", sem, val)
            em("vector", lambda E, j=j, fc=fc, bank=bank, slot=slot:
               E.tensor_scalar(
                   out=SC1[:, slot * D + fc * 512: slot * D + (fc + 1) * 512],
                   in0=pb[bank][:, 0:512], scalar1=1.0, scalar2=None,
                   op0=ALU.mult, op1=ALU.add,
                   accum_out=STS[:, C_XA + 4 * j + fc:C_XA + 4 * j + fc + 1]))
            fl = vflush()
            wo_ev_v[(g, fc, j)] = fl
            wait("scalar", sv_, fl)
            em("scalar", lambda E, j=j, fc=fc, slot=slot: E.activation(
                out=SQD[:, 0:512],
                in_=SC1[:, slot * D + fc * 512: slot * D + (fc + 1) * 512],
                func=AF.Square,
                accum_out=STS[:, C_SQ4 + 4 * j + fc:C_SQ4 + 4 * j + fc + 1]))
            return fl

        for j in range(4):
            wait("scalar", sv_, wo_ev_v[(0, 0, j)])
            em("scalar", lambda E, j=j: E.activation(
                out=SQD[:, 0:512], in_=SC1[:, j * D: j * D + 512],
                func=AF.Square,
                accum_out=STS[:, C_SQ4 + 4 * j:C_SQ4 + 4 * j + 1]))
        for fc in range(1, 4):
            halves = wo_strips(fc)
            if fc == 1:
                wait("tensor", sv_, C["v"])
                wait("tensor", sa_, C["a"])
            pws = wo_mms(0, fc, halves, [0, 1, 2, 3])
            for j in range(4):
                wo_evct(0, fc, j, pws[j])
        t1 = [4, 5, 6, 7]
        pws10 = wo_mms(1, 0, wo_strips(0), t1)

        aqA0 = [act_fin(j, True) for j in range(4)]
        xw = {}
        xw[0] = sdma(lambda E: E.dma_start(
            out=XST[:, 0:D], in_=x_h.ap()[0:128, :]))
        xw[1] = sdma(lambda E: E.dma_start(
            out=XST[:, D:2 * D], in_=x_h.ap()[128:256, :]))
        rfl5, aq2, ap2fl, tpd, tpg1 = {}, {}, {}, {}, {}
        xw5 = {}

        def lna_apply_resid(g, j, tmp_gate=None):
            t = 4 * g + j
            slot = j if g == 0 else G1SLOT[j]
            aq = aqA0[j] if g == 0 else aqA1[j]
            xwd = xw if g == 0 else xw5
            wait("vector", sa_, aq)
            if tmp_gate is not None:
                wait("vector", sp_, tmp_gate)
            em("vector", lambda E, j=j, slot=slot: E.tensor_scalar(
                out=TMP32[:, (j % 2) * D:(j % 2 + 1) * D],
                in0=SC1[:, slot * D:(slot + 1) * D],
                scalar1=STS[:, C_MEAN + j:C_MEAN + j + 1],
                scalar2=STS[:, C_RSTD + j:C_RSTD + j + 1],
                op0=ALU.subtract, op1=ALU.mult))
            wait("vector", sx_, xwd[j])
            em("vector", lambda E, j=j, slot=slot: E.scalar_tensor_tensor(
                out=SC1[:, slot * D:(slot + 1) * D],
                in0=TMP32[:, (j % 2) * D:(j % 2 + 1) * D], scalar=1.0,
                in1=XST[:, (j % 2) * D:(j % 2 + 1) * D],
                op0=ALU.mult, op1=ALU.add,
                accum_out=STS[:, C_SX + j:C_SX + j + 1]))
            rfl = vflush()
            rfl5[(g, j)] = rfl
            wait("sync", sv_, rfl)
            x2_dma[t] = sdma(lambda E, t=t, slot=slot: E.dma_start(
                out=x2_h.ap()[t * 128:(t + 1) * 128, :],
                in_=SC1[:, slot * D:(slot + 1) * D]))
            wait("scalar", sv_, rfl)
            for q in range(4):
                em("scalar", lambda E, j=j, q=q, slot=slot: E.activation(
                    out=SQD[:, 0:512],
                    in_=SC1[:, slot * D + q * 512: slot * D + (q + 1) * 512],
                    func=AF.Square,
                    accum_out=STS[:, C_SQ4 + 4 * j + q:C_SQ4 + 4 * j + q + 1]))
            aq2[(g, j)] = act_fin(j, False)

        def ln2_apply(g, j, src_ap, gates=()):
            wait("vector", sa_, aq2[(g, j)])
            for sem, val in gates:
                wait("vector", sem, val)
            em("vector", lambda E, j=j, src_ap=src_ap: E.tensor_scalar(
                out=TMP32[:, (j % 2) * D:(j % 2 + 1) * D],
                in0=src_ap,
                scalar1=STS[:, C_MEAN + j:C_MEAN + j + 1],
                scalar2=STS[:, C_RSTD + j:C_RSTD + j + 1],
                op0=ALU.subtract, op1=ALU.mult))
            ap2fl[(g, j)] = vflush()

        lna_apply_resid(0, 0)
        wait("sync", sv_, rfl5[(0, 0)])
        xw[2] = sdma(lambda E: E.dma_start(
            out=XST[:, 0:D], in_=x_h.ap()[256:384, :]))
        lna_apply_resid(0, 1)
        wait("sync", sv_, rfl5[(0, 1)])
        xw[3] = sdma(lambda E: E.dma_start(
            out=XST[:, D:2 * D], in_=x_h.ap()[384:512, :]))
        ln2_apply(0, 0, SC1[:, 0:D])
        wo_evct(1, 0, 0, pws10[0], gates=((sa_, aqA0[0]),))
        emit_transposes(0, (B, 0), [4, 5],
                        extra_gates=((sv_, ap2fl[(0, 0)]),
                                     (sv_, wo_ev_v[(0, 3, 1)])))
        tpd[0] = C["p"]
        lna_apply_resid(0, 2, tmp_gate=tpd[0])
        ln2_apply(0, 1, SC1[:, D:2 * D])
        wo_evct(1, 0, 1, pws10[1],
                gates=((sx_, x2_dma[0]), (sa_, aq2[(0, 0)])))
        emit_transposes(D, (B, 1), [4, 5], extra_gates=((sv_, ap2fl[(0, 1)]),))
        tpd[1] = C["p"]
        wo_evct(1, 0, 2, pws10[2],
                gates=((sx_, x2_dma[1]), (sa_, aq2[(0, 1)])))
        ln2_apply(0, 2, SC1[:, 2 * D:3 * D])
        emit_transposes(0, (B, 2), [4, 5], extra_gates=((sv_, ap2fl[(0, 2)]),))
        tpd[2] = C["p"]
        lna_apply_resid(0, 3, tmp_gate=tpd[1])
        wo_evct(1, 0, 3, pws10[3],
                gates=((sx_, x2_dma[2]), (sa_, aq2[(0, 2)])))
        ln2_apply(0, 3, SC1[:, 3 * D:4 * D])
        ln2_ready_g0 = emit_transposes(D, (B, 3), [4, 5],
                                       extra_gates=((sv_, ap2fl[(0, 3)]),))
        tpd[3] = C["p"]

        wait("tensor", sv_, ln2_ready_g0[0])
        wait("tensor", sa_, ln2_ready_g0[1])
        pws11 = wo_mms(1, 1, wo_strips(1), t1)
        for j in range(4):
            wo_evct(1, 1, j, pws11[j])


        pws12 = wo_mms(1, 2, wo_strips(2), t1)
        for j in range(4):
            wo_evct(1, 2, j, pws12[j])
        pws13 = wo_mms(1, 3, wo_strips(3), t1)

        gu_slot, gu_dma, gu_prev, gu_div_v = {}, {}, {}, {}
        gu_pg = {}

        def gu_load(jp):
            s = ring_acquire()
            gu_slot[jp] = s
            gdma(lambda E, jp=jp, s=s: E.dma_start(
                out=rsl(s, 4096)[:, 0:2048], in_=gu_h.ap()[jp],
                max_dma_last_dim=2048))
            gu_dma[jp] = gdma(lambda E, jp=jp, s=s: E.dma_start(
                out=rsl(s, 4096)[:, 2048:4096], in_=gu_h.ap()[16 + jp],
                max_dma_last_dim=2048))

        def gu_banks(jp, tg):
            if tg == 0:
                return (2, 3) if jp % 2 == 0 else (6, 7)
            return (4, 5) if jp % 2 == 0 else (2, 3)

        def gu_pe(jp, tg, extra=()):
            s = gu_slot[jp]
            bg, bu = gu_banks(jp, tg)
            wait("tensor", sw_, gu_dma[jp])
            for sem, val in extra:
                wait("tensor", sem, val)
            prev = gu_prev.get(bg)
            if prev:
                wait("tensor", sv_, prev)
            for part, bank in ((0, bg), (1, bu)):
                for kc in range(KC):
                    mk = (lambda E, kc=kc, part=part, bank=bank, s=s, tg=tg:
                          E.matmul(
                              out=pb[bank][:, 0:512],
                              lhsT=rsl(s, 4096)[:, part * 2048 + kc * 128:
                                                part * 2048 + (kc + 1) * 128],
                              rhs=B[:, kc * 1024 + tg * 512:
                                    kc * 1024 + tg * 512 + 512],
                              start=(kc == 0), stop=(kc == KC - 1)))
                    if part == 1 and kc == KC - 1:
                        pg = pinc(mk)
                    else:
                        em("tensor", lambda E, f=mk: f(E))
            if tg == 1:
                ring_rel[s] = pg
            gu_pg[(jp, tg)] = pg

        def gu_div(jp, tg):
            bg, bu = gu_banks(jp, tg)
            wait("scalar", sp_, gu_pg[(jp, tg)])
            em("scalar", lambda E, tg=tg, bg=bg: E.activation(
                out=GT[:, tg * 512:(tg + 1) * 512], in_=pb[bg][:, 0:512],
                func=AF.Silu))
            ga = aflush()
            wait("vector", sa_, ga)
            em("vector", lambda E, jp=jp, tg=tg, bu=bu: E.tensor_tensor(
                out=A[:, jp * 1024 + tg * 512: jp * 1024 + tg * 512 + 512],
                in0=GT[:, tg * 512:(tg + 1) * 512],
                in1=pb[bu][:, 0:512], op=ALU.mult))
            fl = vflush()
            gu_div_v[(jp, tg)] = fl
            gu_prev[bg] = fl

        def gu_mms(jp, tg, extra=()):
            gu_pe(jp, tg, extra=extra)
            gu_div(jp, tg)

        aqA1 = []
        for j in range(4):
            wo_evct(1, 3, j, pws13[j])
            aqA1.append(act_fin(j, True))
        wait("sync", sv_, ap2fl[(0, 2)])
        xw5[0] = sdma(lambda E: E.dma_start(
            out=XST[:, 0:D], in_=x_h.ap()[4 * 128:5 * 128, :]))
        wait("sync", sv_, rfl5[(0, 3)])
        xw5[1] = sdma(lambda E: E.dma_start(
            out=XST[:, D:2 * D], in_=x_h.ap()[5 * 128:6 * 128, :]))
        for jp in range(4):
            gu_load(jp)
        lna_apply_resid(1, 0, tmp_gate=tpd[2])
        wait("sync", sv_, rfl5[(1, 0)])
        xw5[2] = sdma(lambda E: E.dma_start(
            out=XST[:, 0:D], in_=x_h.ap()[6 * 128:7 * 128, :]))
        lna_apply_resid(1, 1, tmp_gate=tpd[3])
        wait("sync", sv_, rfl5[(1, 1)])
        xw5[3] = sdma(lambda E: E.dma_start(
            out=XST[:, D:2 * D], in_=x_h.ap()[7 * 128:8 * 128, :]))
        ln2_apply(1, 0, SC1[:, 4 * D:5 * D])
        emit_transposes(0, (B, 4), [0, 1],
                        extra_gates=((sv_, ap2fl[(1, 0)]),
                                     (sv_, wo_ev_v[(1, 2, 1)])))
        tpg1[0] = C["p"]
        gu_pe(0, 0, extra=((sv_, wo_ev_v[(1, 2, 3)]),
                           (sv_, ln2_ready_g0[0]), (sa_, ln2_ready_g0[1])))
        lna_apply_resid(1, 2, tmp_gate=tpg1[0])
        ln2_apply(1, 1, SC1[:, 0:D])
        emit_transposes(D, (B, 5), [0, 1], extra_gates=((sv_, ap2fl[(1, 1)]),))
        tpg1[1] = C["p"]
        gu_div(0, 0)
        gu_pe(1, 0, extra=((sv_, wo_ev_v[(1, 3, 3)]),))
        lna_apply_resid(1, 3, tmp_gate=tpg1[1])
        ln2_apply(1, 2, SC1[:, D:2 * D])
        emit_transposes(0, (B, 6), [0, 1], extra_gates=((sv_, ap2fl[(1, 2)]),))
        tpg1[2] = C["p"]
        gu_div(1, 0)
        gu_pe(2, 0)
        ln2_apply(1, 3, SC1[:, 2 * D:3 * D])
        ln2_ready_g1 = emit_transposes(D, (B, 7), [0, 1],
                                       extra_gates=((sv_, ap2fl[(1, 3)]),))
        tpg1[3] = C["p"]
        gu_div(2, 0)
        gu_pe(3, 0)
        gu_div(3, 0)
        for j in range(4):
            extra = []
            if j == 0:
                extra = [(sv_, ln2_ready_g1[0]), (sa_, ln2_ready_g1[1]),
                         (sv_, wo_ev_v[(1, 3, 1)])]
            elif j == 1:
                extra = [(sv_, wo_ev_v[(1, 3, 3)])]
            gu_mms(j, 1, extra=extra)
        for jp in range(4, 16):
            gu_load(jp)
            gu_mms(jp, 0)
            gu_mms(jp, 1)
        inter_ready_v = gu_div_v[(15, 1)]

        dn_ev_v = {}

        def dn_strips(fc):
            halves = []
            for half in range(2):
                s = ring_acquire()
                halves.append((s, gdma(
                    lambda E, fc=fc, half=half, s=s: E.dma_start(
                        out=rsl(s, 4096),
                        in_=dn_h.ap()[fc][:, half * 4096:(half + 1) * 4096],
                        max_dma_last_dim=2048))))
            return halves

        def dn_mms(g, fc, halves, tiles):
            fcp = fc % 2
            pws = []
            for j, t in enumerate(tiles):
                bank = fcp * 4 + j
                if g == 0 and fc == 0 and j == 0:
                    wait("tensor", sv_, inter_ready_v)
                prev_ev = dn_ev_v.get((g, fc - 2, j))
                if g == 1 and fc < 2:
                    prev_ev = dn_ev_v.get((0, fc + 2, j))
                if prev_ev:
                    wait("tensor", sv_, prev_ev)
                for ic in range(KC):
                    s, wdma = halves[ic // 8]
                    if ic % 8 == 0 and j == 0:
                        wait("tensor", sw_, wdma)
                    mk = (lambda E, ic=ic, t=t, bank=bank, s=s: E.matmul(
                        out=pb[bank][:, 0:512],
                        lhsT=A[:, ic * 1024 + t * 128: ic * 1024 + (t + 1) * 128],
                        rhs=rsl(s, 4096)[:, (ic % 8) * 512:(ic % 8 + 1) * 512],
                        start=(ic == 0), stop=(ic == KC - 1)))
                    if ic == KC - 1:
                        pws.append(pinc(mk))
                    else:
                        em("tensor", lambda E, f=mk: f(E))
            ring_rel[halves[0][0]] = pws[-1]
            ring_rel[halves[1][0]] = pws[-1]
            return pws

        def dn_evct(g, fc, j, pw, gates=()):
            bank = (fc % 2) * 4 + j
            slot = j if g == 0 else G1SLOT[j]
            wait("vector", sp_, pw)
            for sem, val in gates:
                wait("vector", sem, val)
            em("vector", lambda E, j=j, fc=fc, bank=bank, slot=slot:
               E.tensor_scalar(
                   out=SC1[:, slot * D + fc * 512: slot * D + (fc + 1) * 512],
                   in0=pb[bank][:, 0:512], scalar1=1.0, scalar2=None,
                   op0=ALU.mult, op1=ALU.add,
                   accum_out=STS[:, C_XA + 4 * j + fc:C_XA + 4 * j + fc + 1]))
            fl = vflush()
            dn_ev_v[(g, fc, j)] = fl
            wait("scalar", sv_, fl)
            em("scalar", lambda E, j=j, fc=fc, slot=slot: E.activation(
                out=SQD[:, 0:512],
                in_=SC1[:, slot * D + fc * 512: slot * D + (fc + 1) * 512],
                func=AF.Square,
                accum_out=STS[:, C_SQ4 + 4 * j + fc:C_SQ4 + 4 * j + fc + 1]))
            return fl

        aqA0_7 = []
        for fc in range(4):
            halves = dn_strips(fc)
            pws = dn_mms(0, fc, halves, [0, 1, 2, 3])
            for j in range(4):
                dn_evct(0, fc, j, pws[j])
                if fc == 3:
                    aqA0_7.append(act_fin(j, True))
        pws10 = dn_mms(1, 0, dn_strips(0), t1)
        pws11 = dn_mms(1, 1, dn_strips(1), t1)

        g0_odma = {}
        xw7 = {}
        for j in (0, 1):
            xw7[j] = sdma(lambda E, t=j: E.dma_start(
                out=XST[:, (t % 2) * D:(t % 2 + 1) * D],
                in_=x2_h.ap()[t * 128:(t + 1) * 128, :]))
        for j in range(4):
            wait("vector", sa_, aqA0_7[j])
            em("vector", lambda E, j=j: E.tensor_scalar(
                out=TMP32[:, (j % 2) * D:(j % 2 + 1) * D],
                in0=SC1[:, j * D:(j + 1) * D],
                scalar1=STS[:, C_MEAN + j:C_MEAN + j + 1],
                scalar2=STS[:, C_RSTD + j:C_RSTD + j + 1],
                op0=ALU.subtract, op1=ALU.mult))
            wait("vector", sx_, xw7[j])
            em("vector", lambda E, j=j: E.tensor_tensor(
                out=SC1[:, j * D:(j + 1) * D],
                in0=TMP32[:, (j % 2) * D:(j % 2 + 1) * D],
                in1=XST[:, (j % 2) * D:(j % 2 + 1) * D], op=ALU.add))
            rfl = vflush()
            wait("sync", sv_, rfl)
            g0_odma[j] = odma(lambda E, j=j: E.dma_start(
                out=out_h.ap()[j * 128:(j + 1) * 128, :],
                in_=SC1[:, j * D:(j + 1) * D]))
            if j < 2:
                xw7[j + 2] = sdma(lambda E, t=j + 2: E.dma_start(
                    out=XST[:, (t % 2) * D:(t % 2 + 1) * D],
                    in_=x2_h.ap()[t * 128:(t + 1) * 128, :]))
            if j == 0:
                dn_evct(1, 0, 0, pws10[0], gates=((sa_, aqA0_7[0]),))
            else:
                dn_evct(1, 0, j, pws10[j],
                        gates=((so_, g0_odma[j - 1]), (sa_, aqA0_7[j])))
        for j in range(4):
            dn_evct(1, 1, j, pws11[j])
        pws12 = dn_mms(1, 2, dn_strips(2), t1)
        for j in range(4):
            dn_evct(1, 2, j, pws12[j])
        pws13 = dn_mms(1, 3, dn_strips(3), t1)

        xw_g1 = {}
        for j in (0, 1):
            xw_g1[j] = sdma(lambda E, j=j: E.dma_start(
                out=XST[:, (j % 2) * D:(j % 2 + 1) * D],
                in_=x2_h.ap()[(4 + j) * 128:(5 + j) * 128, :]))
        aqA1_7 = []
        for j in range(4):
            dn_evct(1, 3, j, pws13[j])
            aqA1_7.append(act_fin(j, True))
        for j in range(4):
            t = 4 + j
            slot = G1SLOT[j]
            wait("vector", sa_, aqA1_7[j])
            em("vector", lambda E, j=j, slot=slot: E.tensor_scalar(
                out=TMP32[:, (j % 2) * D:(j % 2 + 1) * D],
                in0=SC1[:, slot * D:(slot + 1) * D],
                scalar1=STS[:, C_MEAN + j:C_MEAN + j + 1],
                scalar2=STS[:, C_RSTD + j:C_RSTD + j + 1],
                op0=ALU.subtract, op1=ALU.mult))
            wait("vector", sx_, xw_g1[j])
            em("vector", lambda E, j=j, slot=slot: E.tensor_tensor(
                out=SC1[:, slot * D:(slot + 1) * D],
                in0=TMP32[:, (j % 2) * D:(j % 2 + 1) * D],
                in1=XST[:, (j % 2) * D:(j % 2 + 1) * D], op=ALU.add))
            rfl = vflush()
            wait("sync", sv_, rfl)
            if j < 2:
                xw_g1[j + 2] = sdma(lambda E, j=j: E.dma_start(
                    out=XST[:, (j % 2) * D:(j % 2 + 1) * D],
                    in_=x2_h.ap()[(6 + j) * 128:(7 + j) * 128, :]))
            odma(lambda E, t=t, slot=slot: E.dma_start(
                out=out_h.ap()[t * 128:(t + 1) * 128, :],
                in_=SC1[:, slot * D:(slot + 1) * D]))

        final_o = C["o"]
        em("sync", lambda E, v=final_o: E.wait_ge(so_, v))
        em("gpsimd", lambda E, v=final_o: E.wait_ge(so_, v))

        with nc.Block() as block:
            @block.sync
            def _(sync):
                for fn in P["sync"]:
                    fn(sync)

            @block.gpsimd
            def _(gpsimd):
                for fn in P["gpsimd"]:
                    fn(gpsimd)

            @block.vector
            def _(vector):
                for fn in P["vector"]:
                    fn(vector)

            @block.scalar
            def _(scalar):
                for fn in P["scalar"]:
                    fn(scalar)

            @block.tensor
            def _(tensor):
                for fn in P["tensor"]:
                    fn(tensor)

    return nc


def _strips(w, width):
    n = w.shape[1] // width
    out = np.empty((n, 128, 16 * width), dtype=np.float32)
    for i in range(n):
        s = w[:, i * width:(i + 1) * width].reshape(16, 128, width)
        out[i] = s.transpose(1, 0, 2).reshape(128, 16 * width)
    return np.ascontiguousarray(out)


def kernel(L0_attn_hidden_states, L0_attn_Wq, L0_attn_Wk, L0_attn_Wv,
           L0_attn_Wo, L0_attn_bo, L0_ffn_gate_up_proj, L0_ffn_down_proj):
    X = np.asarray(L0_attn_hidden_states, dtype=np.float32)
    Wq = np.asarray(L0_attn_Wq, dtype=np.float32)
    Wk = np.asarray(L0_attn_Wk, dtype=np.float32)
    Wv = np.asarray(L0_attn_Wv, dtype=np.float32)
    Wo = np.asarray(L0_attn_Wo, dtype=np.float32)
    bo = np.asarray(L0_attn_bo, dtype=np.float32).reshape(1, D)
    GU = np.asarray(L0_ffn_gate_up_proj, dtype=np.float32)
    DN = np.asarray(L0_ffn_down_proj, dtype=np.float32)

    wq_s = _strips(Wq, 128)
    wk_s = _strips(Wk, 512)
    wv_s = _strips(Wv, 512)
    wo_s = _strips(Wo, 512)
    ident = np.eye(128, dtype=np.float32)

    nc = build_program()
    in_maps = []
    for c in range(8):
        G, e = c // 4, c % 4
        xs = np.ascontiguousarray(
            np.concatenate([X[2 * G, e::4, :], X[2 * G + 1, e::4, :]], axis=0))
        gu_s = _strips(np.ascontiguousarray(GU[e]), 128)
        dn_s = _strips(np.ascontiguousarray(DN[e]), 512)
        in_maps.append({
            "x": xs, "wq": wq_s, "wk": wk_s, "wv": wv_s, "wo": wo_s,
            "bo": bo, "gu": gu_s, "dn": dn_s, "ident": ident,
        })
    res = run_bass_kernel_spmd(nc, in_maps, core_ids=list(range(8)))

    out = np.empty((4, 2048, D), dtype=np.float32)
    for c in range(8):
        G, e = c // 4, c % 4
        r = res.results[c]["out"]
        out[2 * G, e::4, :] = r[:512]
        out[2 * G + 1, e::4, :] = r[512:]
    return out


# version 22
# speedup vs baseline: 1.0420x; 1.0228x over previous
import numpy as np
import concourse.bass as bass
import concourse.mybir as mybir
from concourse.bass_utils import run_bass_kernel_spmd
from contextlib import ExitStack

F32 = mybir.dt.float32
BF16 = mybir.dt.bfloat16
AF = mybir.ActivationFunctionType
ALU = mybir.AluOpType
AX = mybir.AxisListType

NT = 8
D = 2048
KC = 16
H = 32
NPAIR = 32
EPS = 1e-5


def build_program():
    nc = bass.Bass("TRN2", target_bir_lowering=False)

    x_h = nc.dram_tensor("x", [1024, D], F32, kind="ExternalInput")
    wq_h = nc.dram_tensor("wq", [16, 128, 2048], F32, kind="ExternalInput")
    wk_h = nc.dram_tensor("wk", [4, 128, 8192], F32, kind="ExternalInput")
    wv_h = nc.dram_tensor("wv", [4, 128, 8192], F32, kind="ExternalInput")
    wo_h = nc.dram_tensor("wo", [4, 128, 8192], F32, kind="ExternalInput")
    bo_h = nc.dram_tensor("bo", [1, D], F32, kind="ExternalInput")
    gu_h = nc.dram_tensor("gu", [32, 128, 2048], F32, kind="ExternalInput")
    dn_h = nc.dram_tensor("dn", [4, 128, 8192], F32, kind="ExternalInput")
    id_h = nc.dram_tensor("ident", [128, 128], F32, kind="ExternalInput")
    out_h = nc.dram_tensor("out", [1024, D], F32, kind="ExternalOutput")

    cc_in = nc.dram_tensor("cc_in", [128, NPAIR * 65], F32)
    cc_out = nc.dram_tensor("cc_out", [128, NPAIR * 65], F32)
    x2_h = nc.dram_tensor("x2scr", [1024, D], F32)

    with ExitStack() as st:
        def sb(name, shape, dt):
            return st.enter_context(nc.sbuf_tensor(name, shape, dt))

        A = sb("A", [128, KC * 1024], BF16)
        B = sb("B", [128, KC * 1024], BF16)
        SC1 = sb("SC1", [128, 5 * D], F32)
        TMP32 = sb("TMP32", [128, 2 * D], F32)
        XST = sb("XST", [128, 2 * D], F32)
        KTM = sb("KTM", [128, 2 * 512], BF16)
        V65 = sb("V65", [128, 2 * 520], BF16)
        KV65 = sb("KV65", [128, 2 * H * 65], BF16)
        RING = sb("RING", [128, 6 * 4096], BF16)
        BOP = sb("BOP", [128, D], BF16)
        E1 = sb("E1", [128, 128], BF16)
        IDS = sb("IDS", [128, 128], F32)
        GT = sb("GT", [128, 1024], BF16)
        STS = sb("STS", [128, 240], F32)
        CST = sb("CST", [128, 6], F32)
        FL = sb("FL", [128, 4], F32)
        SQD = sb("SQD", [128, 512], F32)

        PBIG = st.enter_context(nc.psum_tensor("PBIG", [128, 4096], F32))
        pb = [PBIG[:, i * 512:(i + 1) * 512] for i in range(8)]

        sx_ = st.enter_context(nc.semaphore("sx"))
        sw_ = st.enter_context(nc.semaphore("sw"))
        sv_ = st.enter_context(nc.semaphore("sv"))
        sa_ = st.enter_context(nc.semaphore("sa"))
        sp_ = st.enter_context(nc.semaphore("sp"))
        sc_ = st.enter_context(nc.semaphore("scc"))
        so_ = st.enter_context(nc.semaphore("so"))
        sems = [sx_, sw_, sv_, sa_, sp_, sc_, so_]
        nums = sorted(s.num for s in sems)
        assert nums[-1] - nums[0] == len(sems) - 1, nums

        C_SX, C_SXX, C_MEAN, C_VAR, C_MSQ, C_LNV, C_RSTD = 0, 8, 16, 24, 32, 40, 48
        C_SQ4, C_T4, C_NM, C_LNN, C_RC = 56, 88, 96, 136, 176
        C_XA = 212

        P = {"sync": [], "gpsimd": [], "vector": [], "scalar": [], "tensor": []}
        C = {"x": 0, "w": 0, "v": 0, "a": 0, "p": 0, "cc": 0, "o": 0}

        def em(eng, fn):
            P[eng].append(fn)

        def wait(eng, sem, val):
            if val > 0:
                em(eng, lambda E, s=sem, v=val: E.wait_ge(s, v))

        def vflush():
            em("vector", lambda E: E.memset(FL[:, 0:1], 0.0))
            C["v"] += 1
            em("vector", lambda E: E.memset(FL[:, 1:2], 0.0).then_inc(sv_, 1))
            return C["v"]

        def aflush():
            em("scalar", lambda E: E.mul(out=FL[:, 2:3], in_=CST[:, 0:1], mul=1.0))
            C["a"] += 1
            em("scalar", lambda E: E.mul(out=FL[:, 3:4], in_=CST[:, 0:1], mul=1.0)
               .then_inc(sa_, 1))
            return C["a"]

        def pinc(mk):
            C["p"] += 1
            em("tensor", lambda E, f=mk: f(E).then_inc(sp_, 1))
            return C["p"]

        def sdma(mk):
            C["x"] += 16
            em("sync", lambda E, f=mk: f(E).then_inc(sx_, 16))
            return C["x"]

        def gdma(mk):
            C["w"] += 16
            em("gpsimd", lambda E, f=mk: f(E).then_inc(sw_, 16))
            return C["w"]

        def odma(mk):
            C["o"] += 16
            em("sync", lambda E, f=mk: f(E).then_inc(so_, 16))
            return C["o"]

        ring_rel = [0] * 6
        ring_ptr = [0]

        def ring_acquire():
            s = ring_ptr[0]
            ring_ptr[0] = (s + 1) % 6
            if ring_rel[s] > 0:
                wait("gpsimd", sp_, ring_rel[s])
            return s

        def rsl(s, nels):
            return RING[:, s * 4096: s * 4096 + nels]

        with nc.Block() as blk0:
            @blk0.gpsimd
            def _(gpsimd):
                gpsimd.sem_clear(range(nums[0], nums[-1] + 1))

        with nc.Block() as blk1:
            @blk1.vector
            def _(vector):
                vector.memset(CST[:, 0:1], EPS)
                vector.memset(CST[:, 1:2], -0.5)
                vector.memset(CST[:, 2:3], -1.0)
                vector.memset(CST[:, 3:4], 0.0)
                vector.memset(CST[:, 4:5], 1.0 / D)
                vector.memset(CST[:, 5:6], float(D) ** -0.5)
                vector.memset(E1[:, :], 0.0)
                vector.memset(E1[0:1, :], 1.0)
                vector.memset(BOP[:, :], 0.0)
                vector.memset(KV65[:, :], 0.0)
                for t in range(2):
                    vector.memset(
                        V65[:, t * 520:(t + 1) * 520]
                        .rearrange("p (h e) -> p h e", e=65)[:, :, 64:65], 1.0)

        ident_dma = sdma(lambda E: E.dma_start(out=IDS[:, :], in_=id_h.ap()))
        bo_dma = gdma(lambda E: E.dma_start(out=BOP[0:1, :], in_=bo_h.ap()))

        tp_last = {}

        def emit_ln_stats(n, tile_ap_fn, v_gate=0, a_gate=0, x_waits=None,
                          xa_mode=False, sx_ready=False):
            if a_gate:
                wait("scalar", sv_, a_gate)
            if v_gate:
                wait("vector", sv_, v_gate)
            for j in range(n):
                if x_waits is not None:
                    wait("vector", sx_, x_waits[j])
                    wait("scalar", sx_, x_waits[j])
                if sx_ready:
                    pass
                elif xa_mode:
                    em("vector", lambda E, j=j: E.tensor_scalar(
                        out=STS[:, C_T4:C_T4 + 4],
                        in0=STS[:, C_XA + 4 * j:C_XA + 4 * j + 4],
                        scalar1=1.0, scalar2=None, op0=ALU.mult, op1=ALU.add,
                        accum_out=STS[:, C_SX + j:C_SX + j + 1]))
                else:
                    em("vector", lambda E, ap=tile_ap_fn(j), j=j: E.reduce_sum(
                        out=STS[:, C_SX + j:C_SX + j + 1], in_=ap, axis=AX.X))
                for q in range(4):
                    apq = tile_ap_fn(j)[:, q * 512:(q + 1) * 512]
                    em("scalar", lambda E, apq=apq, j=j, q=q: E.activation(
                        out=GT[:, 0:512], in_=apq, func=AF.Square,
                        accum_out=STS[:, C_SQ4 + 4 * j + q:C_SQ4 + 4 * j + q + 1]))
            av = aflush()
            vv = vflush()
            wait("vector", sv_, vv)
            wait("vector", sa_, av)
            for j in range(n):
                em("vector", lambda E, j=j: E.tensor_scalar(
                    out=STS[:, C_T4:C_T4 + 4],
                    in0=STS[:, C_SQ4 + 4 * j:C_SQ4 + 4 * j + 4],
                    scalar1=1.0, scalar2=None, op0=ALU.mult, op1=ALU.add,
                    accum_out=STS[:, C_SXX + j:C_SXX + j + 1]))
            vv = vflush()
            wait("vector", sv_, vv)
            em("vector", lambda E, n=n: E.tensor_scalar(
                out=STS[:, C_MEAN:C_MEAN + n], in0=STS[:, C_SX:C_SX + n],
                scalar1=1.0 / D, scalar2=None, op0=ALU.mult))
            em("vector", lambda E, n=n: E.tensor_scalar(
                out=STS[:, C_VAR:C_VAR + n], in0=STS[:, C_SXX:C_SXX + n],
                scalar1=1.0 / D, scalar2=None, op0=ALU.mult))
            vv = vflush()
            wait("vector", sv_, vv)
            em("vector", lambda E, n=n: E.tensor_tensor(
                out=STS[:, C_MSQ:C_MSQ + n], in0=STS[:, C_MEAN:C_MEAN + n],
                in1=STS[:, C_MEAN:C_MEAN + n], op=ALU.mult))
            vv = vflush()
            wait("vector", sv_, vv)
            em("vector", lambda E, n=n: E.tensor_tensor(
                out=STS[:, C_VAR:C_VAR + n], in0=STS[:, C_VAR:C_VAR + n],
                in1=STS[:, C_MSQ:C_MSQ + n], op=ALU.subtract))
            vv = vflush()
            wait("scalar", sv_, vv)
            em("scalar", lambda E, n=n: E.activation(
                out=STS[:, C_LNV:C_LNV + n], in_=STS[:, C_VAR:C_VAR + n],
                func=AF.Ln, bias=CST[:, 0:1]))
            em("scalar", lambda E, n=n: E.activation(
                out=STS[:, C_RSTD:C_RSTD + n], in_=STS[:, C_LNV:C_LNV + n],
                func=AF.Exp, scale=CST[:, 1:2]))
            return aflush()

        def act_fin(j, with_sx):
            if with_sx:
                em("scalar", lambda E, j=j: E.activation(
                    out=STS[:, C_T4:C_T4 + 4],
                    in_=STS[:, C_XA + 4 * j:C_XA + 4 * j + 4], func=AF.Copy,
                    accum_out=STS[:, C_SX + j:C_SX + j + 1]))
            em("scalar", lambda E, j=j: E.activation(
                out=STS[:, C_T4 + 4:C_T4 + 8],
                in_=STS[:, C_SQ4 + 4 * j:C_SQ4 + 4 * j + 4], func=AF.Copy,
                accum_out=STS[:, C_SXX + j:C_SXX + j + 1]))
            em("scalar", lambda E, j=j: E.mul(
                out=STS[:, C_MEAN + j:C_MEAN + j + 1],
                in_=STS[:, C_SX + j:C_SX + j + 1], mul=1.0 / D))
            em("scalar", lambda E, j=j: E.activation(
                out=STS[:, C_MSQ + j:C_MSQ + j + 1],
                in_=STS[:, C_SX + j:C_SX + j + 1], func=AF.Square,
                scale=CST[:, 5:6]))
            em("scalar", lambda E, j=j: E.activation(
                out=STS[:, C_VAR + j:C_VAR + j + 1],
                in_=STS[:, C_MSQ + j:C_MSQ + j + 1], func=AF.Relu,
                scale=CST[:, 2:3], bias=STS[:, C_SXX + j:C_SXX + j + 1]))
            em("scalar", lambda E, j=j: E.activation(
                out=STS[:, C_LNV + j:C_LNV + j + 1],
                in_=STS[:, C_VAR + j:C_VAR + j + 1],
                func=AF.Ln, scale=CST[:, 4:5], bias=CST[:, 0:1]))
            em("scalar", lambda E, j=j: E.activation(
                out=STS[:, C_RSTD + j:C_RSTD + j + 1],
                in_=STS[:, C_LNV + j:C_LNV + j + 1],
                func=AF.Exp, scale=CST[:, 1:2]))
            return aflush()

        def emit_transposes(src_col, dst, bank_pair, extra_gates=()):
            buf, t = dst
            key = tuple(bank_pair)
            prev = tp_last.get(key)
            if prev:
                wait("tensor", sv_, prev[0])
                wait("tensor", sa_, prev[1])
            for sem, val in extra_gates:
                wait("tensor", sem, val)
            ev = {}
            for grp in range(4):
                bk = bank_pair[grp % 2]
                if grp >= 2:
                    sem, val = ev[grp - 2]
                    wait("tensor", sem, val)
                for q in range(4):
                    kc = grp * 4 + q
                    mk = (lambda E, kc=kc, bk=bk, q=q, src_col=src_col:
                          E.transpose(
                              out=pb[bk][:, q * 128:(q + 1) * 128],
                              in_=TMP32[:, src_col + kc * 128: src_col + (kc + 1) * 128],
                              identity=IDS[:, :]))
                    if q < 3:
                        em("tensor", lambda E, f=mk: f(E))
                    else:
                        pv = pinc(mk)
                dview = buf[:, :].rearrange("p (kc tt) -> p kc tt", tt=1024)[
                    :, grp * 4:(grp + 1) * 4, t * 128:(t + 1) * 128]
                if grp % 2 == 0:
                    wait("vector", sp_, pv)
                    em("vector", lambda E, bk=bk, dview=dview: E.tensor_copy(
                        out=dview, in_=pb[bk][:, 0:512]))
                    C["v"] += 1
                    em("vector", lambda E: E.memset(FL[:, 0:1], 0.0)
                       .then_inc(sv_, 1))
                    ev[grp] = (sv_, C["v"])
                else:
                    wait("scalar", sp_, pv)
                    em("scalar", lambda E, bk=bk, dview=dview: E.activation(
                        out=dview, in_=pb[bk][:, 0:512], func=AF.Copy))
                    C["a"] += 1
                    em("scalar", lambda E: E.mul(out=FL[:, 2:3],
                                                 in_=CST[:, 0:1], mul=1.0)
                       .then_inc(sa_, 1))
                    ev[grp] = (sa_, C["a"])
            vg = vflush()
            ag = aflush()
            tp_last[key] = (vg, ag)
            return (vg, ag)

        k_ev_a, v_ev_v = {}, {}
        kv_state = {"flush": 0}

        def emit_kv(t, fc):
            ti = fc * NT + t
            bt = t // 4
            wait("tensor", sa_, k_ev_a[ti])
            wait("tensor", sv_, v_ev_v[ti])
            if t == 0 and kv_state["flush"]:
                wait("tensor", sv_, kv_state["flush"])
            done = None
            for j in range(4):
                for par in range(2):
                    hh = 2 * j + par
                    bank = 4 + bt * 2 + j // 2
                    col = (j % 2) * 130 + par * 65
                    first = (t % 4 == 0) and (j % 2 == 0) and (par == 0)
                    mk = (lambda E, t=t, j=j, hh=hh, bank=bank, col=col,
                          first=first: E.matmul(
                              out=pb[bank][:, col:col + 65],
                              lhsT=KTM[:, (t % 2) * 512 + j * 128: (t % 2) * 512 + (j + 1) * 128],
                              rhs=V65[:, (t % 2) * 520 + hh * 65: (t % 2) * 520 + (hh + 1) * 65],
                              start=first, stop=(t % 4 == 3),
                              skip_group_check=not first))
                    if j == 3 and par == 1:
                        done = pinc(mk)
                    else:
                        em("tensor", lambda E, f=mk: f(E))
            return done

        def emit_kproj(t, fc, bank, strip_waits):
            for kc in range(KC):
                s, wv_ = strip_waits[kc // 8]
                if kc % 8 == 0:
                    wait("tensor", sw_, wv_)
                mk = (lambda E, kc=kc, t=t, bank=bank, s=s: E.matmul(
                    out=pb[bank][:, 0:512],
                    lhsT=A[:, kc * 1024 + t * 128: kc * 1024 + (t + 1) * 128],
                    rhs=rsl(s, 4096)[:, (kc % 8) * 512:(kc % 8 + 1) * 512],
                    start=(kc == 0), stop=(kc == KC - 1)))
                if kc < KC - 1:
                    em("tensor", lambda E, f=mk: f(E))
                else:
                    pk = pinc(mk)
            return pk

        fc0_k, fc0_v = [], []
        for half in range(2):
            s = ring_acquire()
            fc0_k.append((s, gdma(lambda E, half=half, s=s: E.dma_start(
                out=rsl(s, 4096),
                in_=wk_h.ap()[0][:, half * 4096:(half + 1) * 4096],
                max_dma_last_dim=2048))))
        for half in range(2):
            s = ring_acquire()
            fc0_v.append((s, gdma(lambda E, half=half, s=s: E.dma_start(
                out=rsl(s, 4096),
                in_=wv_h.ap()[0][:, half * 4096:(half + 1) * 4096],
                max_dma_last_dim=2048))))

        apply_fl = {}
        pend = {}
        kv_done_p = None
        tpB_pinc = {}

        def fm_dst(buf, t, kc0):
            return buf[:, :].rearrange("p (kc tt) -> p kc tt", tt=1024)[
                :, kc0:kc0 + 8, t * 128:(t + 1) * 128]

        def flush_pending(t):
            p = pend.pop(t)
            wait("vector", sp_, p["pv2"])
            em("vector", lambda E, t=t: E.tensor_copy(
                out=V65[:, (t % 2) * 520:(t % 2) * 520 + 520]
                .rearrange("p (h e) -> p h e", e=65)[:, :, 0:64],
                in_=pb[1][:, 0:512].rearrange("p (h f) -> p h f", f=64)))
            v_ev_v[t] = vflush()
            wait("scalar", sp_, p["tpB"])
            em("scalar", lambda E, t=t: E.activation(
                out=fm_dst(A, t, 8), in_=PBIG[:, 1024:2048], func=AF.Copy))
            wait("scalar", sp_, p["pk"])
            em("scalar", lambda E, t=t: E.activation(
                out=KTM[:, (t % 2) * 512:(t % 2) * 512 + 512],
                in_=pb[0][:, 0:512], func=AF.Relu))
            k_ev_a[t] = aflush()

        for g in range(2):
            tiles = list(range(4 * g, 4 * g + 4))
            if g == 1:
                flush_pending(3)
                wait("sync", sv_, apply_fl[3])
                wait("sync", sa_, C["a"])
            xw = []
            for j, t in enumerate(tiles):
                xw.append(sdma(lambda E, t=t, j=j: E.dma_start(
                    out=SC1[:, j * D:(j + 1) * D],
                    in_=x_h.ap()[t * 128:(t + 1) * 128, :])))
            ra = emit_ln_stats(4, lambda j: SC1[:, j * D:(j + 1) * D], x_waits=xw)
            wait("vector", sa_, ra)
            for j, t in enumerate(tiles):
                slot = t % 2
                if t >= 2:
                    wait("vector", sp_, tpB_pinc[t - 2])
                em("vector", lambda E, j=j, slot=slot: E.tensor_scalar(
                    out=TMP32[:, slot * D:(slot + 1) * D],
                    in0=SC1[:, j * D:(j + 1) * D],
                    scalar1=STS[:, C_MEAN + j:C_MEAN + j + 1],
                    scalar2=STS[:, C_RSTD + j:C_RSTD + j + 1],
                    op0=ALU.subtract, op1=ALU.mult))
                apply_fl[t] = vflush()
                if t >= 1 and (t - 1) in pend:
                    flush_pending(t - 1)
                wait("tensor", sv_, apply_fl[t])
                if t == 0:
                    wait("tensor", sx_, ident_dma)
                else:
                    wait("tensor", sa_, k_ev_a[t - 1])
                    wait("tensor", sv_, v_ev_v[t - 1])
                for q in range(8):
                    kc = q
                    bk = q // 4
                    mk = (lambda E, kc=kc, bk=bk, q=q, slot=slot: E.transpose(
                        out=pb[bk][:, (q % 4) * 128:(q % 4 + 1) * 128],
                        in_=TMP32[:, slot * D + kc * 128: slot * D + (kc + 1) * 128],
                        identity=IDS[:, :]))
                    if q < 7:
                        em("tensor", lambda E, f=mk: f(E))
                    else:
                        tpA_p = pinc(mk)
                wait("vector", sp_, tpA_p)
                em("vector", lambda E, t=t: E.tensor_copy(
                    out=fm_dst(A, t, 0), in_=PBIG[:, 0:1024]))
                eaf = vflush()
                for q in range(8):
                    kc = 8 + q
                    bk = 2 + q // 4
                    mk = (lambda E, kc=kc, bk=bk, q=q, slot=slot: E.transpose(
                        out=pb[bk][:, (q % 4) * 128:(q % 4 + 1) * 128],
                        in_=TMP32[:, slot * D + kc * 128: slot * D + (kc + 1) * 128],
                        identity=IDS[:, :]))
                    if q < 7:
                        em("tensor", lambda E, f=mk: f(E))
                    else:
                        tpB_pinc[t] = pinc(mk)
                if t >= 1:
                    kv_done_p = emit_kv(t - 1, 0)
                wait("tensor", sv_, eaf)
                pk = emit_kproj(t, 0, 0, fc0_k)
                pv2 = emit_kproj(t, 0, 1, fc0_v)
                pend[t] = {"pv2": pv2, "tpB": tpB_pinc[t], "pk": pk}
                if t == NT - 1:
                    ring_rel[fc0_k[0][0]] = pk
                    ring_rel[fc0_k[1][0]] = pk
                    ring_rel[fc0_v[0][0]] = pv2
                    ring_rel[fc0_v[1][0]] = pv2
        flush_pending(NT - 1)
        kv_done_p = emit_kv(NT - 1, 0)
        ln1_ready = (C["v"], C["a"])

        em("vector", lambda E: E.memset(SC1[:, 0:NPAIR * 65], 0.0))
        kvz_v = vflush()

        def emit_kv_evict(fc):
            wait("vector", sp_, kv_done_p)
            if fc == 0:
                wait("vector", sv_, kvz_v)
            for bt in range(2):
                for q in range(2):
                    bank = 4 + bt * 2 + q
                    for par in range(2):
                        gp0 = bt * 16 + fc * 4 + 2 * q
                        em("vector", lambda E, bank=bank, par=par, gp0=gp0:
                           E.tensor_copy(
                               out=SC1[par * 64:(par + 1) * 64,
                                       gp0 * 65:(gp0 + 2) * 65]
                               .rearrange("p (n e) -> p n e", e=65),
                               in_=pb[bank][par * 64:(par + 1) * 64, 0:260]
                               .rearrange("p (n e) -> p n e", e=130)
                               [:, :, par * 65: par * 65 + 65]))
            kv_state["flush"] = vflush()

        emit_kv_evict(0)

        for fc in range(1, 4):
            wkw, wvw = [], []
            for half in range(2):
                s = ring_acquire()
                wkw.append((s, gdma(lambda E, fc=fc, half=half, s=s: E.dma_start(
                    out=rsl(s, 4096),
                    in_=wk_h.ap()[fc][:, half * 4096:(half + 1) * 4096],
                    max_dma_last_dim=2048))))
            for half in range(2):
                s = ring_acquire()
                wvw.append((s, gdma(lambda E, fc=fc, half=half, s=s: E.dma_start(
                    out=rsl(s, 4096),
                    in_=wv_h.ap()[fc][:, half * 4096:(half + 1) * 4096],
                    max_dma_last_dim=2048))))
            for t in range(NT):
                ti = fc * NT + t
                bkk = t % 2
                bkv = 2 + t % 2
                if fc == 1 and t < 2:
                    wait("tensor", sa_, k_ev_a[7])
                    wait("tensor", sv_, v_ev_v[7])
                if ti >= 2:
                    wait("tensor", sa_, k_ev_a[ti - 2])
                pk = emit_kproj(t, fc, bkk, wkw)
                if t == NT - 1:
                    ring_rel[wkw[0][0]] = pk
                    ring_rel[wkw[1][0]] = pk
                wait("scalar", sp_, pk)
                em("scalar", lambda E, t=t, bkk=bkk: E.activation(
                    out=KTM[:, (t % 2) * 512:(t % 2) * 512 + 512],
                    in_=pb[bkk][:, 0:512], func=AF.Relu))
                k_ev_a[ti] = aflush()
                if ti >= 2:
                    wait("tensor", sv_, v_ev_v[ti - 2])
                pv2 = emit_kproj(t, fc, bkv, wvw)
                if t == NT - 1:
                    ring_rel[wvw[0][0]] = pv2
                    ring_rel[wvw[1][0]] = pv2
                wait("vector", sp_, pv2)
                em("vector", lambda E, t=t, bkv=bkv: E.tensor_copy(
                    out=V65[:, (t % 2) * 520:(t % 2) * 520 + 520]
                    .rearrange("p (h e) -> p h e", e=65)[:, :, 0:64],
                    in_=pb[bkv][:, 0:512].rearrange("p (h f) -> p h f", f=64)))
                v_ev_v[ti] = vflush()
                if t >= 1:
                    kv_done_p = emit_kv(t - 1, fc)
            kv_done_p = emit_kv(NT - 1, fc)
            emit_kv_evict(fc)
        kv_ev_flush = kv_state["flush"]

        wait("sync", sv_, kv_ev_flush)
        cc_dma = sdma(lambda E: E.dma_start(out=cc_in.ap(),
                                            in_=SC1[:, 0:NPAIR * 65]))
        early_wq = []
        for fq in range(3):
            s = ring_acquire()
            early_wq.append((s, gdma(lambda E, fq=fq, s=s: E.dma_start(
                out=rsl(s, 2048),
                in_=wq_h.ap()[fq], max_dma_last_dim=2048))))
        wait("gpsimd", sx_, cc_dma)
        C["cc"] += 1
        em("gpsimd", lambda E: E.collective_compute(
            "AllReduce", ALU.add,
            replica_groups=[[0, 1, 2, 3], [4, 5, 6, 7]],
            ins=[cc_in.ap().opt()], outs=[cc_out.ap().opt()]).then_inc(sc_, 1))

        q_ev_a = {}
        for fq in range(16):
            if fq < 3:
                s, wq_dma = early_wq[fq]
            else:
                s = ring_acquire()
                wq_dma = gdma(lambda E, fq=fq, s=s: E.dma_start(
                    out=rsl(s, 2048),
                    in_=wq_h.ap()[fq], max_dma_last_dim=2048))
            if fq == 0:
                wait("tensor", sv_, ln1_ready[0])
                wait("tensor", sa_, ln1_ready[1])
            if fq >= 2:
                wait("tensor", sa_, q_ev_a[fq - 2])
            wait("tensor", sw_, wq_dma)
            for kc in range(KC):
                for tg in range(2):
                    bank = (fq % 2) * 2 + tg
                    mk = (lambda E, kc=kc, tg=tg, bank=bank, s=s: E.matmul(
                        out=pb[bank][:, 0:512],
                        lhsT=rsl(s, 2048)[:, kc * 128:(kc + 1) * 128],
                        rhs=A[:, kc * 1024 + tg * 512: kc * 1024 + (tg + 1) * 512],
                        start=(kc == 0), stop=(kc == KC - 1)))
                    if kc == KC - 1 and tg == 1:
                        pq = pinc(mk)
                    else:
                        em("tensor", lambda E, f=mk: f(E))
            ring_rel[s] = pq
            wait("scalar", sp_, pq)
            for tg in range(2):
                bank = (fq % 2) * 2 + tg
                em("scalar", lambda E, fq=fq, tg=tg, bank=bank: E.activation(
                    out=B[:, fq * 1024 + tg * 512: fq * 1024 + (tg + 1) * 512],
                    in_=pb[bank][:, 0:512], func=AF.Relu))
            q_ev_a[fq] = aflush()
        q_ready_a = q_ev_a[15]

        wait("gpsimd", sc_, C["cc"])
        kv_ready_w = 0
        for par in range(2):
            src_ = cc_out.ap().rearrange("p (b fc jp e) -> p b fc jp e",
                                         b=2, fc=4, jp=4)[par * 64:(par + 1) * 64]
            dst = KV65[par * 64:(par + 1) * 64, :].rearrange(
                "p (b fc jp two e) -> p b fc jp two e",
                b=2, fc=4, jp=4, two=2)[:, :, :, :, par, :]
            kv_ready_w = gdma(lambda E, src_=src_, dst=dst: E.dma_start(
                out=dst, in_=src_))

        ROUNDS = [7, 7, 7, 7, 4]
        div_fl = {}
        tp_done = {}
        tile_inner = {}
        inner_ready = (0, 0)
        wo_ev_v = {}
        wo0_halves = []
        for half in range(2):
            s = ring_acquire()
            wo0_halves.append((s, gdma(
                lambda E, half=half, s=s: E.dma_start(
                    out=rsl(s, 4096),
                    in_=wo_h.ap()[0][:, half * 4096:(half + 1) * 4096],
                    max_dma_last_dim=2048))))
        wo0_pw = {}
        for t in range(NT):
            bt = t // 4
            if t == 0:
                wait("tensor", sw_, kv_ready_w)
                wait("tensor", sa_, q_ready_a)
            if t >= 2:
                wait("vector", sp_, tp_done[t - 2])
            if 2 <= t and t - 2 < 4:
                tw = t - 2
                gates = tile_inner[tw]
                wait("tensor", sv_, gates[0])
                wait("tensor", sa_, gates[1])
                if tw == 0:
                    wait("tensor", sw_, bo_dma)
                    wait("tensor", sw_, wo0_halves[1][1])
                else:
                    wait("tensor", sv_, wo_ev_v[(0, 0, tw - 1)])
                for ic in range(KC):
                    s = wo0_halves[ic // 8][0]
                    mk = (lambda E, ic=ic, tw=tw, s=s: E.matmul(
                        out=pb[7][:, 0:512],
                        lhsT=A[:, ic * 1024 + tw * 128: ic * 1024 + (tw + 1) * 128],
                        rhs=rsl(s, 4096)[:, (ic % 8) * 512:(ic % 8 + 1) * 512],
                        start=(ic == 0), stop=False))
                    em("tensor", lambda E, f=mk: f(E))
                mkb = (lambda E: E.matmul(
                    out=pb[7][:, 0:512], lhsT=E1[:, :],
                    rhs=BOP[:, 0:512],
                    start=False, stop=True, skip_group_check=True))
                wo0_pw[tw] = pinc(mkb)
                wait("vector", sp_, wo0_pw[tw])
                if tw == 0:
                    wait("vector", sx_, cc_dma)
                em("vector", lambda E, tw=tw: E.tensor_scalar(
                    out=SC1[:, tw * D: tw * D + 512],
                    in0=pb[7][:, 0:512], scalar1=1.0, scalar2=None,
                    op0=ALU.mult, op1=ALU.add,
                    accum_out=STS[:, C_XA + 4 * tw:C_XA + 4 * tw + 1]))
                wo_ev_v[(0, 0, tw)] = vflush()
            if t >= 1:
                wait("tensor", sv_, div_fl[t - 1])
                tile_inner[t - 1] = emit_transposes(
                    ((t - 1) % 2) * D, (A, t - 1), [5, 6])
                tp_done[t - 1] = C["p"]
            h = 0
            for r, nh in enumerate(ROUNDS):
                for i in range(nh):
                    mk = (lambda E, h=h, i=i, t=t, r=r, bt=bt, nh=nh: E.matmul(
                        out=PBIG[:, r * 512 + i * 65: r * 512 + (i + 1) * 65],
                        lhsT=B[:, (h // 2) * 1024 + t * 128:
                               (h // 2) * 1024 + (t + 1) * 128],
                        rhs=KV65[:, (bt * H + h) * 65:(bt * H + h + 1) * 65],
                        start=(i == 0), stop=(i == nh - 1),
                        skip_group_check=(i != 0)))
                    if r == len(ROUNDS) - 1 and i == nh - 1:
                        pr = pinc(mk)
                    else:
                        em("tensor", lambda E, f=mk: f(E))
                    h += 1
            wait("vector", sp_, pr)
            em("vector", lambda E: E.tensor_scalar(
                out=STS[:, C_NM:C_NM + 35].rearrange("p (r i) -> p r i", i=7),
                in0=PBIG[:, :].rearrange("p (r x) -> p r x", x=512)
                [:, 0:5, 0:455]
                .rearrange("p r (i e) -> p r i e", e=65)[:, :, :, 64:65],
                scalar1=1e-6, scalar2=None, op0=ALU.max))
            nv = vflush()
            wait("scalar", sv_, nv)
            em("scalar", lambda E: E.activation(
                out=STS[:, C_LNN:C_LNN + 35], in_=STS[:, C_NM:C_NM + 35],
                func=AF.Ln))
            em("scalar", lambda E: E.activation(
                out=STS[:, C_RC:C_RC + 35], in_=STS[:, C_LNN:C_LNN + 35],
                func=AF.Exp, scale=CST[:, 2:3]))
            ra = aflush()
            wait("vector", sa_, ra)
            slot = t % 2
            in0a = PBIG[:, :].rearrange("p (r x) -> p r x", x=512)[:, 0:4, 0:455] \
                .rearrange("p r (i e) -> p r i e", e=65)[:, :, :, 0:64]
            in1a = STS[:, C_RC:C_RC + 28].rearrange(
                "p (r i one) -> p r i one", r=4, i=7, one=1)
            a0, a1 = bass.broadcast_tensor_aps(in0a, in1a)
            outa = TMP32[:, slot * D: slot * D + 1792].rearrange(
                "p (r i e) -> p r i e", i=7, e=64)
            em("vector", lambda E, a0=a0, a1=a1, outa=outa: E.tensor_tensor(
                out=outa, in0=a0, in1=a1, op=ALU.mult))
            in0b = PBIG[:, 4 * 512: 4 * 512 + 4 * 65].rearrange(
                "p (i e) -> p i e", e=65)[:, :, 0:64]
            in1b = STS[:, C_RC + 28:C_RC + 32].rearrange(
                "p (i one) -> p i one", one=1)
            b0, b1 = bass.broadcast_tensor_aps(in0b, in1b)
            outb = TMP32[:, slot * D + 1792: slot * D + 2048].rearrange(
                "p (i e) -> p i e", e=64)
            em("vector", lambda E, b0=b0, b1=b1, outb=outb: E.tensor_tensor(
                out=outb, in0=b0, in1=b1, op=ALU.mult))
            div_fl[t] = vflush()
        wait("tensor", sv_, div_fl[NT - 1])
        tile_inner[NT - 1] = emit_transposes(
            ((NT - 1) % 2) * D, (A, NT - 1), [5, 6])
        tp_done[NT - 1] = C["p"]
        ring_rel[wo0_halves[0][0]] = wo0_pw[3]
        ring_rel[wo0_halves[1][0]] = wo0_pw[3]
        inner_ready = (C["v"], C["a"])

        x2_dma = {}

        def wo_strips(fc):
            halves = []
            for half in range(2):
                s = ring_acquire()
                halves.append((s, gdma(
                    lambda E, fc=fc, half=half, s=s: E.dma_start(
                        out=rsl(s, 4096),
                        in_=wo_h.ap()[fc][:, half * 4096:(half + 1) * 4096],
                        max_dma_last_dim=2048))))
            return halves

        def wo_mms(g, fc, halves, tiles):
            fcp = fc % 2
            pws = []
            for j, t in enumerate(tiles):
                bank = fcp * 4 + j
                if g == 0 and fc == 0 and j == 0:
                    wait("tensor", sv_, inner_ready[0])
                    wait("tensor", sa_, inner_ready[1])
                    wait("tensor", sw_, bo_dma)
                prev_ev = wo_ev_v.get((g, fc - 2, j))
                if g == 1 and fc < 2:
                    prev_ev = wo_ev_v.get((0, fc + 2, j))
                if prev_ev:
                    wait("tensor", sv_, prev_ev)
                for ic in range(KC):
                    s, wdma = halves[ic // 8]
                    if ic % 8 == 0 and j == 0:
                        wait("tensor", sw_, wdma)
                    mk = (lambda E, ic=ic, t=t, bank=bank, s=s: E.matmul(
                        out=pb[bank][:, 0:512],
                        lhsT=A[:, ic * 1024 + t * 128: ic * 1024 + (t + 1) * 128],
                        rhs=rsl(s, 4096)[:, (ic % 8) * 512:(ic % 8 + 1) * 512],
                        start=(ic == 0), stop=False))
                    em("tensor", lambda E, f=mk: f(E))
                mkb = (lambda E, fc=fc, bank=bank: E.matmul(
                    out=pb[bank][:, 0:512], lhsT=E1[:, :],
                    rhs=BOP[:, fc * 512:(fc + 1) * 512],
                    start=False, stop=True, skip_group_check=True))
                pws.append(pinc(mkb))
            ring_rel[halves[0][0]] = pws[-1]
            ring_rel[halves[1][0]] = pws[-1]
            return pws

        G1SLOT = [4, 0, 1, 2]


        def wo_evct(g, fc, j, pw, gates=()):
            bank = (fc % 2) * 4 + j
            slot = j if g == 0 else G1SLOT[j]
            wait("vector", sp_, pw)
            for sem, val in gates:
                wait("vector", sem, val)
            em("vector", lambda E, j=j, fc=fc, bank=bank, slot=slot:
               E.tensor_scalar(
                   out=SC1[:, slot * D + fc * 512: slot * D + (fc + 1) * 512],
                   in0=pb[bank][:, 0:512], scalar1=1.0, scalar2=None,
                   op0=ALU.mult, op1=ALU.add,
                   accum_out=STS[:, C_XA + 4 * j + fc:C_XA + 4 * j + fc + 1]))
            fl = vflush()
            wo_ev_v[(g, fc, j)] = fl
            wait("scalar", sv_, fl)
            em("scalar", lambda E, j=j, fc=fc, slot=slot: E.activation(
                out=SQD[:, 0:512],
                in_=SC1[:, slot * D + fc * 512: slot * D + (fc + 1) * 512],
                func=AF.Square,
                accum_out=STS[:, C_SQ4 + 4 * j + fc:C_SQ4 + 4 * j + fc + 1]))
            return fl

        for j in range(4):
            wait("scalar", sv_, wo_ev_v[(0, 0, j)])
            em("scalar", lambda E, j=j: E.activation(
                out=SQD[:, 0:512], in_=SC1[:, j * D: j * D + 512],
                func=AF.Square,
                accum_out=STS[:, C_SQ4 + 4 * j:C_SQ4 + 4 * j + 1]))
        for fc in range(1, 4):
            halves = wo_strips(fc)
            if fc == 1:
                wait("tensor", sv_, C["v"])
                wait("tensor", sa_, C["a"])
            pws = wo_mms(0, fc, halves, [0, 1, 2, 3])
            for j in range(4):
                wo_evct(0, fc, j, pws[j])
        t1 = [4, 5, 6, 7]
        pws10 = wo_mms(1, 0, wo_strips(0), t1)

        aqA0 = [act_fin(j, True) for j in range(4)]
        xw = {}
        xw[0] = sdma(lambda E: E.dma_start(
            out=XST[:, 0:D], in_=x_h.ap()[0:128, :]))
        xw[1] = sdma(lambda E: E.dma_start(
            out=XST[:, D:2 * D], in_=x_h.ap()[128:256, :]))
        rfl5, aq2, ap2fl, tpd, tpg1 = {}, {}, {}, {}, {}
        xw5 = {}

        def lna_apply_resid(g, j, tmp_gate=None):
            t = 4 * g + j
            slot = j if g == 0 else G1SLOT[j]
            aq = aqA0[j] if g == 0 else aqA1[j]
            xwd = xw if g == 0 else xw5
            wait("vector", sa_, aq)
            if tmp_gate is not None:
                wait("vector", sp_, tmp_gate)
            em("vector", lambda E, j=j, slot=slot: E.tensor_scalar(
                out=TMP32[:, (j % 2) * D:(j % 2 + 1) * D],
                in0=SC1[:, slot * D:(slot + 1) * D],
                scalar1=STS[:, C_MEAN + j:C_MEAN + j + 1],
                scalar2=STS[:, C_RSTD + j:C_RSTD + j + 1],
                op0=ALU.subtract, op1=ALU.mult))
            wait("vector", sx_, xwd[j])
            em("vector", lambda E, j=j, slot=slot: E.scalar_tensor_tensor(
                out=SC1[:, slot * D:(slot + 1) * D],
                in0=TMP32[:, (j % 2) * D:(j % 2 + 1) * D], scalar=1.0,
                in1=XST[:, (j % 2) * D:(j % 2 + 1) * D],
                op0=ALU.mult, op1=ALU.add,
                accum_out=STS[:, C_SX + j:C_SX + j + 1]))
            rfl = vflush()
            rfl5[(g, j)] = rfl
            wait("sync", sv_, rfl)
            x2_dma[t] = sdma(lambda E, t=t, slot=slot: E.dma_start(
                out=x2_h.ap()[t * 128:(t + 1) * 128, :],
                in_=SC1[:, slot * D:(slot + 1) * D]))
            wait("scalar", sv_, rfl)
            for q in range(4):
                em("scalar", lambda E, j=j, q=q, slot=slot: E.activation(
                    out=SQD[:, 0:512],
                    in_=SC1[:, slot * D + q * 512: slot * D + (q + 1) * 512],
                    func=AF.Square,
                    accum_out=STS[:, C_SQ4 + 4 * j + q:C_SQ4 + 4 * j + q + 1]))
            aq2[(g, j)] = act_fin(j, False)

        def ln2_apply(g, j, src_ap, gates=()):
            wait("vector", sa_, aq2[(g, j)])
            for sem, val in gates:
                wait("vector", sem, val)
            em("vector", lambda E, j=j, src_ap=src_ap: E.tensor_scalar(
                out=TMP32[:, (j % 2) * D:(j % 2 + 1) * D],
                in0=src_ap,
                scalar1=STS[:, C_MEAN + j:C_MEAN + j + 1],
                scalar2=STS[:, C_RSTD + j:C_RSTD + j + 1],
                op0=ALU.subtract, op1=ALU.mult))
            ap2fl[(g, j)] = vflush()

        lna_apply_resid(0, 0)
        wait("sync", sv_, rfl5[(0, 0)])
        xw[2] = sdma(lambda E: E.dma_start(
            out=XST[:, 0:D], in_=x_h.ap()[256:384, :]))
        lna_apply_resid(0, 1)
        wait("sync", sv_, rfl5[(0, 1)])
        xw[3] = sdma(lambda E: E.dma_start(
            out=XST[:, D:2 * D], in_=x_h.ap()[384:512, :]))
        ln2_apply(0, 0, SC1[:, 0:D])
        wo_evct(1, 0, 0, pws10[0], gates=((sa_, aqA0[0]),))
        emit_transposes(0, (B, 0), [4, 5],
                        extra_gates=((sv_, ap2fl[(0, 0)]),
                                     (sv_, wo_ev_v[(0, 3, 1)])))
        tpd[0] = C["p"]
        lna_apply_resid(0, 2, tmp_gate=tpd[0])
        ln2_apply(0, 1, SC1[:, D:2 * D])
        wo_evct(1, 0, 1, pws10[1],
                gates=((sx_, x2_dma[0]), (sa_, aq2[(0, 0)])))
        emit_transposes(D, (B, 1), [4, 5], extra_gates=((sv_, ap2fl[(0, 1)]),))
        tpd[1] = C["p"]
        wo_evct(1, 0, 2, pws10[2],
                gates=((sx_, x2_dma[1]), (sa_, aq2[(0, 1)])))
        ln2_apply(0, 2, SC1[:, 2 * D:3 * D])
        emit_transposes(0, (B, 2), [4, 5], extra_gates=((sv_, ap2fl[(0, 2)]),))
        tpd[2] = C["p"]
        lna_apply_resid(0, 3, tmp_gate=tpd[1])
        wo_evct(1, 0, 3, pws10[3],
                gates=((sx_, x2_dma[2]), (sa_, aq2[(0, 2)])))
        ln2_apply(0, 3, SC1[:, 3 * D:4 * D])
        ln2_ready_g0 = emit_transposes(D, (B, 3), [4, 5],
                                       extra_gates=((sv_, ap2fl[(0, 3)]),))
        tpd[3] = C["p"]

        wait("tensor", sv_, ln2_ready_g0[0])
        wait("tensor", sa_, ln2_ready_g0[1])
        pws11 = wo_mms(1, 1, wo_strips(1), t1)
        for j in range(4):
            wo_evct(1, 1, j, pws11[j])


        pws12 = wo_mms(1, 2, wo_strips(2), t1)
        for j in range(4):
            wo_evct(1, 2, j, pws12[j])
        pws13 = wo_mms(1, 3, wo_strips(3), t1)

        gu_slot, gu_dma, gu_prev, gu_div_v = {}, {}, {}, {}
        gu_pg = {}

        def gu_load(jp):
            s = ring_acquire()
            gu_slot[jp] = s
            gdma(lambda E, jp=jp, s=s: E.dma_start(
                out=rsl(s, 4096)[:, 0:2048], in_=gu_h.ap()[jp],
                max_dma_last_dim=2048))
            gu_dma[jp] = gdma(lambda E, jp=jp, s=s: E.dma_start(
                out=rsl(s, 4096)[:, 2048:4096], in_=gu_h.ap()[16 + jp],
                max_dma_last_dim=2048))

        def gu_banks(jp, tg):
            if tg == 0:
                return (2, 3) if jp % 2 == 0 else (6, 7)
            return (4, 5) if jp % 2 == 0 else (2, 3)

        def gu_pe(jp, tg, extra=()):
            s = gu_slot[jp]
            bg, bu = gu_banks(jp, tg)
            wait("tensor", sw_, gu_dma[jp])
            for sem, val in extra:
                wait("tensor", sem, val)
            prev = gu_prev.get(bg)
            if prev:
                wait("tensor", sv_, prev)
            for part, bank in ((0, bg), (1, bu)):
                for kc in range(KC):
                    mk = (lambda E, kc=kc, part=part, bank=bank, s=s, tg=tg:
                          E.matmul(
                              out=pb[bank][:, 0:512],
                              lhsT=rsl(s, 4096)[:, part * 2048 + kc * 128:
                                                part * 2048 + (kc + 1) * 128],
                              rhs=B[:, kc * 1024 + tg * 512:
                                    kc * 1024 + tg * 512 + 512],
                              start=(kc == 0), stop=(kc == KC - 1)))
                    if part == 1 and kc == KC - 1:
                        pg = pinc(mk)
                    else:
                        em("tensor", lambda E, f=mk: f(E))
            if tg == 1:
                ring_rel[s] = pg
            gu_pg[(jp, tg)] = pg

        def gu_div(jp, tg):
            bg, bu = gu_banks(jp, tg)
            wait("scalar", sp_, gu_pg[(jp, tg)])
            em("scalar", lambda E, tg=tg, bg=bg: E.activation(
                out=GT[:, tg * 512:(tg + 1) * 512], in_=pb[bg][:, 0:512],
                func=AF.Silu))
            ga = aflush()
            wait("vector", sa_, ga)
            em("vector", lambda E, jp=jp, tg=tg, bu=bu: E.tensor_tensor(
                out=A[:, jp * 1024 + tg * 512: jp * 1024 + tg * 512 + 512],
                in0=GT[:, tg * 512:(tg + 1) * 512],
                in1=pb[bu][:, 0:512], op=ALU.mult))
            fl = vflush()
            gu_div_v[(jp, tg)] = fl
            gu_prev[bg] = fl

        def gu_mms(jp, tg, extra=()):
            gu_pe(jp, tg, extra=extra)
            gu_div(jp, tg)

        aqA1 = []
        for j in range(4):
            wo_evct(1, 3, j, pws13[j])
            aqA1.append(act_fin(j, True))
        wait("sync", sv_, ap2fl[(0, 2)])
        xw5[0] = sdma(lambda E: E.dma_start(
            out=XST[:, 0:D], in_=x_h.ap()[4 * 128:5 * 128, :]))
        wait("sync", sv_, rfl5[(0, 3)])
        xw5[1] = sdma(lambda E: E.dma_start(
            out=XST[:, D:2 * D], in_=x_h.ap()[5 * 128:6 * 128, :]))
        for jp in range(4):
            gu_load(jp)
        lna_apply_resid(1, 0, tmp_gate=tpd[2])
        wait("sync", sv_, rfl5[(1, 0)])
        xw5[2] = sdma(lambda E: E.dma_start(
            out=XST[:, 0:D], in_=x_h.ap()[6 * 128:7 * 128, :]))
        lna_apply_resid(1, 1, tmp_gate=tpd[3])
        wait("sync", sv_, rfl5[(1, 1)])
        xw5[3] = sdma(lambda E: E.dma_start(
            out=XST[:, D:2 * D], in_=x_h.ap()[7 * 128:8 * 128, :]))
        ln2_apply(1, 0, SC1[:, 4 * D:5 * D])
        emit_transposes(0, (B, 4), [0, 1],
                        extra_gates=((sv_, ap2fl[(1, 0)]),
                                     (sv_, wo_ev_v[(1, 2, 1)])))
        tpg1[0] = C["p"]
        gu_pe(0, 0, extra=((sv_, wo_ev_v[(1, 2, 3)]),
                           (sv_, ln2_ready_g0[0]), (sa_, ln2_ready_g0[1])))
        lna_apply_resid(1, 2, tmp_gate=tpg1[0])
        ln2_apply(1, 1, SC1[:, 0:D])
        emit_transposes(D, (B, 5), [0, 1], extra_gates=((sv_, ap2fl[(1, 1)]),))
        tpg1[1] = C["p"]
        gu_div(0, 0)
        gu_pe(1, 0, extra=((sv_, wo_ev_v[(1, 3, 3)]),))
        lna_apply_resid(1, 3, tmp_gate=tpg1[1])
        ln2_apply(1, 2, SC1[:, D:2 * D])
        emit_transposes(0, (B, 6), [0, 1], extra_gates=((sv_, ap2fl[(1, 2)]),))
        tpg1[2] = C["p"]
        gu_div(1, 0)
        gu_pe(2, 0)
        ln2_apply(1, 3, SC1[:, 2 * D:3 * D])
        ln2_ready_g1 = emit_transposes(D, (B, 7), [0, 1],
                                       extra_gates=((sv_, ap2fl[(1, 3)]),))
        tpg1[3] = C["p"]
        gu_div(2, 0)
        gu_pe(3, 0)
        gu_div(3, 0)
        for j in range(4):
            extra = []
            if j == 0:
                extra = [(sv_, ln2_ready_g1[0]), (sa_, ln2_ready_g1[1]),
                         (sv_, wo_ev_v[(1, 3, 1)])]
            elif j == 1:
                extra = [(sv_, wo_ev_v[(1, 3, 3)])]
            gu_mms(j, 1, extra=extra)
        for jp in range(4, 16):
            gu_load(jp)
            gu_mms(jp, 0)
            gu_mms(jp, 1)
        inter_ready_v = gu_div_v[(15, 1)]

        dn_ev_v = {}

        def dn_strips(fc):
            halves = []
            for half in range(2):
                s = ring_acquire()
                halves.append((s, gdma(
                    lambda E, fc=fc, half=half, s=s: E.dma_start(
                        out=rsl(s, 4096),
                        in_=dn_h.ap()[fc][:, half * 4096:(half + 1) * 4096],
                        max_dma_last_dim=2048))))
            return halves

        def dn_mms(g, fc, halves, tiles):
            fcp = fc % 2
            pws = []
            for j, t in enumerate(tiles):
                bank = fcp * 4 + j
                if g == 0 and fc == 0 and j == 0:
                    wait("tensor", sv_, inter_ready_v)
                prev_ev = dn_ev_v.get((g, fc - 2, j))
                if g == 1 and fc < 2:
                    prev_ev = dn_ev_v.get((0, fc + 2, j))
                if prev_ev:
                    wait("tensor", sv_, prev_ev)
                for ic in range(KC):
                    s, wdma = halves[ic // 8]
                    if ic % 8 == 0 and j == 0:
                        wait("tensor", sw_, wdma)
                    mk = (lambda E, ic=ic, t=t, bank=bank, s=s: E.matmul(
                        out=pb[bank][:, 0:512],
                        lhsT=A[:, ic * 1024 + t * 128: ic * 1024 + (t + 1) * 128],
                        rhs=rsl(s, 4096)[:, (ic % 8) * 512:(ic % 8 + 1) * 512],
                        start=(ic == 0), stop=(ic == KC - 1)))
                    if ic == KC - 1:
                        pws.append(pinc(mk))
                    else:
                        em("tensor", lambda E, f=mk: f(E))
            ring_rel[halves[0][0]] = pws[-1]
            ring_rel[halves[1][0]] = pws[-1]
            return pws

        def dn_evct(g, fc, j, pw, gates=()):
            bank = (fc % 2) * 4 + j
            slot = j if g == 0 else G1SLOT[j]
            wait("vector", sp_, pw)
            for sem, val in gates:
                wait("vector", sem, val)
            em("vector", lambda E, j=j, fc=fc, bank=bank, slot=slot:
               E.tensor_scalar(
                   out=SC1[:, slot * D + fc * 512: slot * D + (fc + 1) * 512],
                   in0=pb[bank][:, 0:512], scalar1=1.0, scalar2=None,
                   op0=ALU.mult, op1=ALU.add,
                   accum_out=STS[:, C_XA + 4 * j + fc:C_XA + 4 * j + fc + 1]))
            fl = vflush()
            dn_ev_v[(g, fc, j)] = fl
            wait("scalar", sv_, fl)
            em("scalar", lambda E, j=j, fc=fc, slot=slot: E.activation(
                out=SQD[:, 0:512],
                in_=SC1[:, slot * D + fc * 512: slot * D + (fc + 1) * 512],
                func=AF.Square,
                accum_out=STS[:, C_SQ4 + 4 * j + fc:C_SQ4 + 4 * j + fc + 1]))
            return fl

        aqA0_7 = []
        for fc in range(4):
            halves = dn_strips(fc)
            pws = dn_mms(0, fc, halves, [0, 1, 2, 3])
            for j in range(4):
                dn_evct(0, fc, j, pws[j])
                if fc == 3:
                    aqA0_7.append(act_fin(j, True))
        pws10 = dn_mms(1, 0, dn_strips(0), t1)
        pws11 = dn_mms(1, 1, dn_strips(1), t1)

        g0_odma = {}
        xw7 = {}
        for j in (0, 1):
            xw7[j] = sdma(lambda E, t=j: E.dma_start(
                out=XST[:, (t % 2) * D:(t % 2 + 1) * D],
                in_=x2_h.ap()[t * 128:(t + 1) * 128, :]))
        for j in range(4):
            wait("vector", sa_, aqA0_7[j])
            em("vector", lambda E, j=j: E.tensor_scalar(
                out=TMP32[:, (j % 2) * D:(j % 2 + 1) * D],
                in0=SC1[:, j * D:(j + 1) * D],
                scalar1=STS[:, C_MEAN + j:C_MEAN + j + 1],
                scalar2=STS[:, C_RSTD + j:C_RSTD + j + 1],
                op0=ALU.subtract, op1=ALU.mult))
            wait("vector", sx_, xw7[j])
            em("vector", lambda E, j=j: E.tensor_tensor(
                out=SC1[:, j * D:(j + 1) * D],
                in0=TMP32[:, (j % 2) * D:(j % 2 + 1) * D],
                in1=XST[:, (j % 2) * D:(j % 2 + 1) * D], op=ALU.add))
            rfl = vflush()
            wait("sync", sv_, rfl)
            g0_odma[j] = odma(lambda E, j=j: E.dma_start(
                out=out_h.ap()[j * 128:(j + 1) * 128, :],
                in_=SC1[:, j * D:(j + 1) * D]))
            if j < 2:
                xw7[j + 2] = sdma(lambda E, t=j + 2: E.dma_start(
                    out=XST[:, (t % 2) * D:(t % 2 + 1) * D],
                    in_=x2_h.ap()[t * 128:(t + 1) * 128, :]))
            if j == 0:
                dn_evct(1, 0, 0, pws10[0], gates=((sa_, aqA0_7[0]),))
            else:
                dn_evct(1, 0, j, pws10[j],
                        gates=((so_, g0_odma[j - 1]), (sa_, aqA0_7[j])))
        for j in range(4):
            dn_evct(1, 1, j, pws11[j])
        pws12 = dn_mms(1, 2, dn_strips(2), t1)
        for j in range(4):
            dn_evct(1, 2, j, pws12[j])
        pws13 = dn_mms(1, 3, dn_strips(3), t1)

        xw_g1 = {}
        for j in (0, 1):
            xw_g1[j] = sdma(lambda E, j=j: E.dma_start(
                out=XST[:, (j % 2) * D:(j % 2 + 1) * D],
                in_=x2_h.ap()[(4 + j) * 128:(5 + j) * 128, :]))
        aqA1_7 = []
        for j in range(4):
            dn_evct(1, 3, j, pws13[j])
            aqA1_7.append(act_fin(j, True))
        for j in range(4):
            t = 4 + j
            slot = G1SLOT[j]
            wait("vector", sa_, aqA1_7[j])
            em("vector", lambda E, j=j, slot=slot: E.tensor_scalar(
                out=TMP32[:, (j % 2) * D:(j % 2 + 1) * D],
                in0=SC1[:, slot * D:(slot + 1) * D],
                scalar1=STS[:, C_MEAN + j:C_MEAN + j + 1],
                scalar2=STS[:, C_RSTD + j:C_RSTD + j + 1],
                op0=ALU.subtract, op1=ALU.mult))
            wait("vector", sx_, xw_g1[j])
            em("vector", lambda E, j=j, slot=slot: E.tensor_tensor(
                out=SC1[:, slot * D:(slot + 1) * D],
                in0=TMP32[:, (j % 2) * D:(j % 2 + 1) * D],
                in1=XST[:, (j % 2) * D:(j % 2 + 1) * D], op=ALU.add))
            rfl = vflush()
            wait("sync", sv_, rfl)
            if j < 2:
                xw_g1[j + 2] = sdma(lambda E, j=j: E.dma_start(
                    out=XST[:, (j % 2) * D:(j % 2 + 1) * D],
                    in_=x2_h.ap()[(6 + j) * 128:(7 + j) * 128, :]))
            odma(lambda E, t=t, slot=slot: E.dma_start(
                out=out_h.ap()[t * 128:(t + 1) * 128, :],
                in_=SC1[:, slot * D:(slot + 1) * D]))

        final_o = C["o"]
        em("sync", lambda E, v=final_o: E.wait_ge(so_, v))
        em("gpsimd", lambda E, v=final_o: E.wait_ge(so_, v))

        with nc.Block() as block:
            @block.sync
            def _(sync):
                for fn in P["sync"]:
                    fn(sync)

            @block.gpsimd
            def _(gpsimd):
                for fn in P["gpsimd"]:
                    fn(gpsimd)

            @block.vector
            def _(vector):
                for fn in P["vector"]:
                    fn(vector)

            @block.scalar
            def _(scalar):
                for fn in P["scalar"]:
                    fn(scalar)

            @block.tensor
            def _(tensor):
                for fn in P["tensor"]:
                    fn(tensor)

    return nc


def _strips(w, width):
    n = w.shape[1] // width
    out = np.empty((n, 128, 16 * width), dtype=np.float32)
    for i in range(n):
        s = w[:, i * width:(i + 1) * width].reshape(16, 128, width)
        out[i] = s.transpose(1, 0, 2).reshape(128, 16 * width)
    return np.ascontiguousarray(out)


def kernel(L0_attn_hidden_states, L0_attn_Wq, L0_attn_Wk, L0_attn_Wv,
           L0_attn_Wo, L0_attn_bo, L0_ffn_gate_up_proj, L0_ffn_down_proj):
    X = np.asarray(L0_attn_hidden_states, dtype=np.float32)
    Wq = np.asarray(L0_attn_Wq, dtype=np.float32)
    Wk = np.asarray(L0_attn_Wk, dtype=np.float32)
    Wv = np.asarray(L0_attn_Wv, dtype=np.float32)
    Wo = np.asarray(L0_attn_Wo, dtype=np.float32)
    bo = np.asarray(L0_attn_bo, dtype=np.float32).reshape(1, D)
    GU = np.asarray(L0_ffn_gate_up_proj, dtype=np.float32)
    DN = np.asarray(L0_ffn_down_proj, dtype=np.float32)

    wq_s = _strips(Wq, 128)
    wk_s = _strips(Wk, 512)
    wv_s = _strips(Wv, 512)
    wo_s = _strips(Wo, 512)
    ident = np.eye(128, dtype=np.float32)

    nc = build_program()
    in_maps = []
    for c in range(8):
        G, e = c // 4, c % 4
        xs = np.ascontiguousarray(
            np.concatenate([X[2 * G, e::4, :], X[2 * G + 1, e::4, :]], axis=0))
        gu_s = _strips(np.ascontiguousarray(GU[e]), 128)
        dn_s = _strips(np.ascontiguousarray(DN[e]), 512)
        in_maps.append({
            "x": xs, "wq": wq_s, "wk": wk_s, "wv": wv_s, "wo": wo_s,
            "bo": bo, "gu": gu_s, "dn": dn_s, "ident": ident,
        })
    res = run_bass_kernel_spmd(nc, in_maps, core_ids=list(range(8)))

    out = np.empty((4, 2048, D), dtype=np.float32)
    for c in range(8):
        G, e = c // 4, c % 4
        r = res.results[c]["out"]
        out[2 * G, e::4, :] = r[:512]
        out[2 * G + 1, e::4, :] = r[512:]
    return out
